# version 06
# baseline (speedup 1.0000x reference)
import math
from contextlib import ExitStack

import numpy as np
import concourse.bass as bass
import concourse.mybir as mybir
from concourse.bass_utils import run_bass_kernel_spmd

F32 = mybir.dt.float32
BF16 = mybir.dt.bfloat16
AF = mybir.ActivationFunctionType
ALU = mybir.AluOpType
AX = mybir.AxisListType

L = 2048
D = 2048
TT = 16
DT = 16
CH = 1024
NH, HD = 4, 512
NAH, AHD = 16, 128
WIN = 1024
BASE_LR_INV = float(0.01 + np.log(-np.expm1(-0.01)))
NS_COEFFS = [(4.0848, -6.8946, 2.927), (3.9505, -6.3029, 2.6377),
             (3.7418, -5.5913, 2.3037), (2.8769, -3.1427, 1.2046),
             (2.8366, -3.0525, 1.2012)]
RS = 1.0 / math.sqrt(AHD)

COMPUTE_ENGINES = ("tensor", "vector", "scalar", "gpsimd")
QUEUES = ("sync", "gpsimd", "scalar")
ALL_STREAMS = COMPUTE_ENGINES + ("sync",)


class Buf:
    __slots__ = ("name", "writer", "readers", "fresh", "psum")

    def __init__(self, name="", psum=False):
        self.name = name
        self.writer = None
        self.readers = []
        self.fresh = True
        self.psum = psum


class OpRec:
    __slots__ = ("engine", "emit", "deps", "needs_inc", "token", "is_dma", "tag",
                 "slot")

    def __init__(self, engine, emit, is_dma, tag):
        self.engine = engine
        self.emit = emit
        self.deps = []
        self.needs_inc = False
        self.token = None
        self.is_dma = is_dma
        self.tag = tag
        self.slot = 0


QSLOTS = {"sync": 8, "gpsimd": 6, "scalar": 4}


class KB:
    def __init__(self, nc):
        self.nc = nc
        self.ops = []
        self.per_engine = {e: [] for e in ALL_STREAMS}
        self.es = ExitStack()
        self.consts = {}
        self.cbuf = None
        self.const_buf = Buf("consts")
        self.guard_deps = []
        self.last_comp = {}
        self.last_dma = {}
        self.qcount = {q: 0 for q in QUEUES}
        self.slot_user = {q: [None] * QSLOTS[q] for q in QUEUES}

    def sb(self, name, shape, dtype, es=None):
        return (es or self.es).enter_context(self.nc.sbuf_tensor(name, shape, dtype))

    def ps(self, name, shape, dtype=F32, es=None):
        return (es or self.es).enter_context(self.nc.psum_tensor(name, shape, dtype))

    def constv(self, v):
        v = float(v)
        if v not in self.consts:
            idx = len(self.consts)
            assert idx < 32
            self.consts[v] = idx
            self.nc.const_aps.register(F32, v, self.cbuf[:, idx:idx + 1])
        return v

    def barrier(self):
        deps = list(self.last_comp.values()) + list(self.last_dma.values())
        for d in deps:
            d.needs_inc = True
        self.guard_deps = deps

    def op(self, engine, emit, reads=(), writes=(), tag=""):
        rec = OpRec(engine, emit, False, tag)
        self._resolve(rec, reads, writes, True)
        self.last_comp[engine] = rec
        return rec

    def dma(self, queue, out, in_, reads=(), writes=(), tag="", **kw):
        def emit(eng, out=out, in_=in_, kw=kw):
            return eng.dma_start(out=out, in_=in_, **kw)
        rec = OpRec(queue, emit, True, tag)
        i = self.qcount[queue]
        self.qcount[queue] += 1
        nslots = QSLOTS[queue]
        rec.slot = i % nslots
        rec.token = 16 * (i // nslots + 1)
        prev = self.slot_user[queue][rec.slot]
        self._resolve(rec, reads, writes, False, extra_dep=prev)
        self.slot_user[queue][rec.slot] = rec
        self.last_dma[queue] = rec
        return rec

    def _resolve(self, rec, reads, writes, same_engine_skip, extra_dep=None):
        deps = []
        if extra_dep is not None:
            deps.append(extra_dep)

        def need(d):
            if d is None or d is rec:
                return
            if (same_engine_skip and (not d.is_dma)
                    and d.engine == rec.engine == "tensor"):
                return
            deps.append(d)

        for b in reads:
            need(b.writer)
            if b.psum:
                for r in b.readers:
                    if r.engine != rec.engine or r.is_dma:
                        need(r)
        for b in writes:
            need(b.writer)
            for r in b.readers:
                need(r)
            if b.fresh:
                b.fresh = False
                for d in self.guard_deps:
                    need(d)
        seen = set()
        rec.deps = []
        for d in deps:
            if id(d) not in seen:
                seen.add(id(d))
                rec.deps.append(d)
                d.needs_inc = True
        for b in reads:
            b.readers.append(rec)
        for b in writes:
            b.writer = rec
            b.readers = []
        self.ops.append(rec)
        self.per_engine[rec.engine].append(rec)

    def build_program(self):
        nc = self.nc
        ecount = {e: 0 for e in COMPUTE_ENGINES}
        for rec in self.ops:
            if (not rec.is_dma) and rec.needs_inc:
                ecount[rec.engine] += 1
                rec.token = ecount[rec.engine]

        sem_e = {e: self.es.enter_context(nc.semaphore(f"se_{e}"))
                 for e in COMPUTE_ENGINES}
        sem_q = {(q, s): self.es.enter_context(nc.semaphore(f"sq_{q}{s}"))
                 for q in QUEUES for s in range(QSLOTS[q])}
        block = self.es.enter_context(nc.Block())
        kb = self

        def section(engine_name):
            def run(eng):
                wm = {}
                for rec in kb.per_engine[engine_name]:
                    for d in rec.deps:
                        key = (d.is_dma, d.engine, d.slot)
                        if wm.get(key, 0) < d.token:
                            wm[key] = d.token
                            sem = (sem_q[(d.engine, d.slot)] if d.is_dma
                                   else sem_e[d.engine])
                            eng.wait_ge(sem, d.token)
                    inst = rec.emit(eng)
                    if rec.is_dma:
                        inst.then_inc(sem_q[(rec.engine, rec.slot)], 16)
                    elif rec.needs_inc:
                        inst.then_inc(sem_e[rec.engine], 1)
                if engine_name == "sync":
                    for q in QUEUES:
                        for s in range(QSLOTS[q]):
                            u = kb.slot_user[q][s]
                            if u is not None:
                                eng.wait_ge(sem_q[(q, s)], u.token)
            return run

        block.sync(section("sync"))
        block.tensor(section("tensor"))
        block.vector(section("vector"))
        block.scalar(section("scalar"))
        block.gpsimd(section("gpsimd"))


def build(debug=False, stages=('ln','qkv','p1c','p2','p3','p5')):
    nc = bass.Bass("TRN2", target_bir_lowering=False)
    k = KB(nc)

    I = {}
    def inp(name, shape):
        I[name] = nc.dram_tensor(name, shape, F32, kind="ExternalInput").ap()
    inp("x", [L, D])
    inp("Wqkv", [D, 3 * D]); inp("Wlr", [D, 12]); inp("Wm", [D, 4])
    inp("bm", [4]); inp("Ws", [D, 4]); inp("bs", [4]); inp("Wo", [D, D])
    inp("w0", [NH, 512, 512]); inp("w1", [NH, 512, 512]); inp("w2", [NH, 512, 512])
    inp("o_norm_g", [512]); inp("qk_scale", [D, 2]); inp("qk_offset", [D, 2])
    inp("in_gamma", [D]); inp("in_beta", [D])
    inp("agg_w", [D, D]); inp("agg_b", [D])
    inp("pn_gamma", [D]); inp("pn_beta", [D])
    inp("d1_sl", [D, 1024]); inp("d1b_sl", [1024])
    inp("d2_sl", [1024, D]); inp("d2_b", [D])
    inp("onehot", [8, 1])
    inp("bc_w", [D, D]); inp("bc_b", [D])
    inp("gate_w", [D, D]); inp("gate_b", [D])
    inp("out_gamma", [D]); inp("out_beta", [D])
    out_t = nc.dram_tensor("out", [L, D], F32, kind="ExternalOutput").ap()

    sckind = "ExternalOutput" if debug else "Internal"
    sc = {}
    scb = {}
    def scratch(name, shape, dtype):
        sc[name] = nc.dram_tensor("sc_" + name, shape, dtype, kind=sckind).ap()
        scb[name] = Buf("scb_" + name)
    scratch("aq_tm", [L, D], BF16); scratch("ak_tm", [L, D], BF16)
    scratch("av", [L, D], BF16)
    scratch("Q_tm", [L, D], BF16); scratch("K", [L, D], BF16)
    scratch("V", [L, D], BF16)
    scratch("aq", [D, L], BF16); scratch("ak", [D, L], BF16)
    scratch("QT", [D, L], BF16); scratch("KT", [D, L], BF16)
    scratch("VT", [D, L], BF16)
    scratch("ttt", [D, L], BF16)
    scratch("x2", [L, D], F32); scratch("xf", [L, D], F32)
    scratch("gate", [L, D], BF16)
    scratch("xfT", [D, L], BF16)

    dbg = {}
    if debug:
        def dbgout(name, shape, dtype):
            dbg[name] = nc.dram_tensor("dbg_" + name, shape, dtype,
                                       kind="ExternalOutput").ap()
        dbgout("lrv", [128, TT * 20], F32)
        dbgout("ln0T", [128, DT * L], BF16)
        dbgout("throw", [8, 2048], F32)
        dbgout("open0", [128, 2048], F32)
        dbgout("gb0", [128, 2048], F32)
        dbgout("gam6", [128, 2048], F32)
        dbgout("TB8", [8, 2048], F32)
        dbgout("Bt5", [128, 2048], F32)

    k.cbuf = k.sb("cbuf", [128, 32], F32)
    ident = k.sb("ident", [128, 128], BF16)
    ident_f = k.sb("ident_f", [128, 128], F32)
    ones_col = k.sb("ones_col", [128, 1], BF16)
    ones_row = k.sb("ones_row", [1, 128], BF16)
    ones_col_f = k.sb("ones_col_f", [128, 1], F32)
    ones_row_f = k.sb("ones_row_f", [1, 128], F32)
    b_ident = Buf("ident"); b_identf = Buf("identf"); b_ones = Buf("ones")

    qksc = k.sb("qksc", [128, 32], F32)
    qkof = k.sb("qkof", [128, 32], F32)
    sAs = k.sb("sAs", [128, 16], F32)
    oAs = k.sb("oAs", [128, 16], F32)
    b_qk = Buf("qk"); b_qks = Buf("qks")

    lrms_w = k.sb("lrms_w", [128, DT * 20], BF16)
    bmbs = k.sb("bmbs", [1, 20], BF16)
    b_lrmsw = Buf("lrms_w"); b_bmbs = Buf("bmbs")
    lrv = k.sb("lrv", [128, TT * 20], F32)
    b_lrv = [Buf(f"lrv{t}") for t in range(TT)]

    for v in (1e-5, -0.5, 0.5, -1.0, 1.0, 1e-6, 1.0 / 512.0, 1.0 / D, 1e-7):
        k.constv(v)

    def init_cbuf(g):
        inst = None
        for v, idx in k.consts.items():
            inst = g.memset(k.cbuf[:, idx:idx + 1], v)
        return inst
    k.op("gpsimd", init_cbuf, writes=[k.const_buf], tag="init_cbuf")

    def init_ones(g):
        g.memset(ones_col[:], 1.0)
        g.memset(ones_row[:], 1.0)
        g.memset(ones_col_f[:], 1.0)
        return g.memset(ones_row_f[:], 1.0)
    k.op("gpsimd", init_ones, writes=[b_ones], tag="init_ones")
    k.op("gpsimd", lambda g: g.memset(ident_f[:], 1.0), writes=[b_identf])
    k.op("gpsimd", lambda g: g.affine_select(
        ident_f[:], ident_f[:], [[1, 128]], ALU.is_equal, 0.0,
        base=0, channel_multiplier=-1), reads=[b_identf], writes=[b_identf])
    k.op("gpsimd", lambda g: g.memset(ident[:], 1.0), writes=[b_ident])
    k.op("gpsimd", lambda g: g.affine_select(
        ident[:], ident[:], [[1, 128]], ALU.is_equal, 0.0,
        base=0, channel_multiplier=-1), reads=[b_ident], writes=[b_ident])

    for f in range(16):
        k.dma("sync", qksc[:, 2 * f:2 * f + 2], I["qk_scale"][128 * f:128 * (f + 1), :],
              writes=[b_qk], tag="qksc")
        k.dma("sync", qkof[:, 2 * f:2 * f + 2], I["qk_offset"][128 * f:128 * (f + 1), :],
              writes=[b_qk], tag="qkof")
    qksc_r = qksc[:].rearrange("p (f two) -> p two f", two=2)
    qkof_r = qkof[:].rearrange("p (f two) -> p two f", two=2)
    k.op("vector", lambda v: v.tensor_scalar_mul(sAs[:], qksc_r[:, 0], RS),
         reads=[b_qk], writes=[b_qks], tag="sAs")
    k.op("vector", lambda v: v.tensor_scalar_mul(oAs[:], qkof_r[:, 0], RS),
         reads=[b_qk], writes=[b_qks], tag="oAs")

    for d in range(DT):
        r = slice(128 * d, 128 * (d + 1))
        c0 = d * 20
        k.dma("gpsimd", lrms_w[:, c0:c0 + 12], I["Wlr"][r, :], writes=[b_lrmsw])
        k.dma("gpsimd", lrms_w[:, c0 + 12:c0 + 16], I["Wm"][r, :], writes=[b_lrmsw])
        k.dma("gpsimd", lrms_w[:, c0 + 16:c0 + 20], I["Ws"][r, :], writes=[b_lrmsw])
    onorm = k.sb("onorm", [128, 4], F32)
    b_onorm = Buf("onorm")
    for ot in range(4):
        k.dma("sync", onorm[:, ot:ot + 1], I["o_norm_g"][128 * ot:128 * (ot + 1)],
              writes=[b_onorm], tag="onorm")
    k.op("gpsimd", lambda g: g.memset(bmbs[:], 0.0), writes=[b_bmbs])
    k.dma("gpsimd", bmbs[:, 12:16], I["bm"][None, :], writes=[b_bmbs])
    k.dma("gpsimd", bmbs[:, 16:20], I["bs"][None, :], writes=[b_bmbs])

    p1ps = ExitStack()
    ps_tp = [k.ps(f"ps_tp{i}", [128, 2048], BF16, es=p1ps) for i in range(2)]
    b_pstp = [Buf(f"pstp{i}", psum=True) for i in range(2)]
    ps_f = [k.ps(f"ps_f{i}", [128, 512], F32, es=p1ps) for i in range(4)]
    b_psf = [Buf(f"psf{i}", psum=True) for i in range(4)]

    ln0T_es = ExitStack()
    ln0T = k.sb("ln0T", [128, DT * L], BF16, es=ln0T_es)
    b_ln0T = [Buf(f"ln0T{t}") for t in range(TT)]

    def ln0T_tile(d, t):
        return ln0T[:, d * L + 128 * t: d * L + 128 * (t + 1)]

    if 'ln' not in stages:
        stages = stages
    with ExitStack() as esln:
        xb = [k.sb(f"xb{i}", [128, D], F32, es=esln) for i in range(2)]
        b_xb = [Buf(f"xb{i}") for i in range(2)]
        ln0b = [k.sb(f"ln0b{i}", [128, D], BF16, es=esln) for i in range(2)]
        b_ln0b = [Buf(f"ln0b{i}") for i in range(2)]
        sqv = k.sb("sqv", [128, D], F32, es=esln)
        b_sqv = Buf("sqv")
        stat = k.sb("stat", [128, 8], F32, es=esln)
        b_stat = Buf("stat")
        lr_scr = k.sb("lr_scr", [128, 24], F32, es=esln)
        b_lrscr = Buf("lrscr")
        b_pslr = b_psf[0]
        ps_lr = ps_f[0]

        for t in range(TT):
            xi = t % 2
            bx, bl = b_xb[xi], b_ln0b[xi]
            xcur, lcur = xb[xi], ln0b[xi]
            k.dma("sync", xcur[:], I["x"][128 * t:128 * (t + 1), :],
                  writes=[bx], tag=f"x{t}")
            k.op("vector", lambda v, x_=xcur: v.reduce_sum(
                stat[:, 0:1], x_[:], axis=AX.X), reads=[bx], writes=[b_stat])
            k.op("scalar", lambda s, x_=xcur: s.activation(
                sqv[:], x_[:], AF.Square, accum_out=stat[:, 1:2]),
                reads=[bx, k.const_buf], writes=[b_sqv, b_stat])
            k.op("vector", lambda v: v.tensor_scalar_mul(
                stat[:, 2:4], stat[:, 0:2], 1.0 / D),
                reads=[b_stat], writes=[b_stat])
            k.op("vector", lambda v: v.tensor_tensor(
                stat[:, 4:5], stat[:, 2:3], stat[:, 2:3], ALU.mult),
                reads=[b_stat], writes=[b_stat])
            k.op("vector", lambda v: v.tensor_sub(
                stat[:, 5:6], stat[:, 3:4], stat[:, 4:5]),
                reads=[b_stat], writes=[b_stat])
            k.op("scalar", lambda s: s.activation(
                stat[:, 6:7], stat[:, 5:6], AF.Ln, bias=k.constv(1e-5)),
                reads=[b_stat, k.const_buf], writes=[b_stat])
            k.op("scalar", lambda s: s.activation(
                stat[:, 7:8], stat[:, 6:7], AF.Exp, scale=k.constv(-0.5)),
                reads=[b_stat, k.const_buf], writes=[b_stat])
            k.op("vector", lambda v, x_=xcur, o=lcur: v.tensor_scalar(
                o[:], x_[:], stat[:, 2:3], stat[:, 7:8], ALU.subtract, ALU.mult),
                reads=[bx, b_stat], writes=[bl])
            pi = t % 2
            for j in range(DT):
                k.op("tensor", lambda tt_, j=j, pi=pi, src=lcur: tt_.transpose(
                    ps_tp[pi][:, 128 * j:128 * (j + 1)],
                    src[:, 128 * j:128 * (j + 1)], ident[:]),
                    reads=[bl, b_ident], writes=[b_pstp[pi]], tag=f"tpln{t}_{j}")
            dstv = ln0T[:].rearrange("p (d c) -> p d c", d=DT)[:, :, 128 * t:128 * (t + 1)]
            srcv = ps_tp[pi][:].rearrange("p (d c) -> p d c", d=DT)
            k.op("vector", lambda v, dstv=dstv, srcv=srcv: v.tensor_copy(dstv, srcv),
                 reads=[b_pstp[pi]], writes=[b_ln0T[t]], tag=f"evln0T{t}")

            k.op("tensor", lambda tt_: tt_.matmul(
                ps_lr[:, 0:20], ones_row[:], bmbs[:], start=True, stop=False,
                skip_group_check=True),
                reads=[b_ones, b_bmbs], writes=[b_pslr], tag=f"lrbias{t}")
            for d in range(DT):
                k.op("tensor", lambda tt_, d=d, t=t: tt_.matmul(
                    ps_lr[:, 0:20], ln0T_tile(d, t),
                    lrms_w[:, d * 20:(d + 1) * 20],
                    start=False, stop=(d == DT - 1), skip_group_check=True),
                    reads=[b_ln0T[t], b_lrmsw], writes=[b_pslr], tag=f"lrmm{t}")
            c0 = t * 20
            k.op("vector", lambda v: v.tensor_scalar_add(
                lr_scr[:, 0:12], ps_lr[:, 0:12], BASE_LR_INV),
                reads=[b_pslr], writes=[b_lrscr])
            k.op("scalar", lambda s: s.activation(
                lr_scr[:, 12:24], lr_scr[:, 0:12], AF.Abs),
                reads=[b_lrscr, k.const_buf], writes=[b_lrscr])
            k.op("scalar", lambda s: s.activation(
                lr_scr[:, 12:24], lr_scr[:, 12:24], AF.Exp, scale=k.constv(-1.0)),
                reads=[b_lrscr, k.const_buf], writes=[b_lrscr])
            k.op("scalar", lambda s: s.activation(
                lr_scr[:, 12:24], lr_scr[:, 12:24], AF.Ln, bias=k.constv(1.0)),
                reads=[b_lrscr, k.const_buf], writes=[b_lrscr])
            k.op("vector", lambda v: v.tensor_scalar_max(
                lr_scr[:, 0:12], lr_scr[:, 0:12], 0.0),
                reads=[b_lrscr], writes=[b_lrscr])
            k.op("vector", lambda v, c0=c0: v.tensor_add(
                lrv[:, c0:c0 + 12], lr_scr[:, 0:12], lr_scr[:, 12:24]),
                reads=[b_lrscr], writes=[b_lrv[t]])
            k.op("scalar", lambda s, c0=c0: s.activation(
                lrv[:, c0 + 12:c0 + 16], ps_lr[:, 12:16], AF.Sigmoid),
                reads=[b_pslr, k.const_buf], writes=[b_lrv[t]])
            k.op("scalar", lambda s: s.activation(
                lr_scr[:, 0:4], ps_lr[:, 16:20], AF.Sigmoid),
                reads=[b_pslr, k.const_buf], writes=[b_lrscr])
            k.op("vector", lambda v, c0=c0: v.tensor_tensor(
                lrv[:, c0 + 16:c0 + 20], lr_scr[:, 0:4], ps_lr[:, 16:20], ALU.mult),
                reads=[b_lrscr, b_pslr], writes=[b_lrv[t]])
    k.barrier()

    if debug:
        k.dma("scalar", dbg["lrv"], lrv[:], reads=[*b_lrv], tag="dbg_lrv")
        k.dma("scalar", dbg["ln0T"], ln0T[:], reads=[*b_ln0T], tag="dbg_ln0T")

    if 'qkv' in stages:
     with ExitStack() as esq:
        wqb = [k.sb(f"wqb{i}", [128, DT * 512], BF16, es=esq) for i in range(2)]
        b_wqb = [Buf(f"wqb{i}") for i in range(2)]
        sigA = [k.sb(f"sig{i}", [128, 512], F32, es=esq) for i in range(2)]
        b_sigA = [Buf(f"sig{i}") for i in range(2)]
        fvalA = [k.sb(f"fval{i}", [128, 512], F32, es=esq) for i in range(2)]
        b_fvalA = [Buf(f"fval{i}") for i in range(2)]
        st8 = [k.sb(f"st8_{i}", [128, 512], BF16, es=esq) for i in range(8)]
        b_st8 = [Buf(f"st8_{i}") for i in range(8)]
        nstatA = [k.sb(f"nstat{i}", [128, 6], F32, es=esq) for i in range(2)]
        b_nstatA = [Buf(f"nstat{i}") for i in range(2)]
        bc_sA = k.sb("bc_sA", [128, 2048], F32, es=esq)
        bc_oA = k.sb("bc_oA", [128, 2048], F32, es=esq)
        bc_sB = k.sb("bc_sB", [128, 2048], F32, es=esq)
        bc_oB = k.sb("bc_oB", [128, 2048], F32, es=esq)
        b_bcqk = Buf("bcqk")
        rowtmp = k.sb("rowtmp", [16, 128], F32, es=esq)
        b_rowtmp = Buf("rowtmp")
        rowbuf = k.sb("rowbuf", [1, 2048], F32, es=esq)
        b_rowbuf = Buf("rowbuf")
        rowdram = nc.dram_tensor("rowdram", [16, 128], F32, kind="Internal").ap()

        def build_qk_bcast(src_ap, b_src, dst, tag):
            k.op("tensor", lambda tt_, s=src_ap: tt_.transpose(
                ps_f[0][0:16, 0:128], s, ident_f[:]),
                reads=[b_src, b_identf], writes=[b_psf[0]], tag=f"tp_{tag}")
            k.op("vector", lambda v: v.tensor_copy(rowtmp[:], ps_f[0][0:16, 0:128]),
                 reads=[b_psf[0]], writes=[b_rowtmp])
            k.dma("sync", rowdram, rowtmp[:], reads=[b_rowtmp], writes=[b_rowtmp],
                  tag=f"r1_{tag}")
            k.dma("sync", rowbuf[:], rowdram.rearrange("p n -> (p n)")[None, :],
                  reads=[b_rowtmp], writes=[b_rowbuf], tag=f"r2_{tag}")
            for i in range(4):
                k.op("tensor", lambda tt_, i=i: tt_.matmul(
                    ps_f[i][:], ones_row_f[:],
                    rowbuf[:, 512 * i:512 * (i + 1)], start=True, stop=True),
                    reads=[b_ones, b_rowbuf], writes=[b_psf[i]], tag=f"bc_{tag}")
            for i in range(4):
                k.op("vector", lambda v, i=i, dst=dst: v.tensor_copy(
                    dst[:, 512 * i:512 * (i + 1)], ps_f[i][:]),
                    reads=[b_psf[i]], writes=[b_bcqk], tag=f"bcc_{tag}")

        import os
        if os.environ.get("NO_BCAST") != "1":
            build_qk_bcast(sAs[:], b_qks, bc_sA, "sA")
            build_qk_bcast(oAs[:], b_qks, bc_oA, "oA")
            build_qk_bcast(qksc_r[:, 1], b_qk, bc_sB, "sB")
            build_qk_bcast(qkof_r[:, 1], b_qk, bc_oB, "oB")

        stg_i = [0]
        def next_stage():
            i = stg_i[0] % 8
            stg_i[0] += 1
            return st8[i], b_st8[i]

        def silu_l2norm_store(bp, ps, dst_ap, dst_buf, tag, ri=0):
            sig, b_sig = sigA[ri], b_sigA[ri]
            fval, b_fval = fvalA[ri], b_fvalA[ri]
            nstat, b_nstat = nstatA[ri], b_nstatA[ri]
            stg2, bstg2 = next_stage()
            k.op("scalar", lambda s, ps=ps: s.activation(sig[:], ps[:], AF.Sigmoid),
                 reads=[bp, k.const_buf], writes=[b_sig])
            k.op("vector", lambda v, ps=ps: v.tensor_tensor(
                fval[:], sig[:], ps[:], ALU.mult),
                reads=[b_sig, bp], writes=[b_fval])
            k.op("scalar", lambda s: s.activation(
                sig[:], fval[:], AF.Square, accum_out=nstat[:, 0:1]),
                reads=[b_fval, k.const_buf], writes=[b_sig, b_nstat])
            k.op("scalar", lambda s: s.activation(
                nstat[:, 1:2], nstat[:, 0:1], AF.Ln),
                reads=[b_nstat, k.const_buf], writes=[b_nstat])
            k.op("scalar", lambda s: s.activation(
                nstat[:, 2:3], nstat[:, 1:2], AF.Exp, scale=k.constv(0.5)),
                reads=[b_nstat, k.const_buf], writes=[b_nstat])
            k.op("vector", lambda v: v.tensor_scalar_add(
                nstat[:, 3:4], nstat[:, 2:3], 1e-5),
                reads=[b_nstat], writes=[b_nstat])
            k.op("vector", lambda v: v.reciprocal(nstat[:, 4:5], nstat[:, 3:4]),
                 reads=[b_nstat], writes=[b_nstat])
            k.op("vector", lambda v, stg=stg2: v.tensor_scalar_mul(
                stg[:], fval[:], nstat[:, 4:5]),
                reads=[b_fval, b_nstat], writes=[bstg2])
            k.dma("sync", dst_ap, stg2[:], reads=[bstg2],
                  writes=[dst_buf], tag=tag)

        def affine_store(bp, ps, bcs, bco, dst_ap, dst_buf, cs, tag, ri=0):
            fval, b_fval = fvalA[ri], b_fvalA[ri]
            stg, bstg = next_stage()
            k.op("vector", lambda v, ps=ps, c=cs: v.tensor_tensor(
                fval[:], ps[:], bcs[:, c], ALU.mult),
                reads=[bp, b_bcqk], writes=[b_fval])
            k.op("vector", lambda v, stg=stg, c=cs: v.tensor_add(
                stg[:], fval[:], bco[:, c]),
                reads=[b_fval, b_bcqk], writes=[bstg])
            k.dma("sync", dst_ap, stg[:], reads=[bstg],
                  writes=[dst_buf], tag=tag)

        qkv_blocks = [int(v) for v in os.environ.get(
            "QKV_BLOCKS", "0,1,2,3,4,5,6,7,8,9,10,11").split(",") if v]
        for b in qkv_blocks:
            wi = b % 2
            bw = b_wqb[wi]
            wcur = wqb[wi]
            for d in range(DT):
                k.dma("gpsimd", wcur[:, 512 * d:512 * (d + 1)],
                      I["Wqkv"][128 * d:128 * (d + 1), 512 * b:512 * (b + 1)],
                      writes=[bw], tag=f"wqb{b}_{d}")
            for t in range(TT):
                pi = (b * TT + t) % 4
                bp = b_psf[pi]
                ps = ps_f[pi]
                for d in range(DT):
                    k.op("tensor", lambda tt_, d=d, t=t, w=wcur, ps=ps: tt_.matmul(
                        ps[:], ln0T_tile(d, t), w[:, 512 * d:512 * (d + 1)],
                        start=(d == 0), stop=(d == DT - 1)),
                        reads=[b_ln0T[t], bw], writes=[bp], tag=f"qkv{b}_{t}")
                tr = slice(128 * t, 128 * (t + 1))
                cs = slice(512 * (b % 4), 512 * (b % 4 + 1))
                if os.environ.get("SKIP_EVAC") == "1":
                    continue
                ri = t % 2
                if b < 4:
                    affine_store(bp, ps, bc_sA, bc_oA, sc["aq_tm"][tr, cs], scb["aq_tm"], cs,
                                 f"sp_aq{b}_{t}", ri=ri)
                    silu_l2norm_store(bp, ps, sc["Q_tm"][tr, cs], scb["Q_tm"], f"sp_Q{b}_{t}", ri=ri)
                elif b < 8:
                    affine_store(bp, ps, bc_sB, bc_oB, sc["ak_tm"][tr, cs], scb["ak_tm"], cs,
                                 f"sp_ak{b}_{t}", ri=ri)
                    silu_l2norm_store(bp, ps, sc["K"][tr, cs], scb["K"], f"sp_K{b}_{t}", ri=ri)
                else:
                    sub = os.environ.get("VSUB", "all")
                    sig, b_sig = sigA[t % 2], b_sigA[t % 2]
                    stg, bstg = next_stage()
                    if sub in ("all", "copy"):
                        k.op("vector", lambda v, ps=ps, stg=stg: v.tensor_copy(
                            stg[:], ps[:]), reads=[bp], writes=[bstg])
                    if sub in ("all", "copy-spill"):
                        k.op("vector", lambda v, ps=ps, stg=stg: v.tensor_copy(
                            stg[:], ps[:]), reads=[bp], writes=[bstg])
                        k.dma("sync", sc["av"][tr, cs], stg[:], reads=[bstg], writes=[scb["av"]],
                              tag=f"sp_av{b}_{t}")
                    if sub == "sig":
                        stg2, bstg2 = next_stage()
                        k.op("scalar", lambda s, ps=ps: s.activation(
                            sig[:], ps[:], AF.Sigmoid),
                            reads=[bp, k.const_buf], writes=[b_sig])
                        k.op("vector", lambda v, ps=ps, stg=stg2: v.tensor_tensor(
                            stg[:], sig[:], ps[:], ALU.mult),
                            reads=[b_sig, bp], writes=[bstg2])
                        k.dma("sync", sc["V"][tr, cs], stg2[:], reads=[bstg2], writes=[scb["V"]],
                              tag=f"sp_V{b}_{t}")
                    if sub == "all":
                        k.dma("sync", sc["av"][tr, cs], stg[:], reads=[bstg], writes=[scb["av"]],
                              tag=f"sp_av{b}_{t}")
                        stg2, bstg2 = next_stage()
                        k.op("scalar", lambda s, ps=ps: s.activation(
                            sig[:], ps[:], AF.Sigmoid),
                            reads=[bp, k.const_buf], writes=[b_sig])
                        k.op("vector", lambda v, ps=ps, stg=stg2: v.tensor_tensor(
                            stg[:], sig[:], ps[:], ALU.mult),
                            reads=[b_sig, bp], writes=[bstg2])
                        k.dma("sync", sc["V"][tr, cs], stg2[:], reads=[bstg2], writes=[scb["V"]],
                              tag=f"sp_V{b}_{t}")
    ln0T_es.close()
    k.barrier()

    if 'p1c' in stages:
     with ExitStack() as esc:
        fmres = k.sb("fmres", [128, DT * 2048], BF16, es=esc)
        b_fm = Buf("fmres")
        tmb = [k.sb(f"tmb{i}", [128, 2048], BF16, es=esc) for i in range(2)]
        b_tmb = [Buf(f"tmb{i}") for i in range(2)]
        for src_name, dst_name in (("aq_tm", "aq"), ("ak_tm", "ak"),
                                   ("Q_tm", "QT"), ("K", "KT"), ("V", "VT")):
            for t in range(TT):
                ti = t % 2
                tcur = tmb[ti]
                k.dma("sync", tcur[:], sc[src_name][128 * t:128 * (t + 1), :],
                      reads=[scb[src_name]], writes=[b_tmb[ti]],
                      tag=f"ld_{src_name}{t}")
                pi = t % 2
                for j in range(DT):
                    k.op("tensor", lambda tt_, j=j, pi=pi, s=tcur: tt_.transpose(
                        ps_tp[pi][:, 128 * j:128 * (j + 1)],
                        s[:, 128 * j:128 * (j + 1)], ident[:]),
                        reads=[b_tmb[ti], b_ident], writes=[b_pstp[pi]],
                        tag=f"tp_{src_name}{t}_{j}")
                dstv = fmres[:].rearrange("p (d c) -> p d c", d=DT)[:, :, 128 * t:128 * (t + 1)]
                srcv = ps_tp[pi][:].rearrange("p (d c) -> p d c", d=DT)
                k.op("vector", lambda v, dstv=dstv, srcv=srcv: v.tensor_copy(dstv, srcv),
                     reads=[b_pstp[pi]], writes=[b_fm], tag=f"ev_{src_name}{t}")
            for j in range(DT):
                k.dma("sync", sc[dst_name][128 * j:128 * (j + 1), :],
                      fmres[:, 2048 * j:2048 * (j + 1)],
                      reads=[b_fm], writes=[scb[dst_name]],
                      tag=f"sp_{dst_name}{j}")
            k.barrier()
    p1ps.close()
    k.barrier()

    if 'p2' in stages:
     with ExitStack() as es2:
        pb = [k.ps(f"pb{i}", [128, 512], F32, es=es2) for i in range(6)]
        bpb = [Buf(f"pb{i}", psum=True) for i in range(6)]
        ps2tp = k.ps("ps2tp", [128, 2048], BF16, es=es2); b_ps2tp = Buf("ps2tp", psum=True)
        _bi = [0]

        def nb():
            i = _bi[0] % 6
            _bi[0] += 1
            return pb[i], bpb[i]

        W = [k.sb(f"Wst{s}", [128, 2048], BF16, es=es2) for s in range(3)]
        b_W = [Buf(f"W{s}") for s in range(3)]
        WT = [k.sb(f"WTst{s}", [128, 2048], BF16, es=es2) for s in range(3)]
        b_WT = [Buf(f"WT{s}") for s in range(3)]
        Mst = [k.sb(f"Mst{s}", [128, 2048], BF16, es=es2) for s in range(3)]
        b_M = [Buf(f"M{s}") for s in range(3)]
        nbuf = k.sb("nbuf", [128, 12], F32, es=es2); b_nbuf = Buf("nbuf")
        qt_h = k.sb("qt_h", [128, 4096], BF16, es=es2); b_qt = Buf("qt")
        kt_h = k.sb("kt_h", [128, 4096], BF16, es=es2); b_kt = Buf("kt")
        vt_h = k.sb("vt_h", [128, 4096], BF16, es=es2); b_vt = Buf("vt")
        k_h = k.sb("k_h", [128, 4096], BF16, es=es2); b_kh = Buf("kh")
        v_h = k.sb("v_h", [128, 4096], BF16, es=es2); b_vh = Buf("vh")
        GHT = k.sb("GHT", [128, 4096], BF16, es=es2); b_GHT = Buf("GHT")
        sgg2 = [k.sb(f"sgg2_{i}", [128, 512], F32, es=es2) for i in range(2)]
        b_sgg2 = [Buf(f"sgg2_{i}") for i in range(2)]
        t1g2 = [k.sb(f"t1g2_{i}", [128, 512], F32, es=es2) for i in range(2)]
        b_t1g2 = [Buf(f"t1g2_{i}") for i in range(2)]
        outT = k.sb("outT", [128, 4096], F32, es=es2); b_outT = Buf("outT")
        wf32v = outT
        b_wf32 = b_outT
        sq2 = outT[:, 2048:2560]
        b_sq2 = b_outT
        wnscr = outT[:, 2560:3072]
        b_wnscr = b_outT
        t_scr = outT[:, 0:2048]
        b_tscr = b_outT
        sqb1_0 = k.sb("sqb1", [128, 1024], BF16, es=es2)
        sqb1 = [sqb1_0, sqb1_0]
        bsq0 = Buf("sqb1", )
        b_sqb1 = [bsq0, bsq0]
        gbaF = [k.sb(f"gbaF{i}", [128, 512], F32, es=es2) for i in range(2)]
        b_gbaF = [Buf(f"gbaF{i}") for i in range(2)]
        hbmF = [k.sb(f"hbmF{i}", [128, 512], F32, es=es2) for i in range(2)]
        b_hbmF = [Buf(f"hbmF{i}") for i in range(2)]
        sFA = [k.sb(f"sF{i}", [128, 512], F32, es=es2) for i in range(2)]
        b_sFA = [Buf(f"sF{i}") for i in range(2)]
        dhF_0 = k.sb("dhF", [128, 512], F32, es=es2)
        dhFA = [dhF_0, dhF_0]
        bdh0 = Buf("dhF")
        b_dhFA = [bdh0, bdh0]
        sgF = k.sb("sgF", [128, 512], F32, es=es2); b_sgF = Buf("sgF")
        t1F = k.sb("t1F", [128, 512], F32, es=es2); b_t1F = Buf("t1F")
        v1F = k.sb("v1F", [128, 512], F32, es=es2); b_v1F = Buf("v1F")
        dgba = k.sb("dgba", [128, 4096], BF16, es=es2); b_dgba = Buf("dgba")
        dhbm = k.sb("dhbm", [128, 4096], BF16, es=es2); b_dhbm = Buf("dhbm")
        H1b = k.sb("H1b", [128, 4096], BF16, es=es2); b_H1 = Buf("H1")
        KL0 = k.sb("KL0", [128, 4096], BF16, es=es2); b_KL0 = Buf("KL0")
        KL2 = k.sb("KL2", [128, 4096], BF16, es=es2); b_KL2 = Buf("KL2")
        Xb = [k.sb(f"Xb{i}", [128, 2048], BF16, es=es2) for i in range(2)]
        b_Xb = [Buf(f"X{i}") for i in range(2)]
        Ab = k.sb("Ab", [128, 2048], BF16, es=es2); b_Ab = Buf("Ab")
        XTb = k.sb("XTb", [128, 2048], BF16, es=es2); b_XTb = Buf("XTb")
        Yb = k.sb("Yb", [128, 2048], BF16, es=es2); b_Yb = Buf("Yb")
        nst2 = k.sb("nst2", [128, 8], F32, es=es2); b_nst2 = Buf("nst2")
        mmb = k.sb("mmb", [128, 1], F32, es=es2); b_mmb = Buf("mmb")
        momb = k.sb("momb", [128, 8], BF16, es=es2); b_momb = Buf("momb")
        rnst = k.sb("rnst", [128, 8], F32, es=es2); b_rnst = Buf("rnst")
        srow_c = k.sb("srow_c", [128, 8], F32, es=es2); b_srowc = Buf("srow_c")
        rowt2 = k.sb("rowt2", [8, 128], F32, es=es2); b_rowt2 = Buf("rowt2")
        scrow = k.sb("scrow", [1, 1024], F32, es=es2); b_scrow = Buf("scrow")
        Btl = k.sb("Btl", [128, 1024], F32, es=es2); b_Btl = Buf("Btl")
        msrow = Btl[0:1, 0:1024]; b_msrow = b_Btl
        frow = msrow; b_frow = b_Btl
        stg2 = [k.sb(f"stg2_{i}", [128, 1024], BF16, es=es2) for i in range(2)]
        b_stg2 = [Buf(f"stg2_{i}") for i in range(2)]
        rowd2 = nc.dram_tensor("rowd2", [8, 128], F32, kind="Internal").ap()
        lrview = lrv[:].rearrange("p (t j) -> p j t", j=20)

        for a_, b_, c_ in NS_COEFFS:
            k.constv(b_)
            k.constv(c_ / b_)

        def tp16_pack(src_sb, b_src, dst_sb, b_dst, tag):
            for r in range(4):
                for j in range(4):
                    k.op("tensor", lambda tt_, r=r, j=j: tt_.transpose(
                        ps2tp[:, (r * 4 + j) * 128:(r * 4 + j) * 128 + 128],
                        src_sb[:, 512 * r + 128 * j: 512 * r + 128 * j + 128],
                        ident[:]),
                        reads=[b_src, b_ident], writes=[b_ps2tp], tag=f"tp_{tag}")
            dstv = dst_sb[:].rearrange("p (j r c) -> p j r c", j=4, r=4)
            dstv = dstv.transpose([0, 2, 1, 3])
            srcv = ps2tp[:].rearrange("p (r j c) -> p r j c", r=4, j=4)
            k.op("vector", lambda v: v.tensor_copy(dstv, srcv),
                 reads=[b_ps2tp], writes=[b_dst], tag=f"ev_{tag}")

        def wtile(sb_, r, j):
            return sb_[:, 512 * r + 128 * j: 512 * r + 128 * j + 128]

        def wblk(sb_, r):
            return sb_[:, 512 * r:512 * (r + 1)]

        for h in range(NH):
            for s, wname in enumerate(("w0", "w1", "w2")):
                for r in range(4):
                    k.dma("gpsimd", wblk(W[s], r),
                          I[wname][h, 128 * r:128 * (r + 1), :],
                          writes=[b_W[s]], tag=f"p2W{s}")
                    k.dma("sync", wblk(wf32v, r),
                          I[wname][h, 128 * r:128 * (r + 1), :],
                          writes=[b_wf32], tag=f"p2Wf{s}")
                for r in range(4):
                    k.op("vector", lambda v, s=s, r=r: v.scalar_tensor_tensor(
                        sq2, wblk(wf32v, r), 1.0, wblk(wf32v, r),
                        ALU.mult, ALU.mult,
                        accum_out=nbuf[:, 4 * s + r:4 * s + r + 1]),
                        reads=[b_wf32], writes=[b_sq2, b_nbuf], tag="p2norm")
            k.op("scalar", lambda s_: s_.activation(nbuf[:], nbuf[:], AF.Ln),
                 reads=[b_nbuf, k.const_buf], writes=[b_nbuf])
            k.op("scalar", lambda s_: s_.activation(
                nbuf[:], nbuf[:], AF.Exp, scale=k.constv(0.5)),
                reads=[b_nbuf, k.const_buf], writes=[b_nbuf])

            for c in range(2):
                cs_ = slice(1024 * c, 1024 * (c + 1))
                for d in range(4):
                    rr = slice(512 * h + 128 * d, 512 * h + 128 * (d + 1))
                    k.dma("sync", qt_h[:, 1024 * d:1024 * (d + 1)],
                          sc["QT"][rr, cs_], reads=[scb["QT"]], writes=[b_qt],
                          tag=f"p2qt{h}{c}")
                    k.dma("sync", kt_h[:, 1024 * d:1024 * (d + 1)],
                          sc["KT"][rr, cs_], reads=[scb["KT"]], writes=[b_kt],
                          tag=f"p2kt{h}{c}")
                    k.dma("sync", vt_h[:, 1024 * d:1024 * (d + 1)],
                          sc["VT"][rr, cs_], reads=[scb["VT"]], writes=[b_vt],
                          tag=f"p2vt{h}{c}")
                for ct in range(8):
                    tr = slice(128 * (8 * c + ct), 128 * (8 * c + ct + 1))
                    hc = slice(512 * h, 512 * (h + 1))
                    k.dma("sync", k_h[:, 512 * ct:512 * (ct + 1)],
                          sc["K"][tr, hc], reads=[scb["K"]], writes=[b_kh],
                          tag=f"p2k{h}{c}")
                    k.dma("sync", v_h[:, 512 * ct:512 * (ct + 1)],
                          sc["V"][tr, hc], reads=[scb["V"]], writes=[b_vh],
                          tag=f"p2v{h}{c}")

                for s in range(3):
                    tp16_pack(W[s], b_W[s], WT[s], b_WT[s], f"WT{h}{c}{s}")

                for ht in range(4):
                    for cc in range(2):
                        pA, bA = nb()
                        pB, bB = nb()
                        for d in range(4):
                            k.op("tensor", lambda tt_, d=d, cc=cc, ht=ht, pA=pA:
                                 tt_.matmul(
                                     pA[:], wtile(WT[0], d, ht),
                                     qt_h[:, 1024 * d + 512 * cc:
                                          1024 * d + 512 * cc + 512],
                                     start=(d == 0), stop=(d == 3)),
                                 reads=[b_WT[0], b_qt], writes=[bA], tag="gp_mm")
                        for d in range(4):
                            k.op("tensor", lambda tt_, d=d, cc=cc, ht=ht, pB=pB:
                                 tt_.matmul(
                                     pB[:], wtile(WT[2], d, ht),
                                     qt_h[:, 1024 * d + 512 * cc:
                                          1024 * d + 512 * cc + 512],
                                     start=(d == 0), stop=(d == 3)),
                                 reads=[b_WT[2], b_qt], writes=[bB], tag="hd_mm")
                        ri = (ht * 2 + cc) % 2
                        k.op("scalar", lambda s_, pA=pA, ri=ri: s_.activation(
                            sgg2[ri][:], pA[:], AF.Sigmoid),
                            reads=[bA, k.const_buf], writes=[b_sgg2[ri]])
                        k.op("vector", lambda v, pA=pA, ri=ri: v.tensor_tensor(
                            t1g2[ri][:], sgg2[ri][:], pA[:], ALU.mult),
                            reads=[b_sgg2[ri], bA], writes=[b_t1g2[ri]])
                        k.op("vector", lambda v, ht=ht, cc=cc, pB=pB, ri=ri:
                             v.tensor_tensor(
                                 GHT[:, 1024 * ht + 512 * cc:
                                     1024 * ht + 512 * cc + 512],
                                 t1g2[ri][:], pB[:], ALU.mult),
                             reads=[b_t1g2[ri], bB], writes=[b_GHT], tag="GHT")

                for ot in range(4):
                    for cc in range(2):
                        pC, bC = nb()
                        for ht in range(4):
                            k.op("tensor", lambda tt_, ht=ht, cc=cc, ot=ot, pC=pC:
                                 tt_.matmul(
                                     pC[:], wtile(WT[1], ht, ot),
                                     GHT[:, 1024 * ht + 512 * cc:
                                         1024 * ht + 512 * cc + 512],
                                     start=(ht == 0), stop=(ht == 3)),
                                 reads=[b_WT[1], b_GHT], writes=[bC], tag="out_mm")
                        k.op("vector", lambda v, ot=ot, cc=cc, pC=pC: v.tensor_copy(
                            outT[:, 1024 * ot + 512 * cc:1024 * ot + 512 * cc + 512],
                            pC[:]),
                            reads=[bC], writes=[b_outT], tag="ev_out")

                pM0, bM0 = nb()
                pM1, bM1 = nb()
                for ot in range(4):
                    sqi = ot % 2
                    k.op("scalar", lambda s_, ot=ot, sqi=sqi: s_.activation(
                        sqb1[sqi][:], outT[:, 1024 * ot:1024 * (ot + 1)], AF.Square),
                        reads=[b_outT, k.const_buf], writes=[b_sqb1[sqi]],
                        tag="ep_sq")
                    for cc in range(2):
                        pM, bM = (pM0, bM0) if cc == 0 else (pM1, bM1)
                        k.op("tensor", lambda tt_, ot=ot, cc=cc, pM=pM, sqi=sqi:
                             tt_.matmul(
                                 pM[0:1, :], ones_col[:],
                                 sqb1[sqi][:, 512 * cc:512 * cc + 512],
                                 start=(ot == 0), stop=(ot == 3)),
                             reads=[b_ones, b_sqb1[sqi]], writes=[bM], tag="ep_ms")
                k.op("vector", lambda v, pM0=pM0: v.tensor_copy(
                    msrow[:, 0:512], pM0[0:1, :]), reads=[bM0], writes=[b_msrow])
                k.op("vector", lambda v, pM1=pM1: v.tensor_copy(
                    msrow[:, 512:1024], pM1[0:1, :]), reads=[bM1], writes=[b_msrow])
                k.op("scalar", lambda s_: s_.activation(
                    msrow[:], msrow[:], AF.Ln, bias=k.constv(1e-5),
                    scale=k.constv(1.0 / 512.0)),
                    reads=[b_msrow, k.const_buf], writes=[b_msrow])
                k.op("scalar", lambda s_: s_.activation(
                    msrow[:], msrow[:], AF.Exp, scale=k.constv(-0.5)),
                    reads=[b_msrow, k.const_buf], writes=[b_msrow])
                k.op("vector", lambda v, h=h, c=c: v.tensor_copy(
                    srow_c[:], lrview[:, 16 + h, 8 * c:8 * c + 8]),
                    reads=[*b_lrv], writes=[b_srowc], tag="ep_sc")
                pT, bT = nb()
                k.op("tensor", lambda tt_, pT=pT: tt_.transpose(
                    pT[0:8, 0:128], srow_c[:], ident_f[:]),
                    reads=[b_srowc, b_identf], writes=[bT], tag="ep_tp")
                k.op("vector", lambda v, pT=pT: v.tensor_copy(
                    rowt2[:], pT[0:8, 0:128]), reads=[bT], writes=[b_rowt2])
                k.dma("sync", rowd2, rowt2[:], reads=[b_rowt2], writes=[b_rowt2],
                      tag="ep_r1")
                k.dma("sync", scrow[:], rowd2.rearrange("p n -> (p n)")[None, :],
                      reads=[b_rowt2], writes=[b_scrow], tag="ep_r2")
                k.op("vector", lambda v: v.tensor_tensor(
                    frow[:], msrow[:], scrow[:], ALU.mult),
                    reads=[b_msrow, b_scrow], writes=[b_frow])
                pB0, bB0 = nb()
                pB1, bB1 = nb()
                for cc in range(2):
                    pX, bX = (pB0, bB0) if cc == 0 else (pB1, bB1)
                    k.op("tensor", lambda tt_, cc=cc, pX=pX: tt_.matmul(
                        pX[:], ones_row_f[:],
                        frow[:, 512 * cc:512 * cc + 512], start=True, stop=True),
                        reads=[b_ones, b_frow], writes=[bX], tag="ep_bc")
                k.op("vector", lambda v, pB0=pB0: v.tensor_copy(
                    Btl[:, 0:512], pB0[:]), reads=[bB0], writes=[b_Btl])
                k.op("vector", lambda v, pB1=pB1: v.tensor_copy(
                    Btl[:, 512:1024], pB1[:]), reads=[bB1], writes=[b_Btl])
                for ot in range(4):
                    sg_i = ot % 2
                    k.op("vector", lambda v, ot=ot, sg_i=sg_i: v.scalar_tensor_tensor(
                        stg2[sg_i][:], outT[:, 1024 * ot:1024 * (ot + 1)],
                        onorm[:, ot:ot + 1], Btl[:], ALU.mult, ALU.mult),
                        reads=[b_outT, b_onorm, b_Btl], writes=[b_stg2[sg_i]],
                        tag="ep_apply")
                    k.dma("sync",
                          sc["ttt"][512 * h + 128 * ot:512 * h + 128 * (ot + 1), cs_],
                          stg2[sg_i][:], reads=[b_stg2[sg_i]],
                          writes=[scb["ttt"]], tag="ep_spill")

                for ct in range(8):
                    gt = c * 8 + ct
                    ri = ct % 2
                    p1_, b1_ = nb()
                    p2_, b2_ = nb()
                    p3_, b3_ = nb()
                    for d in range(4):
                        k.op("tensor", lambda tt_, d=d, ct=ct, p1_=p1_: tt_.matmul(
                            p1_[:], kt_h[:, 1024 * d + 128 * ct:
                                         1024 * d + 128 * ct + 128],
                            wblk(WT[0], d), start=(d == 0), stop=(d == 3)),
                            reads=[b_kt, b_WT[0]], writes=[b1_], tag="gba_mm")
                    for d in range(4):
                        k.op("tensor", lambda tt_, d=d, ct=ct, p2_=p2_: tt_.matmul(
                            p2_[:], kt_h[:, 1024 * d + 128 * ct:
                                         1024 * d + 128 * ct + 128],
                            wblk(WT[2], d), start=(d == 0), stop=(d == 3)),
                            reads=[b_kt, b_WT[2]], writes=[b2_], tag="hbm_mm")
                    for o in range(4):
                        k.op("tensor", lambda tt_, o=o, ct=ct, p3_=p3_: tt_.matmul(
                            p3_[:], vt_h[:, 1024 * o + 128 * ct:
                                         1024 * o + 128 * ct + 128],
                            wblk(W[1], o), start=(o == 0), stop=(o == 3)),
                            reads=[b_vt, b_W[1]], writes=[b3_], tag="dh_mm")
                    k.op("scalar", lambda s_, p1_=p1_, ri=ri: s_.activation(
                        sFA[ri][:], p1_[:], AF.Sigmoid),
                        reads=[b1_, k.const_buf], writes=[b_sFA[ri]])
                    k.op("vector", lambda v, p1_=p1_, ri=ri: v.tensor_copy(
                        gbaF[ri][:], p1_[:]), reads=[b1_], writes=[b_gbaF[ri]])
                    k.op("vector", lambda v, p2_=p2_, ri=ri: v.tensor_copy(
                        hbmF[ri][:], p2_[:]), reads=[b2_], writes=[b_hbmF[ri]])
                    k.op("vector", lambda v, p3_=p3_, ri=ri: v.tensor_copy(
                        dhFA[ri][:], p3_[:]), reads=[b3_], writes=[b_dhFA[ri]])
                    k.op("vector", lambda v, ri=ri: v.tensor_tensor(
                        sgF[:], sFA[ri][:], gbaF[ri][:], ALU.mult),
                        reads=[b_sFA[ri], b_gbaF[ri]], writes=[b_sgF])
                    k.op("gpsimd", lambda g, ct=ct, ri=ri: g.tensor_tensor(
                        dhbm[:, 512 * ct:512 * (ct + 1)], dhFA[ri][:], sgF[:],
                        ALU.mult),
                        reads=[b_dhFA[ri], b_sgF], writes=[b_dhbm], tag="dhbm")
                    k.op("vector", lambda v, ri=ri: v.tensor_tensor(
                        t1F[:], dhFA[ri][:], hbmF[ri][:], ALU.mult),
                        reads=[b_dhFA[ri], b_hbmF[ri]], writes=[b_t1F])
                    k.op("vector", lambda v, ri=ri: v.scalar_tensor_tensor(
                        v1F[:], gbaF[ri][:], 1.0, sgF[:], ALU.add, ALU.subtract),
                        reads=[b_gbaF[ri], b_sgF], writes=[b_v1F])
                    k.op("vector", lambda v, ri=ri: v.tensor_tensor(
                        t1F[:], t1F[:], sFA[ri][:], ALU.mult),
                        reads=[b_t1F, b_sFA[ri]], writes=[b_t1F])
                    k.op("vector", lambda v, ct=ct: v.tensor_tensor(
                        dgba[:, 512 * ct:512 * (ct + 1)], t1F[:], v1F[:], ALU.mult),
                        reads=[b_t1F, b_v1F], writes=[b_dgba], tag="dgba")
                    l0ap = lrview[:, 3 * h + 0, gt:gt + 1]
                    l1ap = lrview[:, 3 * h + 1, gt:gt + 1]
                    l2ap = lrview[:, 3 * h + 2, gt:gt + 1]
                    k.op("vector", lambda v, ct=ct, l1ap=l1ap, ri=ri:
                         v.scalar_tensor_tensor(
                             H1b[:, 512 * ct:512 * (ct + 1)], sgF[:], l1ap,
                             hbmF[ri][:], ALU.mult, ALU.mult),
                         reads=[b_sgF, b_hbmF[ri], *b_lrv], writes=[b_H1], tag="H1")
                    k.op("vector", lambda v, ct=ct, l0ap=l0ap: v.tensor_scalar_mul(
                        KL0[:, 512 * ct:512 * (ct + 1)],
                        k_h[:, 512 * ct:512 * (ct + 1)], l0ap),
                        reads=[b_kh, *b_lrv], writes=[b_KL0], tag="KL0")
                    k.op("vector", lambda v, ct=ct, l2ap=l2ap: v.tensor_scalar_mul(
                        KL2[:, 512 * ct:512 * (ct + 1)],
                        k_h[:, 512 * ct:512 * (ct + 1)], l2ap),
                        reads=[b_kh, *b_lrv], writes=[b_KL2], tag="KL2")

                if c == 1:
                    k.op("vector", lambda v, h=h, c=c: v.tensor_copy(
                        momb[:], lrview[:, 12 + h, 8 * c:8 * c + 8]),
                        reads=[*b_lrv], writes=[b_momb])
                    pm, bm_ = nb()
                    k.op("tensor", lambda tt_, pm=pm: tt_.matmul(
                        pm[0:1, 0:8], ones_col[:], momb[:], start=True, stop=True),
                        reads=[b_ones, b_momb], writes=[bm_], tag="mom_mm")
                    k.op("vector", lambda v, pm=pm: v.reduce_sum(
                        nst2[0:1, 6:7], pm[0:1, 0:8], axis=AX.X),
                        reads=[bm_], writes=[b_nst2])
                    k.op("vector", lambda v: v.tensor_scalar_mul(
                        nst2[0:1, 7:8], nst2[0:1, 6:7], 1.0 / CH),
                        reads=[b_nst2], writes=[b_nst2])
                    pm2, bm2 = nb()
                    k.op("tensor", lambda tt_, pm2=pm2: tt_.matmul(
                        pm2[:, 0:1], ones_row_f[:], nst2[0:1, 7:8],
                        start=True, stop=True),
                        reads=[b_ones, b_nst2], writes=[bm2], tag="mom_bc")
                    k.op("vector", lambda v, pm2=pm2: v.tensor_copy(
                        mmb[:], pm2[:, 0:1]), reads=[bm2], writes=[b_mmb])

                for s, (lh, bl, rh, br) in enumerate((
                        (dgba, b_dgba, KL0, b_KL0),
                        (v_h, b_vh, H1b, b_H1),
                        (dhbm, b_dhbm, KL2, b_KL2))):
                    for ht in range(4):
                        pD, bD = nb()
                        for ct in range(8):
                            k.op("tensor", lambda tt_, lh=lh, rh=rh, ht=ht, ct=ct,
                                 pD=pD: tt_.matmul(
                                     pD[:], lh[:, 512 * ct + 128 * ht:
                                               512 * ct + 128 * ht + 128],
                                     rh[:, 512 * ct:512 * (ct + 1)],
                                     start=(ct == 0), stop=(ct == 7)),
                                 reads=[bl, br], writes=[bD], tag=f"dw{s}_mm")
                        if c == 0:
                            k.op("vector", lambda v, s=s, ht=ht, pD=pD: v.tensor_copy(
                                wblk(Mst[s], ht), pD[:]),
                                reads=[bD], writes=[b_M[s]], tag=f"dw{s}_ev")
                        else:
                            k.op("vector", lambda v, s=s, ht=ht, pD=pD:
                                 v.scalar_tensor_tensor(
                                     wblk(Mst[s], ht), wblk(Mst[s], ht), mmb[:],
                                     pD[:], ALU.mult, ALU.add),
                                 reads=[bD, b_mmb, b_M[s]], writes=[b_M[s]],
                                 tag=f"dw{s}_ev")

                for s in range(3):
                    G, bG = Mst[s], b_M[s]
                    k.op("vector", lambda v, G=G: v.scalar_tensor_tensor(
                        GHT[:, 0:2048], G[:], 1.0, G[:], ALU.mult, ALU.mult,
                        accum_out=nst2[:, 0:1]),
                        reads=[bG], writes=[b_GHT, b_nst2], tag="ns_frob")
                    pF, bF = nb()
                    k.op("tensor", lambda tt_, pF=pF: tt_.matmul(
                        pF[0:1, 0:1], ones_col_f[:], nst2[:, 0:1],
                        start=True, stop=True),
                        reads=[b_ones, b_nst2], writes=[bF], tag="ns_fsum")
                    k.op("vector", lambda v, pF=pF: v.tensor_copy(
                        nst2[0:1, 1:2], pF[0:1, 0:1]), reads=[bF], writes=[b_nst2])
                    k.op("scalar", lambda s_: s_.activation(
                        nst2[0:1, 2:3], nst2[0:1, 1:2], AF.Ln),
                        reads=[b_nst2, k.const_buf], writes=[b_nst2])
                    k.op("scalar", lambda s_: s_.activation(
                        nst2[0:1, 3:4], nst2[0:1, 2:3], AF.Exp, scale=k.constv(0.5)),
                        reads=[b_nst2, k.const_buf], writes=[b_nst2])
                    k.op("vector", lambda v: v.tensor_scalar_add(
                        nst2[0:1, 4:5], nst2[0:1, 3:4], 1e-7),
                        reads=[b_nst2], writes=[b_nst2])
                    k.op("vector", lambda v: v.reciprocal(
                        nst2[0:1, 5:6], nst2[0:1, 4:5]),
                        reads=[b_nst2], writes=[b_nst2])
                    pF2, bF2 = nb()
                    k.op("tensor", lambda tt_, pF2=pF2: tt_.matmul(
                        pF2[:, 0:1], ones_row_f[:], nst2[0:1, 5:6],
                        start=True, stop=True),
                        reads=[b_ones, b_nst2], writes=[bF2], tag="ns_fbc")
                    k.op("vector", lambda v, pF2=pF2: v.tensor_copy(
                        nst2[:, 6:7], pF2[:, 0:1]), reads=[bF2], writes=[b_nst2])
                    xcur, bx = Xb[0], b_Xb[0]
                    k.op("vector", lambda v, G=G, xcur=xcur: v.tensor_scalar_mul(
                        xcur[:], G[:], nst2[:, 6:7]),
                        reads=[bG, b_nst2], writes=[bx], tag="ns_x0")
                    for it, (a_, b_, c_) in enumerate(NS_COEFFS):
                        xn, bxn = Xb[(it + 1) % 2], b_Xb[(it + 1) % 2]
                        tp16_pack(xcur, bx, XTb, b_XTb, f"XT{s}{it}")
                        abanks = [nb(), nb()]
                        ybanks = [nb() for _ in range(4)]

                        def emit_A(j):
                            pA2, bA2 = abanks[j % 2]
                            for kk in range(4):
                                k.op("tensor", lambda tt_, j=j, kk=kk, pA2=pA2:
                                     tt_.matmul(
                                         pA2[:], wtile(XTb, kk, j), wblk(XTb, kk),
                                         start=(kk == 0), stop=(kk == 3)),
                                     reads=[b_XTb], writes=[bA2], tag="ns_A")

                        def evac_A(j):
                            pA2, bA2 = abanks[j % 2]
                            k.op("scalar", lambda s_, j=j, pA2=pA2: s_.copy(
                                wblk(Ab, j), pA2[:]),
                                reads=[bA2, k.const_buf], writes=[b_Ab],
                                tag="ns_evA")

                        def emit_Y(j):
                            for i in range(4):
                                pY, bY = ybanks[i]
                                k.op("tensor", lambda tt_, i=i, j=j, pY=pY,
                                     xcur=xcur: tt_.matmul(
                                         pY[:], wtile(Ab, j, i), wblk(xcur, j),
                                         start=(j == 0), stop=(j == 3)),
                                     reads=[b_Ab, bx], writes=[bY], tag="ns_Y")

                        emit_A(0)
                        emit_A(1)
                        evac_A(0)
                        emit_A(2)
                        emit_Y(0)
                        evac_A(1)
                        emit_A(3)
                        emit_Y(1)
                        evac_A(2)
                        emit_Y(2)
                        evac_A(3)
                        emit_Y(3)
                        for i in range(4):
                            pY, bY = ybanks[i]
                            k.op("scalar", lambda s_, i=i, pY=pY, b_=b_: s_.mul(
                                wblk(Yb, i), pY[:], k.constv(b_)),
                                reads=[bY, k.const_buf], writes=[b_Yb],
                                tag="ns_yh")
                        for i in range(4):
                            k.op("vector", lambda v, i=i, a_=a_, xcur=xcur:
                                 v.scalar_tensor_tensor(
                                     t_scr[:, 512 * i:512 * (i + 1)],
                                     wblk(xcur, i), a_, wblk(Yb, i),
                                     ALU.mult, ALU.add),
                                 reads=[bx, b_Yb], writes=[b_tscr], tag="ns_t")
                        for j in range(4):
                            for i in range(4):
                                pY, bY = ybanks[i]
                                k.op("tensor", lambda tt_, i=i, j=j, pY=pY:
                                     tt_.matmul(
                                         pY[:], wtile(Ab, j, i), wblk(Yb, j),
                                         start=(j == 0), stop=(j == 3)),
                                     reads=[b_Ab, b_Yb], writes=[bY], tag="ns_Z")
                        for i in range(4):
                            pY, bY = ybanks[i]
                            k.op("vector", lambda v, i=i, pY=pY, cb=c_ / b_, xn=xn:
                                 v.scalar_tensor_tensor(
                                     wblk(xn, i), t_scr[:, 512 * i:512 * (i + 1)],
                                     1.0, pY[:], ALU.mult, ALU.add,
                                     ) if False else v.scalar_tensor_tensor(
                                     wblk(xn, i), pY[:], cb,
                                     t_scr[:, 512 * i:512 * (i + 1)],
                                     ALU.mult, ALU.add),
                                 reads=[bY, b_tscr], writes=[bxn], tag="ns_xn")
                        xcur, bx = xn, bxn
                    for r in range(4):
                        k.op("vector", lambda v, s=s, r=r, xcur=xcur:
                             v.scalar_tensor_tensor(
                                 wnscr, wblk(W[s], r), 1.0, wblk(xcur, r),
                                 ALU.mult, ALU.add),
                             reads=[b_W[s], bx], writes=[b_wnscr], tag="rn_add")
                        k.op("vector", lambda v, r=r: v.scalar_tensor_tensor(
                            sq2, wnscr, 1.0, wnscr, ALU.mult, ALU.mult,
                            accum_out=rnst[:, r:r + 1]),
                            reads=[b_wnscr], writes=[b_sq2, b_rnst], tag="rn_sq")
                        k.op("scalar", lambda s_, r=r: s_.activation(
                            rnst[:, 4 + r:5 + r], rnst[:, r:r + 1], AF.Ln),
                            reads=[b_rnst, k.const_buf], writes=[b_rnst])
                        k.op("scalar", lambda s_, r=r: s_.activation(
                            rnst[:, 4 + r:5 + r], rnst[:, 4 + r:5 + r], AF.Exp,
                            scale=k.constv(0.5)),
                            reads=[b_rnst, k.const_buf], writes=[b_rnst])
                        k.op("vector", lambda v, r=r: v.tensor_scalar_add(
                            rnst[:, 4 + r:5 + r], rnst[:, 4 + r:5 + r], 1e-5),
                            reads=[b_rnst], writes=[b_rnst])
                        k.op("vector", lambda v, r=r: v.reciprocal(
                            rnst[:, 4 + r:5 + r], rnst[:, 4 + r:5 + r]),
                            reads=[b_rnst], writes=[b_rnst])
                        k.op("vector", lambda v, s=s, r=r: v.tensor_tensor(
                            rnst[:, 4 + r:5 + r], rnst[:, 4 + r:5 + r],
                            nbuf[:, 4 * s + r:4 * s + r + 1], ALU.mult),
                            reads=[b_rnst, b_nbuf], writes=[b_rnst])
                        k.op("vector", lambda v, s=s, r=r: v.tensor_scalar_mul(
                            wblk(W[s], r), wnscr, rnst[:, 4 + r:5 + r]),
                            reads=[b_wnscr, b_rnst], writes=[b_W[s]], tag="rn_app")

    k.barrier()

    if 'p3' in stages:
     with ExitStack() as es34:
        sum_fm = k.sb("sum_fm", [128, 16 * 2048], BF16, es=es34)
        b_sum = Buf("sum_fm")
        with ExitStack() as es3:
            ps_sA = [k.ps(f"ps_s{i}", [128, 1536], F32, es=es3) for i in range(2)]
            b_pssA = [Buf(f"pss{i}", psum=True) for i in range(2)]
            ps_et = k.ps("ps_et", [128, 2048], BF16, es=es3); b_pset = Buf("pset", psum=True)
            ps_pv = ps_sA[0][:, 0:1024]; b_pspv = b_pssA[0]
            ps_x = ps_sA[1][:, 0:512]; b_psx = b_pssA[1]

            aq_hA = [k.sb(f"aq_h{i}", [128, 1024], BF16, es=es3) for i in range(2)]
            b_aqhA = [Buf(f"aqh{i}") for i in range(2)]
            ak_hA = [k.sb(f"ak_h{i}", [128, 2048], BF16, es=es3) for i in range(2)]
            b_akhA = [Buf(f"akh{i}") for i in range(2)]
            av_hA = [k.sb(f"av_h{i}", [128, 2048], BF16, es=es3) for i in range(2)]
            b_avhA = [Buf(f"avh{i}") for i in range(2)]
            ttt_hA = [k.sb(f"ttt_h{i}", [128, 1024], BF16, es=es3) for i in range(2)]
            b_tthA = [Buf(f"ttth{i}") for i in range(2)]
            E2 = [k.sb(f"E2_{i}", [128, 1152], BF16, es=es3) for i in range(2)]
            b_E2 = [Buf(f"E2_{i}") for i in range(2)]
            ETb = k.sb("ETb", [128, 16384], BF16, es=es3); b_ET = Buf("ET")
            linv8 = k.sb("linv8", [128, 8], F32, es=es3); b_linv = Buf("linv")
            negm = k.sb("negm", [128, 2], F32, es=es3); b_negm = Buf("negm")
            rowt3 = k.sb("rowt3", [8, 128], F32, es=es3); b_rowt3 = Buf("rowt3")
            lrow = k.sb("lrow", [1, 1024], F32, es=es3); b_lrow = Buf("lrow")
            Bsb = k.sb("Bsb", [128, 1024], F32, es=es3); b_Bsb = Buf("Bsb")
            tF3 = k.sb("tF3", [128, 1024], F32, es=es3); b_tF3 = Buf("tF3")
            Mlo = k.sb("Mlo", [128, 128], F32, es=es3); b_Mlo = Buf("Mlo")
            Mhi = k.sb("Mhi", [128, 128], F32, es=es3); b_Mhi = Buf("Mhi")
            rowd3 = nc.dram_tensor("rowd3", [8, 128], F32, kind="Internal").ap()

            k.op("gpsimd", lambda g: g.memset(Mlo[:], 0.0), writes=[b_Mlo])
            k.op("gpsimd", lambda g: g.affine_select(
                Mlo[:], Mlo[:], [[1, 128]], ALU.is_ge, -30000.0,
                base=0, channel_multiplier=-1), reads=[b_Mlo], writes=[b_Mlo])
            k.op("gpsimd", lambda g: g.memset(Mhi[:], 0.0), writes=[b_Mhi])
            k.op("gpsimd", lambda g: g.affine_select(
                Mhi[:], Mhi[:], [[-1, 128]], ALU.is_ge, -30000.0,
                base=0, channel_multiplier=1), reads=[b_Mhi], writes=[b_Mhi])
            k.op("vector", lambda v: v.memset(ETb[:], 0.0), writes=[b_ET])

            ETv = ETb[:].rearrange("p (kt q) -> p kt q", kt=16)

            for blk in range(2):
                if blk == 1:
                    for kt in range(8):
                        for qt in range(kt + 1, 8):
                            k.op("vector", lambda v, kt=kt, qt=qt: v.memset(
                                ETv[:, kt, 128 * qt:128 * (qt + 1)], 0.0),
                                writes=[b_ET], tag="et_clr")
                nk_blk = 8 if blk == 0 else 16
                for h in range(NAH):
                    hi_ = (blk * NAH + h) % 2
                    aq_h, b_aqh = aq_hA[hi_], b_aqhA[hi_]
                    ak_h, b_akh = ak_hA[hi_], b_akhA[hi_]
                    av_h, b_avh = av_hA[hi_], b_avhA[hi_]
                    ttt_h, b_ttth = ttt_hA[hi_], b_tthA[hi_]
                    hr = slice(128 * h, 128 * (h + 1))
                    k.dma("sync", aq_h[:], sc["aq"][hr, 1024 * blk:1024 * (blk + 1)],
                          reads=[scb["aq"]], writes=[b_aqh], tag=f"a_q{blk}{h}")
                    k.dma("sync", ak_h[:, 0:1024 * (blk + 1)],
                          sc["ak"][hr, 0:1024 * (blk + 1)],
                          reads=[scb["ak"]], writes=[b_akh], tag=f"a_k{blk}{h}")
                    for kt in range(nk_blk):
                        k.dma("sync", av_h[:, 128 * kt:128 * (kt + 1)],
                              sc["av"][128 * kt:128 * (kt + 1), 128 * h:128 * (h + 1)],
                              reads=[scb["av"]], writes=[b_avh], tag=f"a_v{blk}{h}")
                    k.dma("scalar", ttt_h[:], sc["ttt"][hr, 1024 * blk:1024 * (blk + 1)],
                          reads=[scb["ttt"]], writes=[b_ttth], tag=f"a_t{blk}{h}")

                    for qt in range(8):
                        ps_s = ps_sA[qt % 2]
                        b_pss = b_pssA[qt % 2]
                        w0 = 128 * qt if blk == 1 else 0
                        w = 1152 if blk == 1 else 128 * (qt + 1)
                        nk = w // 128
                        ei = qt % 2
                        off = 0
                        first = True
                        while off < w:
                            n = min(512, w - off)
                            k.op("tensor", lambda tt_, off=off, n=n, qt=qt, w0=w0,
                                 ps_s=ps_s, aq_h=aq_h, ak_h=ak_h,
                                 first=first: tt_.matmul(
                                     ps_s[:, off:off + n],
                                     aq_h[:, 128 * qt:128 * (qt + 1)],
                                     ak_h[:, w0 + off:w0 + off + n],
                                     start=True, stop=True),
                                 reads=[b_aqh, b_akh], writes=[b_pss], tag="s_mm")
                            off += n
                            first = False
                        if blk == 1:
                            k.op("vector", lambda v, ps_s=ps_s: v.tensor_tensor(
                                ps_s[:, 0:128], ps_s[:, 0:128], Mlo[:], ALU.add),
                                reads=[b_Mlo, b_pss], writes=[b_pss], tag="m_lo")
                            k.op("vector", lambda v, ps_s=ps_s: v.tensor_tensor(
                                ps_s[:, 1024:1152], ps_s[:, 1024:1152], Mhi[:],
                                ALU.add),
                                reads=[b_Mhi, b_pss], writes=[b_pss], tag="m_hi")
                        else:
                            k.op("vector", lambda v, w=w, ps_s=ps_s: v.tensor_tensor(
                                ps_s[:, w - 128:w], ps_s[:, w - 128:w], Mhi[:],
                                ALU.add),
                                reads=[b_Mhi, b_pss], writes=[b_pss], tag="m_hi0")
                        k.op("vector", lambda v, w=w, ps_s=ps_s, ei=ei: v.reduce_max(
                            negm[:, ei:ei + 1], ps_s[:, 0:w], axis=AX.X, negate=True),
                            reads=[b_pss], writes=[b_negm])
                        k.op("scalar", lambda s_, w=w, ei=ei, qt=qt, ps_s=ps_s:
                             s_.activation(
                            E2[ei][:, 0:w], ps_s[:, 0:w], AF.Exp, bias=negm[:, ei:ei + 1],
                            accum_out=linv8[:, qt:qt + 1]),
                            reads=[b_pss, b_negm, k.const_buf],
                            writes=[b_E2[ei], b_linv], tag="exp")
                        k.op("vector", lambda v, qt=qt: v.reciprocal(
                            linv8[:, qt:qt + 1], linv8[:, qt:qt + 1]),
                            reads=[b_linv], writes=[b_linv])
                        for j in range(nk):
                            k.op("tensor", lambda tt_, j=j, ei=ei: tt_.transpose(
                                ps_et[:, 128 * j:128 * (j + 1)],
                                E2[ei][:, 128 * j:128 * (j + 1)], ident[:]),
                                reads=[b_E2[ei], b_ident], writes=[b_pset],
                                tag="e_tp")
                        ka0 = w0 // 128
                        dstv = ETv[:, ka0:ka0 + nk, 128 * qt:128 * (qt + 1)]
                        srcv = ps_et[:, 0:128 * nk].rearrange(
                            "p (n c) -> p n c", n=nk)
                        k.op("vector", lambda v, dstv=dstv, srcv=srcv: v.tensor_copy(
                            dstv, srcv),
                            reads=[b_pset], writes=[b_ET], tag="e_ev")

                    for kt in range(nk_blk):
                        for qg in range(2):
                            k.op("tensor", lambda tt_, kt=kt, qg=qg, nk_blk=nk_blk,
                                 av_h=av_h: tt_.matmul(
                                ps_pv[:, 512 * qg:512 * (qg + 1)],
                                av_h[:, 128 * kt:128 * (kt + 1)],
                                ETv[:, kt, 512 * qg:512 * (qg + 1)],
                                start=(kt == 0), stop=(kt == nk_blk - 1)),
                                reads=[b_avh, b_ET], writes=[b_pspv], tag="pv_mm")
                    k.op("tensor", lambda tt_: tt_.transpose(
                        ps_x[0:8, 0:128], linv8[:], ident_f[:]),
                        reads=[b_linv, b_identf], writes=[b_psx], tag="l_tp")
                    k.op("vector", lambda v: v.tensor_copy(rowt3[:], ps_x[0:8, 0:128]),
                         reads=[b_psx], writes=[b_rowt3])
                    k.dma("sync", rowd3, rowt3[:], reads=[b_rowt3], writes=[b_rowt3],
                          tag="l_r1")
                    k.dma("sync", lrow[:], rowd3.rearrange("p n -> (p n)")[None, :],
                          reads=[b_rowt3], writes=[b_lrow], tag="l_r2")
                    for qg in range(2):
                        k.op("tensor", lambda tt_, qg=qg: tt_.matmul(
                            ps_sA[1][:, 512 * qg:512 * (qg + 1)], ones_row_f[:],
                            lrow[:, 512 * qg:512 * (qg + 1)], start=True, stop=True),
                            reads=[b_ones, b_lrow], writes=[b_pssA[1]], tag="l_bc")
                    k.op("vector", lambda v: v.tensor_copy(Bsb[:], ps_sA[1][:, 0:1024]),
                         reads=[b_pssA[1]], writes=[b_Bsb])
                    for qg in range(2):
                        k.op("vector", lambda v, qg=qg: v.tensor_tensor(
                            tF3[:, 512 * qg:512 * (qg + 1)],
                            ps_pv[:, 512 * qg:512 * (qg + 1)],
                            Bsb[:, 512 * qg:512 * (qg + 1)], ALU.mult),
                            reads=[b_pspv, b_Bsb], writes=[b_tF3], tag="pv_ev")
                    k.op("vector", lambda v, h=h, blk=blk, ttt_h=ttt_h: v.tensor_tensor(
                        sum_fm[:, 2048 * h + 1024 * blk:2048 * h + 1024 * (blk + 1)],
                        tF3[:], ttt_h[:], ALU.add),
                        reads=[b_tF3, b_ttth], writes=[b_sum], tag="sum_w")
        k.barrier()

        with ExitStack() as es4:
            ps4 = [k.ps(f"ps4_{i}", [128, 2048], F32, es=es4) for i in range(2)]
            b_ps4 = [Buf(f"ps4_{i}", psum=True) for i in range(2)]
            Wo_sb = k.sb("Wo_sb", [128, 16 * 2048], BF16, es=es4)
            b_Wo = Buf("Wo_sb")
            xb4 = [k.sb(f"xb4_{i}", [128, 2048], F32, es=es4) for i in range(2)]
            b_xb4 = [Buf(f"xb4_{i}") for i in range(2)]
            aos = k.sb("aos", [128, 2048], F32, es=es4); b_aos = Buf("aos")
            x2s = k.sb("x2s", [128, 2048], F32, es=es4); b_x2s = Buf("x2s")
            xfs = k.sb("xfs", [128, 2048], F32, es=es4); b_xfs = Buf("xfs")
            xfb = k.sb("xfb", [128, 2048], BF16, es=es4); b_xfb = Buf("xfb")
            sqv4 = k.sb("sqv4", [128, 2048], F32, es=es4); b_sqv4 = Buf("sqv4")
            st4 = k.sb("st4", [128, 8], F32, es=es4); b_st4 = Buf("st4")

            for d in range(DT):
                k.dma("gpsimd", Wo_sb[:, 2048 * d:2048 * (d + 1)],
                      I["Wo"][128 * d:128 * (d + 1), :], writes=[b_Wo],
                      tag=f"wold{d}")

            for t in [15] + list(range(15)):
                pi = t % 2
                bp4 = b_ps4[pi]
                for d in range(DT):
                    for oc in range(4):
                        k.op("tensor", lambda tt_, d=d, oc=oc, t=t, pi=pi: tt_.matmul(
                            ps4[pi][:, 512 * oc:512 * (oc + 1)],
                            sum_fm[:, 2048 * d + 128 * t:2048 * d + 128 * (t + 1)],
                            Wo_sb[:, 2048 * d + 512 * oc:2048 * d + 512 * (oc + 1)],
                            start=(d == 0), stop=(d == DT - 1)),
                            reads=[b_sum, b_Wo], writes=[bp4], tag="wo_mm")
                xi = t % 2
                k.dma("sync", xb4[xi][:], I["x"][128 * t:128 * (t + 1), :],
                      writes=[b_xb4[xi]], tag=f"x4_{t}")
                k.op("vector", lambda v, pi=pi: v.reduce_sum(
                    st4[:, 0:1], ps4[pi][:], axis=AX.X),
                    reads=[bp4], writes=[b_st4])
                k.op("scalar", lambda s_, pi=pi: s_.activation(
                    sqv4[:], ps4[pi][:], AF.Square, accum_out=st4[:, 1:2]),
                    reads=[bp4, k.const_buf], writes=[b_sqv4, b_st4])
                k.op("vector", lambda v: v.tensor_scalar_mul(
                    st4[:, 2:4], st4[:, 0:2], 1.0 / D),
                    reads=[b_st4], writes=[b_st4])
                k.op("vector", lambda v: v.tensor_tensor(
                    st4[:, 4:5], st4[:, 2:3], st4[:, 2:3], ALU.mult),
                    reads=[b_st4], writes=[b_st4])
                k.op("vector", lambda v: v.tensor_sub(
                    st4[:, 5:6], st4[:, 3:4], st4[:, 4:5]),
                    reads=[b_st4], writes=[b_st4])
                k.op("scalar", lambda s_: s_.activation(
                    st4[:, 6:7], st4[:, 5:6], AF.Ln, bias=k.constv(1e-5)),
                    reads=[b_st4, k.const_buf], writes=[b_st4])
                k.op("scalar", lambda s_: s_.activation(
                    st4[:, 7:8], st4[:, 6:7], AF.Exp, scale=k.constv(-0.5)),
                    reads=[b_st4, k.const_buf], writes=[b_st4])
                k.op("vector", lambda v, pi=pi: v.tensor_scalar(
                    aos[:], ps4[pi][:], st4[:, 2:3], st4[:, 7:8],
                    ALU.subtract, ALU.mult),
                    reads=[bp4, b_st4], writes=[b_aos], tag="wo_ln")
                k.op("vector", lambda v, xi=xi: v.tensor_add(
                    x2s[:], aos[:], xb4[xi][:]),
                    reads=[b_aos, b_xb4[xi]], writes=[b_x2s], tag="x2")
                k.dma("scalar", sc["x2"][128 * t:128 * (t + 1), :], x2s[:],
                      reads=[b_x2s], writes=[scb["x2"]], tag=f"sp_x2_{t}")
                k.op("vector", lambda v: v.reduce_sum(
                    st4[:, 0:1], x2s[:], axis=AX.X),
                    reads=[b_x2s], writes=[b_st4])
                k.op("scalar", lambda s_: s_.activation(
                    sqv4[:], x2s[:], AF.Square, accum_out=st4[:, 1:2]),
                    reads=[b_x2s, k.const_buf], writes=[b_sqv4, b_st4])
                k.op("vector", lambda v: v.tensor_scalar_mul(
                    st4[:, 2:4], st4[:, 0:2], 1.0 / D),
                    reads=[b_st4], writes=[b_st4])
                k.op("vector", lambda v: v.tensor_tensor(
                    st4[:, 4:5], st4[:, 2:3], st4[:, 2:3], ALU.mult),
                    reads=[b_st4], writes=[b_st4])
                k.op("vector", lambda v: v.tensor_sub(
                    st4[:, 5:6], st4[:, 3:4], st4[:, 4:5]),
                    reads=[b_st4], writes=[b_st4])
                k.op("scalar", lambda s_: s_.activation(
                    st4[:, 6:7], st4[:, 5:6], AF.Ln, bias=k.constv(1e-5)),
                    reads=[b_st4, k.const_buf], writes=[b_st4])
                k.op("scalar", lambda s_: s_.activation(
                    st4[:, 7:8], st4[:, 6:7], AF.Exp, scale=k.constv(-0.5)),
                    reads=[b_st4, k.const_buf], writes=[b_st4])
                k.op("vector", lambda v: v.tensor_scalar(
                    xfs[:], x2s[:], st4[:, 2:3], st4[:, 7:8],
                    ALU.subtract, ALU.mult),
                    reads=[b_x2s, b_st4], writes=[b_xfs], tag="xf")
                k.dma("scalar", sc["xf"][128 * t:128 * (t + 1), :], xfs[:],
                      reads=[b_xfs], writes=[scb["xf"]], tag=f"sp_xf_{t}")
                k.op("gpsimd", lambda g: g.tensor_copy(xfb[:], xfs[:]),
                     reads=[b_xfs], writes=[b_xfb], tag="xfb")
                k.dma("gpsimd", sc["xfT"][128 * t:128 * (t + 1), :], xfb[:],
                      reads=[b_xfb], writes=[scb["xfT"]], tag=f"sp_xfb_{t}")
        k.barrier()

    if 'p5' in stages:
     with ExitStack() as es5o:
        Bt5 = k.sb("Bt5", [128, 2048], F32, es=es5o); b_Bt5 = Buf("Bt5")
        Gam6 = k.sb("Gam6", [128, 2048], F32, es=es5o); b_Gam6 = Buf("Gam6")
        Bet6 = k.sb("Bet6", [128, 2048], F32, es=es5o); b_Bet6 = Buf("Bet6")

        with ExitStack() as es5a:
            ps5 = k.ps("ps5", [8, 2048], F32, es=es5a); b_ps5 = Buf("ps5", psum=True)
            ps5t = k.ps("ps5t", [128, 2048], BF16, es=es5a)
            b_ps5t = Buf("ps5t", psum=True)
            psg = k.ps("psg", [128, 1024], F32, es=es5a)
            b_psg = Buf("psg", psum=True)
            gatew = k.sb("gatew", [128, 16 * 2048], BF16, es=es5a)
            b_gw = Buf("gatew")
            d1b2 = [k.sb(f"d1b2_{i}", [128, 1024], BF16, es=es5a) for i in range(2)]
            b_d1b2 = [Buf(f"d1b2_{i}") for i in range(2)]
            d2b2 = [k.sb(f"d2b2_{i}", [128, 2048], BF16, es=es5a) for i in range(2)]
            b_d2b2 = [Buf(f"d2b2_{i}") for i in range(2)]
            aggb_blk = [k.sb(f"aggblk{i}", [128, 512], BF16, es=es5a)
                        for i in range(2)]
            b_aggblk = [Buf(f"aggblk{i}") for i in range(2)]
            rows5 = k.sb("rows5", [1, 2048], F32, es=es5a); b_rows5 = Buf("rows5")
            rows5b = k.sb("rows5b", [1, 2048], F32, es=es5a); b_rows5b = Buf("rows5b")
            st5 = k.sb("st5", [8, 8], F32, es=es5a); b_st5 = Buf("st5")
            throw = k.sb("throw", [8, 2048], F32, es=es5a); b_throw = Buf("throw")
            trow = k.sb("trow", [8, 2048], F32, es=es5a); b_trow = Buf("trow")
            trowb = k.sb("trowb", [8, 2048], BF16, es=es5a); b_trowb = Buf("trowb")
            tT5 = k.sb("tT5", [128, 128], BF16, es=es5a); b_tT5 = Buf("tT5")
            hrow = k.sb("hrow", [8, 1024], F32, es=es5a); b_hrow = Buf("hrow")
            hrowb = k.sb("hrowb", [8, 1024], BF16, es=es5a); b_hrowb = Buf("hrowb")
            hT5 = k.sb("hT5", [128, 64], BF16, es=es5a); b_hT5 = Buf("hT5")
            rsum = k.sb("rsum", [8, 2048], F32, es=es5a); b_rsum = Buf("rsum")
            Gpn = k.sb("Gpn", [8, 2048], BF16, es=es5a); b_Gpn = Buf("Gpn")
            Bpn = k.sb("Bpn", [8, 2048], BF16, es=es5a); b_Bpn = Buf("Bpn")
            D2b = k.sb("D2b", [8, 2048], BF16, es=es5a); b_D2b = Buf("D2b")
            oh5 = k.sb("oh5", [8, 1], F32, es=es5a); b_oh5 = Buf("oh5")
            sel8 = k.sb("sel8", [8, 128], F32, es=es5a); b_sel8 = Buf("sel8")
            brow = k.sb("brow", [1, 5120], BF16, es=es5a); b_brow = Buf("brow")
            grow = k.sb("grow", [1, 2048], BF16, es=es5a); b_grow = Buf("grow")
            TB8 = rsum; b_TB8 = b_rsum
            xl5 = trow[0:1, :]; b_xl5 = b_trow
            xn5 = rsum[0:1, :]; b_xn5 = b_rsum
            xg8 = throw; b_xg8 = b_throw
            xg8b = trowb; b_xg8b = b_trowb
            xftm0 = k.sb("xftm", [128, 2048], BF16, es=es5a)
            xftm = [xftm0, xftm0]
            bxf0 = Buf("xftm")
            b_xftm = [bxf0, bxf0]
            stg5 = k.sb("stg5", [128, 2048], BF16, es=es5a); b_stg5 = Buf("stg5")
            gst = [k.sb(f"gst{i}", [128, 1024], BF16, es=es5a) for i in range(2)]
            b_gst = [Buf(f"gst{i}") for i in range(2)]
            gsx = [k.sb(f"gsx{i}", [128, 1024], F32, es=es5a) for i in range(2)]
            b_gsx = [Buf(f"gsx{i}") for i in range(2)]
            cg_in = nc.dram_tensor("cg_in", [1, 2048], F32, kind="Internal").ap()
            cg_out = nc.dram_tensor("cg_out", [8, 2048], F32, kind="Internal").ap()
            cc_in = nc.dram_tensor("cc_in", [8, 2048], F32, kind="Internal").ap()
            cc_out = nc.dram_tensor("cc_out", [8, 2048], F32, kind="Internal").ap()
            b_cg = Buf("cg"); b_cc = Buf("cc")

            segs = []

            def seg_loads():
                k.dma("gpsimd", brow[:, 0:2048], I["agg_b"][None, :], writes=[b_brow])
                k.dma("gpsimd", brow[:, 2048:3072], I["d1b_sl"][None, :],
                      writes=[b_brow])
                k.dma("gpsimd", brow[:, 3072:5120], I["bc_b"][None, :],
                      writes=[b_brow])
                k.dma("sync", oh5[:], I["onehot"], writes=[b_oh5])
                k.dma("sync", rows5[:], I["pn_gamma"][None, :], writes=[b_rows5])
                k.dma("sync", rows5b[:], I["pn_beta"][None, :], writes=[b_rows5b])

            def bcast8(src_row, b_src, dst, b_dst, tag):
                for i4 in range(4):
                    k.op("tensor", lambda tt_, i4=i4: tt_.matmul(
                        ps5[0:8, 512 * i4:512 * (i4 + 1)], ones_row_f[0:1, 0:8],
                        src_row[:, 512 * i4:512 * (i4 + 1)], start=True, stop=True),
                        reads=[b_ones, b_src], writes=[b_ps5], tag=f"bc8_{tag}")
                k.op("vector", lambda v: v.tensor_copy(dst[:], ps5[0:8, :]),
                     reads=[b_ps5], writes=[b_dst], tag=f"bc8e_{tag}")

            def seg_pre():
                bcast8(rows5, b_rows5, Gpn, b_Gpn, "gpn")
                bcast8(rows5b, b_rows5b, Bpn, b_Bpn, "bpn")
                k.dma("sync", rows5[:], I["d2_b"][None, :], reads=[b_Gpn],
                      writes=[b_rows5], tag="d2brow")
                bcast8(rows5, b_rows5, D2b, b_D2b, "d2b")
                k.op("vector", lambda v: v.memset(sel8[:], 1.0), writes=[b_sel8])
                k.op("vector", lambda v: v.tensor_scalar_mul(
                    sel8[:], sel8[:], oh5[:]), reads=[b_oh5, b_sel8],
                    writes=[b_sel8])
                for nm, dst, b_dst in (("out_gamma", Gam6, b_Gam6),
                                       ("out_beta", Bet6, b_Bet6)):
                    k.dma("sync", rows5b[:], I[nm][None, :],
                          reads=[b_Bpn, b_Bet6], writes=[b_rows5b], tag=nm)
                    for i4 in range(4):
                        pp_, bp_ = (psg[:, 0:512], b_psg) if i4 % 2 == 0 else (
                            psg[:, 512:1024], b_psg)
                        k.op("tensor", lambda tt_, i4=i4, pp_=pp_: tt_.matmul(
                            pp_, ones_row_f[:],
                            rows5b[:, 512 * i4:512 * (i4 + 1)],
                            start=True, stop=True),
                            reads=[b_ones, b_rows5b], writes=[bp_],
                            tag=f"bcg_{nm}")
                        k.op("vector", lambda v, i4=i4, dst=dst, pp_=pp_:
                             v.tensor_copy(dst[:, 512 * i4:512 * (i4 + 1)], pp_),
                             reads=[bp_], writes=[b_dst], tag=f"bcge_{nm}")

            def seg_xn():
                k.dma("sync", xl5[:], sc["xf"][L - 1:L, :], reads=[scb["xf"]],
                      writes=[b_xl5], tag="xl5")
                k.op("vector", lambda v: v.reduce_sum(
                    st5[0:1, 0:1], xl5[:], axis=AX.X), reads=[b_xl5], writes=[b_st5])
                k.op("scalar", lambda s_: s_.activation(
                    xn5[:], xl5[:], AF.Square, accum_out=st5[0:1, 1:2]),
                    reads=[b_xl5, k.const_buf], writes=[b_xn5, b_st5])
                k.op("vector", lambda v: v.tensor_scalar_mul(
                    st5[0:1, 2:4], st5[0:1, 0:2], 1.0 / D),
                    reads=[b_st5], writes=[b_st5])
                k.op("vector", lambda v: v.tensor_tensor(
                    st5[0:1, 4:5], st5[0:1, 2:3], st5[0:1, 2:3], ALU.mult),
                    reads=[b_st5], writes=[b_st5])
                k.op("vector", lambda v: v.tensor_sub(
                    st5[0:1, 5:6], st5[0:1, 3:4], st5[0:1, 4:5]),
                    reads=[b_st5], writes=[b_st5])
                k.op("scalar", lambda s_: s_.activation(
                    st5[0:1, 6:7], st5[0:1, 5:6], AF.Ln, bias=k.constv(1e-6)),
                    reads=[b_st5, k.const_buf], writes=[b_st5])
                k.op("scalar", lambda s_: s_.activation(
                    st5[0:1, 7:8], st5[0:1, 6:7], AF.Exp, scale=k.constv(-0.5)),
                    reads=[b_st5, k.const_buf], writes=[b_st5])
                k.op("vector", lambda v: v.tensor_scalar(
                    xn5[:], xl5[:], st5[0:1, 2:3], st5[0:1, 7:8],
                    ALU.subtract, ALU.mult),
                    reads=[b_xl5, b_st5], writes=[b_xn5])
                k.dma("sync", rows5[:], I["in_gamma"][None, :], reads=[b_D2b],
                      writes=[b_rows5], tag="ing")
                k.dma("sync", rows5b[:], I["in_beta"][None, :],
                      reads=[b_Bet6, b_Gam6], writes=[b_rows5b], tag="inb")
                k.op("vector", lambda v: v.tensor_tensor(
                    xn5[:], xn5[:], rows5[:], ALU.mult),
                    reads=[b_xn5, b_rows5], writes=[b_xn5])
                k.op("vector", lambda v: v.tensor_tensor(
                    xn5[:], xn5[:], rows5b[:], ALU.add),
                    reads=[b_xn5, b_rows5b], writes=[b_xn5])
                k.dma("sync", cg_in, xn5[:], reads=[b_xn5], writes=[b_cg], tag="cgw")
                k.op("gpsimd", lambda g: g.collective_compute(
                    "AllGather", ALU.bypass, replica_groups=[list(range(8))],
                    ins=[cg_in], outs=[cg_out]),
                    reads=[b_cg], writes=[b_cg], tag="allgather")
                k.dma("sync", xg8[:], cg_out, reads=[b_cg], writes=[b_xg8],
                      tag="cgr")
                k.op("vector", lambda v: v.tensor_copy(xg8b[:], xg8[:]),
                     reads=[b_xg8], writes=[b_xg8b])

            def row_to_cols(src_b, b_src, dst, b_dst, nt, tag):
                for j in range(nt):
                    k.op("tensor", lambda tt_, j=j: tt_.transpose(
                        ps5t[:, 8 * j:8 * (j + 1)],
                        src_b[:, 128 * j:128 * (j + 1)], ident[0:8, 0:8]),
                        reads=[b_src, b_ident], writes=[b_ps5t], tag=f"tp5_{tag}")
                k.op("vector", lambda v, nt=nt: v.tensor_copy(
                    dst[:, 0:8 * nt], ps5t[:, 0:8 * nt]),
                    reads=[b_ps5t], writes=[b_dst], tag=f"tp5e_{tag}")

            def mm_throw(src_cols, b_src, wname, bias_off, tag):
                for cb in range(4):
                    k.op("tensor", lambda tt_, cb=cb, bias_off=bias_off: tt_.matmul(
                        ps5[0:8, 512 * cb:512 * (cb + 1)], ones_row[0:1, 0:8],
                        brow[:, bias_off + 512 * cb:bias_off + 512 * (cb + 1)],
                        start=True, stop=False, skip_group_check=True),
                        reads=[b_ones, b_brow], writes=[b_ps5], tag=f"{tag}b{cb}")
                    for d in range(DT):
                        wi = (cb * DT + d) % 2
                        k.dma("gpsimd", aggb_blk[wi][:],
                              I[wname][128 * d:128 * (d + 1),
                                       512 * cb:512 * (cb + 1)],
                              writes=[b_aggblk[wi]], tag=f"{tag}w{cb}_{d}")
                        k.op("tensor", lambda tt_, cb=cb, d=d, wi=wi: tt_.matmul(
                            ps5[0:8, 512 * cb:512 * (cb + 1)],
                            src_cols[:, 8 * d:8 * (d + 1)],
                            aggb_blk[wi][:],
                            start=False, stop=(d == DT - 1),
                            skip_group_check=True),
                            reads=[b_src, b_aggblk[wi]], writes=[b_ps5],
                            tag=f"{tag}m{cb}")

            def seg_agg():
                row_to_cols(xg8b, b_xg8b, tT5, b_tT5, 16, "xn")
                mm_throw(tT5, b_tT5, "agg_w", 0, "agg")
                k.op("vector", lambda v: v.tensor_copy(throw[:], ps5[0:8, :]),
                     reads=[b_ps5], writes=[b_throw], tag="th0")
                if debug:
                    k.dma("sync", dbg["throw"], throw[:], reads=[b_throw],
                          tag="dbg_throw")

            def seg_iter(it5):
                k.op("vector", lambda v: v.reduce_sum(
                    st5[:, 0:1], throw[:], axis=AX.X),
                    reads=[b_throw], writes=[b_st5])
                k.op("scalar", lambda s_: s_.activation(
                    trow[:], throw[:], AF.Square, accum_out=st5[:, 1:2]),
                    reads=[b_throw, k.const_buf], writes=[b_trow, b_st5])
                k.op("vector", lambda v: v.tensor_scalar_mul(
                    st5[:, 2:4], st5[:, 0:2], 1.0 / D),
                    reads=[b_st5], writes=[b_st5])
                k.op("vector", lambda v: v.tensor_tensor(
                    st5[:, 4:5], st5[:, 2:3], st5[:, 2:3], ALU.mult),
                    reads=[b_st5], writes=[b_st5])
                k.op("vector", lambda v: v.tensor_sub(
                    st5[:, 5:6], st5[:, 3:4], st5[:, 4:5]),
                    reads=[b_st5], writes=[b_st5])
                k.op("scalar", lambda s_: s_.activation(
                    st5[:, 6:7], st5[:, 5:6], AF.Ln, bias=k.constv(1e-6)),
                    reads=[b_st5, k.const_buf], writes=[b_st5])
                k.op("scalar", lambda s_: s_.activation(
                    st5[:, 7:8], st5[:, 6:7], AF.Exp, scale=k.constv(-0.5)),
                    reads=[b_st5, k.const_buf], writes=[b_st5])
                k.op("vector", lambda v: v.tensor_scalar(
                    trow[:], throw[:], st5[:, 2:3], st5[:, 7:8],
                    ALU.subtract, ALU.mult),
                    reads=[b_throw, b_st5], writes=[b_trow])
                k.op("vector", lambda v: v.tensor_tensor(
                    trow[:], trow[:], Gpn[:], ALU.mult),
                    reads=[b_trow, b_Gpn], writes=[b_trow])
                k.op("vector", lambda v: v.tensor_tensor(
                    trow[:], trow[:], Bpn[:], ALU.add),
                    reads=[b_trow, b_Bpn], writes=[b_trow])
                k.op("vector", lambda v: v.tensor_copy(trowb[:], trow[:]),
                     reads=[b_trow], writes=[b_trowb])
                row_to_cols(trowb, b_trowb, tT5, b_tT5, 16, f"t{it5}")
                for nb_ in range(2):
                    k.op("tensor", lambda tt_, nb_=nb_: tt_.matmul(
                        ps5[0:8, 512 * nb_:512 * (nb_ + 1)], ones_row[0:1, 0:8],
                        brow[:, 2048 + 512 * nb_:2048 + 512 * (nb_ + 1)],
                        start=True, stop=False, skip_group_check=True),
                        reads=[b_ones, b_brow], writes=[b_ps5], tag="d1b")
                for d in range(DT):
                    di = d % 2
                    k.dma("gpsimd", d1b2[di][:],
                          I["d1_sl"][128 * d:128 * (d + 1), :],
                          writes=[b_d1b2[di]], tag=f"d1s{d}")
                    for nb_ in range(2):
                        k.op("tensor", lambda tt_, nb_=nb_, d=d, di=di: tt_.matmul(
                            ps5[0:8, 512 * nb_:512 * (nb_ + 1)],
                            tT5[:, 8 * d:8 * (d + 1)],
                            d1b2[di][:, 512 * nb_:512 * nb_ + 512],
                            start=False, stop=(d == DT - 1),
                            skip_group_check=True),
                            reads=[b_tT5, b_d1b2[di]], writes=[b_ps5], tag="d1m")
                k.op("scalar", lambda s_: s_.activation(
                    hrow[:], ps5[0:8, 0:1024], AF.Gelu),
                    reads=[b_ps5, k.const_buf], writes=[b_hrow], tag="gelu")
                k.op("vector", lambda v: v.tensor_copy(hrowb[:], hrow[:]),
                     reads=[b_hrow], writes=[b_hrowb])
                row_to_cols(hrowb, b_hrowb, hT5, b_hT5, 8, f"h{it5}")
                for kk2 in range(8):
                    ki = kk2 % 2
                    k.dma("gpsimd", d2b2[ki][:],
                          I["d2_sl"][128 * kk2:128 * (kk2 + 1), :],
                          writes=[b_d2b2[ki]], tag=f"d2s{kk2}")
                    for oc in range(4):
                        k.op("tensor", lambda tt_, oc=oc, kk2=kk2, ki=ki: tt_.matmul(
                            ps5[0:8, 512 * oc:512 * (oc + 1)],
                            hT5[:, 8 * kk2:8 * (kk2 + 1)],
                            d2b2[ki][:, 512 * oc:512 * oc + 512],
                            start=(kk2 == 0), stop=(kk2 == 7)),
                            reads=[b_hT5, b_d2b2[ki]], writes=[b_ps5], tag="d2m")
                k.op("vector", lambda v: v.tensor_copy(rsum[:], ps5[0:8, :]),
                     reads=[b_ps5], writes=[b_rsum], tag="d2ev")
                k.dma("sync", cc_in, rsum[:], reads=[b_rsum], writes=[b_cc],
                      tag="ccw")
                k.op("gpsimd", lambda g: g.collective_compute(
                    "AllReduce", ALU.add, replica_groups=[list(range(8))],
                    ins=[cc_in], outs=[cc_out]),
                    reads=[b_cc], writes=[b_cc], tag="allreduce")
                k.dma("sync", rsum[:], cc_out, reads=[b_cc], writes=[b_rsum],
                      tag="ccr")
                k.op("vector", lambda v: v.tensor_tensor(
                    rsum[:], rsum[:], D2b[:], ALU.add),
                    reads=[b_rsum, b_D2b], writes=[b_rsum])
                k.op("vector", lambda v: v.tensor_tensor(
                    throw[:], throw[:], rsum[:], ALU.add),
                    reads=[b_throw, b_rsum], writes=[b_throw], tag="thadd")

            def seg_tb():
                k.op("vector", lambda v: v.tensor_copy(trowb[:], throw[:]),
                     reads=[b_throw], writes=[b_trowb])
                row_to_cols(trowb, b_trowb, tT5, b_tT5, 16, "tb")
                mm_throw(tT5, b_tT5, "bc_w", 3072, "bc")
                k.op("vector", lambda v: v.tensor_copy(TB8[:], ps5[0:8, :]),
                     reads=[b_ps5], writes=[b_TB8], tag="tb8")
                if debug:
                    k.dma("sync", dbg["TB8"], TB8[:], reads=[b_TB8], tag="dbg_tb8")
                for i4 in range(4):
                    pp_ = psg[:, 0:512] if i4 % 2 == 0 else psg[:, 512:1024]
                    k.op("tensor", lambda tt_, i4=i4, pp_=pp_: tt_.matmul(
                        pp_, sel8[:], TB8[:, 512 * i4:512 * (i4 + 1)],
                        start=True, stop=True),
                        reads=[b_sel8, b_TB8], writes=[b_psg], tag="sel_mm")
                    k.op("vector", lambda v, i4=i4, pp_=pp_: v.tensor_copy(
                        Bt5[:, 512 * i4:512 * (i4 + 1)], pp_),
                        reads=[b_psg], writes=[b_Bt5], tag="sel_ev")
                if debug:
                    k.dma("sync", dbg["Bt5"], Bt5[:], reads=[b_Bt5], tag="dbg_bt5")

            segs = [seg_loads, seg_pre, seg_xn, seg_agg] + \
                   [lambda it5=i: seg_iter(it5) for i in range(4)] + [seg_tb]

            for d in range(DT):
                k.dma("gpsimd", gatew[:, 2048 * d:2048 * (d + 1)],
                      I["gate_w"][128 * d:128 * (d + 1), :], writes=[b_gw],
                      tag=f"gw{d}")
            k.dma("gpsimd", grow[:], I["gate_b"][None, :], writes=[b_grow])

            si = 0
            for t in range(TT):
                while si < len(segs) and si * 2 <= t:
                    segs[si]()
                    si += 1
                xi = t % 2
                k.dma("sync", xftm[xi][:], sc["xfT"][128 * t:128 * (t + 1), :],
                      reads=[scb["xfT"]], writes=[b_xftm[xi]], tag=f"g_xf{t}")
                for j in range(DT):
                    k.op("tensor", lambda tt_, j=j, xi=xi: tt_.transpose(
                        ps5t[:, 128 * j:128 * (j + 1)],
                        xftm[xi][:, 128 * j:128 * (j + 1)], ident[:]),
                        reads=[b_xftm[xi], b_ident], writes=[b_ps5t],
                        tag=f"g_tp{t}")
                k.op("vector", lambda v: v.tensor_copy(stg5[:], ps5t[:]),
                     reads=[b_ps5t], writes=[b_stg5], tag="g_ev")
                for half in range(2):
                    gi = (2 * t + half) % 2
                    for gc in range(2):
                        k.op("tensor", lambda tt_, gc=gc, half=half: tt_.matmul(
                            psg[:, 512 * gc:512 * (gc + 1)], ones_row[:],
                            grow[:, 1024 * half + 512 * gc:
                                 1024 * half + 512 * gc + 512],
                            start=True, stop=False, skip_group_check=True),
                            reads=[b_ones, b_grow], writes=[b_psg], tag="g_b")
                    for d in range(DT):
                        for gc in range(2):
                            k.op("tensor", lambda tt_, d=d, gc=gc, half=half:
                                 tt_.matmul(
                                     psg[:, 512 * gc:512 * (gc + 1)],
                                     stg5[:, 128 * d:128 * (d + 1)],
                                     gatew[:, 2048 * d + 1024 * half + 512 * gc:
                                           2048 * d + 1024 * half + 512 * gc + 512],
                                     start=False, stop=(d == DT - 1),
                                     skip_group_check=True),
                                 reads=[b_stg5, b_gw], writes=[b_psg], tag="g_mm")
                    k.op("scalar", lambda s_, gi=gi: s_.activation(
                        gsx[gi][:], psg[:], AF.Exp, scale=k.constv(-1.0)),
                        reads=[b_psg, k.const_buf], writes=[b_gsx[gi]],
                        tag="g_sige")
                    k.op("vector", lambda v, gi=gi: v.tensor_scalar_add(
                        gsx[gi][:], gsx[gi][:], 1.0),
                        reads=[b_gsx[gi]], writes=[b_gsx[gi]])
                    k.op("vector", lambda v, gi=gi: v.reciprocal(
                        gsx[gi][:], gsx[gi][:]),
                        reads=[b_gsx[gi]], writes=[b_gsx[gi]])
                    k.op("vector", lambda v, gi=gi: v.tensor_copy(
                        gst[gi][:], gsx[gi][:]),
                        reads=[b_gsx[gi]], writes=[b_gst[gi]], tag="g_sig")
                    k.dma("sync",
                          sc["gate"][128 * t:128 * (t + 1),
                                     1024 * half:1024 * (half + 1)],
                          gst[gi][:], reads=[b_gst[gi]], writes=[scb["gate"]],
                          tag=f"g_sp{t}_{half}")
            while si < len(segs):
                segs[si]()
                si += 1
        k.barrier()

        with ExitStack() as es5b:
            ggb = [k.sb(f"ggb{i}", [128, 2048], BF16, es=es5b) for i in range(2)]
            b_ggb = [Buf(f"ggb{i}") for i in range(2)]
            xfb5 = [k.sb(f"xfb5{i}", [128, 2048], F32, es=es5b) for i in range(2)]
            b_xfb5 = [Buf(f"xfb5{i}") for i in range(2)]
            x2b5 = [k.sb(f"x2b5{i}", [128, 2048], F32, es=es5b) for i in range(2)]
            b_x2b5 = [Buf(f"x2b5{i}") for i in range(2)]
            sc1b = k.sb("sc1b", [128, 2048], F32, es=es5b); b_sc1b = Buf("sc1b")
            sc2b = k.sb("sc2b", [128, 2048], F32, es=es5b); b_sc2b = Buf("sc2b")
            outb = [k.sb(f"outb{i}", [128, 2048], F32, es=es5b) for i in range(2)]
            b_outb = [Buf(f"outb{i}") for i in range(2)]
            st6 = k.sb("st6", [128, 8], F32, es=es5b); b_st6 = Buf("st6")
            sqv6 = k.sb("sqv6", [128, 2048], F32, es=es5b); b_sqv6 = Buf("sqv6")

            for t in range(TT):
                xi = 0
                k.dma("sync", ggb[xi][:], sc["gate"][128 * t:128 * (t + 1), :],
                      reads=[scb["gate"]], writes=[b_ggb[xi]], tag=f"c_g{t}")
                k.dma("sync", xfb5[xi][:], sc["xf"][128 * t:128 * (t + 1), :],
                      reads=[scb["xf"]], writes=[b_xfb5[xi]], tag=f"c_xf{t}")
                k.dma("scalar", x2b5[xi][:], sc["x2"][128 * t:128 * (t + 1), :],
                      reads=[scb["x2"]], writes=[b_x2b5[xi]], tag=f"c_x2{t}")
                k.op("vector", lambda v, xi=xi: v.tensor_tensor(
                    sc1b[:], ggb[xi][:], Bt5[:], ALU.mult),
                    reads=[b_ggb[xi], b_Bt5], writes=[b_sc1b], tag="c_gTB")
                k.op("gpsimd", lambda g, xi=xi: g.tensor_tensor(
                    sc2b[:], sc1b[:], xfb5[xi][:], ALU.add),
                    reads=[b_sc1b, b_xfb5[xi]], writes=[b_sc2b], tag="c_open")
                if debug and t == 0:
                    k.dma("sync", dbg["open0"], sc2b[:], reads=[b_sc2b],
                          tag="dbg_open0")
                k.op("vector", lambda v: v.reduce_sum(
                    st6[:, 0:1], sc2b[:], axis=AX.X),
                    reads=[b_sc2b], writes=[b_st6])
                k.op("scalar", lambda s_: s_.activation(
                    sqv6[:], sc2b[:], AF.Square, accum_out=st6[:, 1:2]),
                    reads=[b_sc2b, k.const_buf], writes=[b_sqv6, b_st6])
                k.op("vector", lambda v: v.tensor_scalar_mul(
                    st6[:, 2:4], st6[:, 0:2], 1.0 / D),
                    reads=[b_st6], writes=[b_st6])
                k.op("vector", lambda v: v.tensor_tensor(
                    st6[:, 4:5], st6[:, 2:3], st6[:, 2:3], ALU.mult),
                    reads=[b_st6], writes=[b_st6])
                k.op("vector", lambda v: v.tensor_sub(
                    st6[:, 5:6], st6[:, 3:4], st6[:, 4:5]),
                    reads=[b_st6], writes=[b_st6])
                k.op("scalar", lambda s_: s_.activation(
                    st6[:, 6:7], st6[:, 5:6], AF.Ln, bias=k.constv(1e-6)),
                    reads=[b_st6, k.const_buf], writes=[b_st6])
                k.op("scalar", lambda s_: s_.activation(
                    st6[:, 7:8], st6[:, 6:7], AF.Exp, scale=k.constv(-0.5)),
                    reads=[b_st6, k.const_buf], writes=[b_st6])
                k.op("vector", lambda v: v.tensor_scalar(
                    sc1b[:], sc2b[:], st6[:, 2:3], st6[:, 7:8],
                    ALU.subtract, ALU.mult),
                    reads=[b_sc2b, b_st6], writes=[b_sc1b], tag="c_ln6")
                k.op("gpsimd", lambda g: g.tensor_tensor(
                    sc2b[:], sc1b[:], Gam6[:], ALU.mult),
                    reads=[b_sc1b, b_Gam6], writes=[b_sc2b])
                k.op("vector", lambda v: v.tensor_tensor(
                    sc1b[:], sc2b[:], Bet6[:], ALU.add),
                    reads=[b_sc2b, b_Bet6], writes=[b_sc1b], tag="c_gb")
                if debug and t == 0:
                    k.dma("sync", dbg["gb0"], sc1b[:], reads=[b_sc1b],
                          tag="dbg_gb0")
                    k.dma("sync", dbg["gam6"], Gam6[:], reads=[b_Gam6],
                          tag="dbg_gam6")
                k.op("vector", lambda v: v.reduce_sum(
                    st6[:, 0:1], sc1b[:], axis=AX.X),
                    reads=[b_sc1b], writes=[b_st6])
                k.op("scalar", lambda s_: s_.activation(
                    sqv6[:], sc1b[:], AF.Square, accum_out=st6[:, 1:2]),
                    reads=[b_sc1b, k.const_buf], writes=[b_sqv6, b_st6])
                k.op("vector", lambda v: v.tensor_scalar_mul(
                    st6[:, 2:4], st6[:, 0:2], 1.0 / D),
                    reads=[b_st6], writes=[b_st6])
                k.op("vector", lambda v: v.tensor_tensor(
                    st6[:, 4:5], st6[:, 2:3], st6[:, 2:3], ALU.mult),
                    reads=[b_st6], writes=[b_st6])
                k.op("vector", lambda v: v.tensor_sub(
                    st6[:, 5:6], st6[:, 3:4], st6[:, 4:5]),
                    reads=[b_st6], writes=[b_st6])
                k.op("scalar", lambda s_: s_.activation(
                    st6[:, 6:7], st6[:, 5:6], AF.Ln, bias=k.constv(1e-5)),
                    reads=[b_st6, k.const_buf], writes=[b_st6])
                k.op("scalar", lambda s_: s_.activation(
                    st6[:, 7:8], st6[:, 6:7], AF.Exp, scale=k.constv(-0.5)),
                    reads=[b_st6, k.const_buf], writes=[b_st6])
                k.op("vector", lambda v: v.tensor_scalar(
                    sc2b[:], sc1b[:], st6[:, 2:3], st6[:, 7:8],
                    ALU.subtract, ALU.mult),
                    reads=[b_sc1b, b_st6], writes=[b_sc2b], tag="c_ln7")
                pi = xi
                k.op("gpsimd", lambda g, xi=xi, pi=pi: g.tensor_tensor(
                    outb[pi][:], sc2b[:], x2b5[xi][:], ALU.add),
                    reads=[b_sc2b, b_x2b5[xi]], writes=[b_outb[pi]], tag="c_out")
                k.dma("sync", out_t[128 * t:128 * (t + 1), :], outb[pi][:],
                      reads=[b_outb[pi]], tag=f"st_out{t}")

    k.build_program()
    return nc, k, sc, dbg


NAME_MAP = {
    "Wqkv": "L0_attn_Wqkv", "Wlr": "L0_attn_Wlr", "Wm": "L0_attn_Wm",
    "bm": "L0_attn_bm", "Ws": "L0_attn_Ws", "bs": "L0_attn_bs",
    "Wo": "L0_attn_Wo", "w0": "L0_attn_w0", "w1": "L0_attn_w1",
    "w2": "L0_attn_w2", "o_norm_g": "L0_attn_o_norm_g",
    "qk_scale": "L0_attn_qk_scale", "qk_offset": "L0_attn_qk_offset",
    "in_gamma": "L0_ffn_in_gamma", "in_beta": "L0_ffn_in_beta",
    "agg_w": "L0_ffn_agg_w", "agg_b": "L0_ffn_agg_b",
    "pn_gamma": "L0_ffn_pn_gamma", "pn_beta": "L0_ffn_pn_beta",
    "d2_b": "L0_ffn_d2_b",
    "bc_w": "L0_ffn_bc_w", "bc_b": "L0_ffn_bc_b",
    "gate_w": "L0_ffn_gate_w", "gate_b": "L0_ffn_gate_b",
    "out_gamma": "L0_ffn_out_gamma", "out_beta": "L0_ffn_out_beta",
}


def kernel(**inputs):
    x = np.ascontiguousarray(inputs["L0_attn_x"])
    nc, k, sc, dbg = build(debug=False)
    base = {k_: np.ascontiguousarray(inputs[v]) for k_, v in NAME_MAP.items()}
    d1w = inputs["L0_ffn_d1_w"]
    d1b = inputs["L0_ffn_d1_b"]
    d2w = inputs["L0_ffn_d2_w"]
    in_maps = []
    for c in range(8):
        m = dict(base)
        m["x"] = np.ascontiguousarray(x[c])
        m["d1_sl"] = np.ascontiguousarray(d1w[:, 1024 * c:1024 * (c + 1)])
        m["d1b_sl"] = np.ascontiguousarray(d1b[1024 * c:1024 * (c + 1)])
        m["d2_sl"] = np.ascontiguousarray(d2w[1024 * c:1024 * (c + 1), :])
        oh = np.zeros((8, 1), dtype=np.float32)
        oh[c, 0] = 1.0
        m["onehot"] = oh
        in_maps.append(m)
    res = run_bass_kernel_spmd(nc, in_maps, core_ids=list(range(8)))
    return np.stack([np.asarray(r["out"]) for r in res.results])


# version 07
# speedup vs baseline: 1.0407x; 1.0407x over previous
import math
from contextlib import ExitStack

import numpy as np
import concourse.bass as bass
import concourse.mybir as mybir
from concourse.bass_utils import run_bass_kernel_spmd

F32 = mybir.dt.float32
BF16 = mybir.dt.bfloat16
AF = mybir.ActivationFunctionType
ALU = mybir.AluOpType
AX = mybir.AxisListType

L = 2048
D = 2048
TT = 16
DT = 16
CH = 1024
NH, HD = 4, 512
NAH, AHD = 16, 128
WIN = 1024
BASE_LR_INV = float(0.01 + np.log(-np.expm1(-0.01)))
NS_COEFFS = [(4.0848, -6.8946, 2.927), (3.9505, -6.3029, 2.6377),
             (3.7418, -5.5913, 2.3037), (2.8769, -3.1427, 1.2046),
             (2.8366, -3.0525, 1.2012)]
RS = 1.0 / math.sqrt(AHD)

COMPUTE_ENGINES = ("tensor", "vector", "scalar", "gpsimd")
QUEUES = ("sync", "gpsimd", "scalar")
ALL_STREAMS = COMPUTE_ENGINES + ("sync",)


class Buf:
    __slots__ = ("name", "writer", "readers", "fresh", "psum")

    def __init__(self, name="", psum=False):
        self.name = name
        self.writer = None
        self.readers = []
        self.fresh = True
        self.psum = psum


class OpRec:
    __slots__ = ("engine", "emit", "deps", "needs_inc", "token", "is_dma", "tag",
                 "slot")

    def __init__(self, engine, emit, is_dma, tag):
        self.engine = engine
        self.emit = emit
        self.deps = []
        self.needs_inc = False
        self.token = None
        self.is_dma = is_dma
        self.tag = tag
        self.slot = 0


QSLOTS = {"sync": 8, "gpsimd": 6, "scalar": 4}


class KB:
    def __init__(self, nc):
        self.nc = nc
        self.ops = []
        self.per_engine = {e: [] for e in ALL_STREAMS}
        self.es = ExitStack()
        self.consts = {}
        self.cbuf = None
        self.const_buf = Buf("consts")
        self.guard_deps = []
        self.last_comp = {}
        self.last_dma = {}
        self.qcount = {q: 0 for q in QUEUES}
        self.slot_user = {q: [None] * QSLOTS[q] for q in QUEUES}

    def sb(self, name, shape, dtype, es=None):
        return (es or self.es).enter_context(self.nc.sbuf_tensor(name, shape, dtype))

    def ps(self, name, shape, dtype=F32, es=None):
        return (es or self.es).enter_context(self.nc.psum_tensor(name, shape, dtype))

    def constv(self, v):
        v = float(v)
        if v not in self.consts:
            idx = len(self.consts)
            assert idx < 32
            self.consts[v] = idx
            self.nc.const_aps.register(F32, v, self.cbuf[:, idx:idx + 1])
        return v

    def barrier(self):
        deps = list(self.last_comp.values()) + list(self.last_dma.values())
        for d in deps:
            d.needs_inc = True
        self.guard_deps = deps

    def op(self, engine, emit, reads=(), writes=(), tag=""):
        rec = OpRec(engine, emit, False, tag)
        self._resolve(rec, reads, writes, True)
        self.last_comp[engine] = rec
        return rec

    def dma(self, queue, out, in_, reads=(), writes=(), tag="", **kw):
        def emit(eng, out=out, in_=in_, kw=kw):
            return eng.dma_start(out=out, in_=in_, **kw)
        rec = OpRec(queue, emit, True, tag)
        i = self.qcount[queue]
        self.qcount[queue] += 1
        nslots = QSLOTS[queue]
        rec.slot = i % nslots
        rec.token = 16 * (i // nslots + 1)
        prev = self.slot_user[queue][rec.slot]
        self._resolve(rec, reads, writes, False, extra_dep=prev)
        self.slot_user[queue][rec.slot] = rec
        self.last_dma[queue] = rec
        return rec

    def _resolve(self, rec, reads, writes, same_engine_skip, extra_dep=None):
        deps = []
        if extra_dep is not None:
            deps.append(extra_dep)

        def need(d):
            if d is None or d is rec:
                return
            if (same_engine_skip and (not d.is_dma)
                    and d.engine == rec.engine == "tensor"):
                return
            deps.append(d)

        for b in reads:
            need(b.writer)
            if b.psum:
                for r in b.readers:
                    if r.engine != rec.engine or r.is_dma:
                        need(r)
        for b in writes:
            need(b.writer)
            for r in b.readers:
                need(r)
            if b.fresh:
                b.fresh = False
                for d in self.guard_deps:
                    need(d)
        seen = set()
        rec.deps = []
        for d in deps:
            if id(d) not in seen:
                seen.add(id(d))
                rec.deps.append(d)
                d.needs_inc = True
        for b in reads:
            b.readers.append(rec)
        for b in writes:
            b.writer = rec
            b.readers = []
        self.ops.append(rec)
        self.per_engine[rec.engine].append(rec)

    def build_program(self):
        nc = self.nc
        ecount = {e: 0 for e in COMPUTE_ENGINES}
        for rec in self.ops:
            if (not rec.is_dma) and rec.needs_inc:
                ecount[rec.engine] += 1
                rec.token = ecount[rec.engine]

        sem_e = {e: self.es.enter_context(nc.semaphore(f"se_{e}"))
                 for e in COMPUTE_ENGINES}
        sem_q = {(q, s): self.es.enter_context(nc.semaphore(f"sq_{q}{s}"))
                 for q in QUEUES for s in range(QSLOTS[q])}
        block = self.es.enter_context(nc.Block())
        kb = self

        def section(engine_name):
            def run(eng):
                wm = {}
                for rec in kb.per_engine[engine_name]:
                    for d in rec.deps:
                        key = (d.is_dma, d.engine, d.slot)
                        if wm.get(key, 0) < d.token:
                            wm[key] = d.token
                            sem = (sem_q[(d.engine, d.slot)] if d.is_dma
                                   else sem_e[d.engine])
                            eng.wait_ge(sem, d.token)
                    inst = rec.emit(eng)
                    if rec.is_dma:
                        inst.then_inc(sem_q[(rec.engine, rec.slot)], 16)
                    elif rec.needs_inc:
                        inst.then_inc(sem_e[rec.engine], 1)
                if engine_name == "sync":
                    for q in QUEUES:
                        for s in range(QSLOTS[q]):
                            u = kb.slot_user[q][s]
                            if u is not None:
                                eng.wait_ge(sem_q[(q, s)], u.token)
            return run

        block.sync(section("sync"))
        block.tensor(section("tensor"))
        block.vector(section("vector"))
        block.scalar(section("scalar"))
        block.gpsimd(section("gpsimd"))


def build(debug=False, stages=('ln','qkv','p1c','p2','p3','p5')):
    nc = bass.Bass("TRN2", target_bir_lowering=False)
    k = KB(nc)

    I = {}
    def inp(name, shape):
        I[name] = nc.dram_tensor(name, shape, F32, kind="ExternalInput").ap()
    inp("x", [L, D])
    inp("Wqkv", [D, 3 * D]); inp("Wlr", [D, 12]); inp("Wm", [D, 4])
    inp("bm", [4]); inp("Ws", [D, 4]); inp("bs", [4]); inp("Wo", [D, D])
    inp("w0", [NH, 512, 512]); inp("w1", [NH, 512, 512]); inp("w2", [NH, 512, 512])
    inp("o_norm_g", [512]); inp("qk_scale", [D, 2]); inp("qk_offset", [D, 2])
    inp("in_gamma", [D]); inp("in_beta", [D])
    inp("agg_w", [D, D]); inp("agg_b", [D])
    inp("pn_gamma", [D]); inp("pn_beta", [D])
    inp("d1_sl", [D, 1024]); inp("d1b_sl", [1024])
    inp("d2_sl", [1024, D]); inp("d2_b", [D])
    inp("onehot", [8, 1])
    inp("bc_w", [D, D]); inp("bc_b", [D])
    inp("gate_w", [D, D]); inp("gate_b", [D])
    inp("out_gamma", [D]); inp("out_beta", [D])
    out_t = nc.dram_tensor("out", [L, D], F32, kind="ExternalOutput").ap()

    sckind = "ExternalOutput" if debug else "Internal"
    sc = {}
    scb = {}
    def scratch(name, shape, dtype):
        sc[name] = nc.dram_tensor("sc_" + name, shape, dtype, kind=sckind).ap()
        scb[name] = Buf("scb_" + name)
    scratch("aq_tm", [L, D], BF16); scratch("ak_tm", [L, D], BF16)
    scratch("av", [L, D], BF16)
    scratch("Q_tm", [L, D], BF16); scratch("K", [L, D], BF16)
    scratch("V", [L, D], BF16)
    scratch("aq", [D, L], BF16); scratch("ak", [D, L], BF16)
    scratch("QT", [D, L], BF16); scratch("KT", [D, L], BF16)
    scratch("VT", [D, L], BF16)
    scratch("ttt", [D, L], BF16)
    scratch("x2", [L, D], F32); scratch("xf", [L, D], F32)
    scratch("gate", [L, D], BF16)
    scratch("xfT", [D, L], BF16)

    dbg = {}
    if debug:
        def dbgout(name, shape, dtype):
            dbg[name] = nc.dram_tensor("dbg_" + name, shape, dtype,
                                       kind="ExternalOutput").ap()
        dbgout("lrv", [128, TT * 20], F32)
        dbgout("ln0T", [128, DT * L], BF16)
        dbgout("throw", [8, 2048], F32)
        dbgout("open0", [128, 2048], F32)
        dbgout("gb0", [128, 2048], F32)
        dbgout("gam6", [128, 2048], F32)
        dbgout("TB8", [8, 2048], F32)
        dbgout("Bt5", [128, 2048], F32)

    k.cbuf = k.sb("cbuf", [128, 32], F32)
    ident = k.sb("ident", [128, 128], BF16)
    ident_f = k.sb("ident_f", [128, 128], F32)
    ones_col = k.sb("ones_col", [128, 1], BF16)
    ones_row = k.sb("ones_row", [1, 128], BF16)
    ones_col_f = k.sb("ones_col_f", [128, 1], F32)
    ones_row_f = k.sb("ones_row_f", [1, 128], F32)
    b_ident = Buf("ident"); b_identf = Buf("identf"); b_ones = Buf("ones")

    qksc = k.sb("qksc", [128, 32], F32)
    qkof = k.sb("qkof", [128, 32], F32)
    sAs = k.sb("sAs", [128, 16], F32)
    oAs = k.sb("oAs", [128, 16], F32)
    b_qk = Buf("qk"); b_qks = Buf("qks")

    lrms_w = k.sb("lrms_w", [128, DT * 20], BF16)
    bmbs = k.sb("bmbs", [1, 20], BF16)
    b_lrmsw = Buf("lrms_w"); b_bmbs = Buf("bmbs")
    lrv = k.sb("lrv", [128, TT * 20], F32)
    b_lrv = [Buf(f"lrv{t}") for t in range(TT)]

    for v in (1e-5, -0.5, 0.5, -1.0, 1.0, 1e-6, 1.0 / 512.0, 1.0 / D, 1e-7):
        k.constv(v)

    def init_cbuf(g):
        inst = None
        for v, idx in k.consts.items():
            inst = g.memset(k.cbuf[:, idx:idx + 1], v)
        return inst
    k.op("gpsimd", init_cbuf, writes=[k.const_buf], tag="init_cbuf")

    def init_ones(g):
        g.memset(ones_col[:], 1.0)
        g.memset(ones_row[:], 1.0)
        g.memset(ones_col_f[:], 1.0)
        return g.memset(ones_row_f[:], 1.0)
    k.op("gpsimd", init_ones, writes=[b_ones], tag="init_ones")
    k.op("gpsimd", lambda g: g.memset(ident_f[:], 1.0), writes=[b_identf])
    k.op("gpsimd", lambda g: g.affine_select(
        ident_f[:], ident_f[:], [[1, 128]], ALU.is_equal, 0.0,
        base=0, channel_multiplier=-1), reads=[b_identf], writes=[b_identf])
    k.op("gpsimd", lambda g: g.memset(ident[:], 1.0), writes=[b_ident])
    k.op("gpsimd", lambda g: g.affine_select(
        ident[:], ident[:], [[1, 128]], ALU.is_equal, 0.0,
        base=0, channel_multiplier=-1), reads=[b_ident], writes=[b_ident])

    for f in range(16):
        k.dma("sync", qksc[:, 2 * f:2 * f + 2], I["qk_scale"][128 * f:128 * (f + 1), :],
              writes=[b_qk], tag="qksc")
        k.dma("sync", qkof[:, 2 * f:2 * f + 2], I["qk_offset"][128 * f:128 * (f + 1), :],
              writes=[b_qk], tag="qkof")
    qksc_r = qksc[:].rearrange("p (f two) -> p two f", two=2)
    qkof_r = qkof[:].rearrange("p (f two) -> p two f", two=2)
    k.op("vector", lambda v: v.tensor_scalar_mul(sAs[:], qksc_r[:, 0], RS),
         reads=[b_qk], writes=[b_qks], tag="sAs")
    k.op("vector", lambda v: v.tensor_scalar_mul(oAs[:], qkof_r[:, 0], RS),
         reads=[b_qk], writes=[b_qks], tag="oAs")

    for d in range(DT):
        r = slice(128 * d, 128 * (d + 1))
        c0 = d * 20
        k.dma("gpsimd", lrms_w[:, c0:c0 + 12], I["Wlr"][r, :], writes=[b_lrmsw])
        k.dma("gpsimd", lrms_w[:, c0 + 12:c0 + 16], I["Wm"][r, :], writes=[b_lrmsw])
        k.dma("gpsimd", lrms_w[:, c0 + 16:c0 + 20], I["Ws"][r, :], writes=[b_lrmsw])
    onorm = k.sb("onorm", [128, 4], F32)
    b_onorm = Buf("onorm")
    for ot in range(4):
        k.dma("sync", onorm[:, ot:ot + 1], I["o_norm_g"][128 * ot:128 * (ot + 1)],
              writes=[b_onorm], tag="onorm")
    k.op("gpsimd", lambda g: g.memset(bmbs[:], 0.0), writes=[b_bmbs])
    k.dma("gpsimd", bmbs[:, 12:16], I["bm"][None, :], writes=[b_bmbs])
    k.dma("gpsimd", bmbs[:, 16:20], I["bs"][None, :], writes=[b_bmbs])

    p1ps = ExitStack()
    ps_tp = [k.ps(f"ps_tp{i}", [128, 2048], BF16, es=p1ps) for i in range(2)]
    b_pstp = [Buf(f"pstp{i}", psum=True) for i in range(2)]
    ps_f = [k.ps(f"ps_f{i}", [128, 512], F32, es=p1ps) for i in range(4)]
    b_psf = [Buf(f"psf{i}", psum=True) for i in range(4)]

    ln0T_es = ExitStack()
    ln0T = k.sb("ln0T", [128, DT * L], BF16, es=ln0T_es)
    b_ln0T = [Buf(f"ln0T{t}") for t in range(TT)]

    def ln0T_tile(d, t):
        return ln0T[:, d * L + 128 * t: d * L + 128 * (t + 1)]

    if 'ln' not in stages:
        stages = stages
    with ExitStack() as esln:
        xb = [k.sb(f"xb{i}", [128, D], F32, es=esln) for i in range(2)]
        b_xb = [Buf(f"xb{i}") for i in range(2)]
        ln0b = [k.sb(f"ln0b{i}", [128, D], BF16, es=esln) for i in range(2)]
        b_ln0b = [Buf(f"ln0b{i}") for i in range(2)]
        sqv = k.sb("sqv", [128, D], F32, es=esln)
        b_sqv = Buf("sqv")
        stat = k.sb("stat", [128, 8], F32, es=esln)
        b_stat = Buf("stat")
        lr_scr = k.sb("lr_scr", [128, 24], F32, es=esln)
        b_lrscr = Buf("lrscr")
        b_pslr = b_psf[0]
        ps_lr = ps_f[0]

        for t in range(TT):
            xi = t % 2
            bx, bl = b_xb[xi], b_ln0b[xi]
            xcur, lcur = xb[xi], ln0b[xi]
            k.dma("sync", xcur[:], I["x"][128 * t:128 * (t + 1), :],
                  writes=[bx], tag=f"x{t}")
            k.op("vector", lambda v, x_=xcur: v.reduce_sum(
                stat[:, 0:1], x_[:], axis=AX.X), reads=[bx], writes=[b_stat])
            k.op("scalar", lambda s, x_=xcur: s.activation(
                sqv[:], x_[:], AF.Square, accum_out=stat[:, 1:2]),
                reads=[bx, k.const_buf], writes=[b_sqv, b_stat])
            k.op("vector", lambda v: v.tensor_scalar_mul(
                stat[:, 2:4], stat[:, 0:2], 1.0 / D),
                reads=[b_stat], writes=[b_stat])
            k.op("vector", lambda v: v.tensor_tensor(
                stat[:, 4:5], stat[:, 2:3], stat[:, 2:3], ALU.mult),
                reads=[b_stat], writes=[b_stat])
            k.op("vector", lambda v: v.tensor_sub(
                stat[:, 5:6], stat[:, 3:4], stat[:, 4:5]),
                reads=[b_stat], writes=[b_stat])
            k.op("scalar", lambda s: s.activation(
                stat[:, 6:7], stat[:, 5:6], AF.Ln, bias=k.constv(1e-5)),
                reads=[b_stat, k.const_buf], writes=[b_stat])
            k.op("scalar", lambda s: s.activation(
                stat[:, 7:8], stat[:, 6:7], AF.Exp, scale=k.constv(-0.5)),
                reads=[b_stat, k.const_buf], writes=[b_stat])
            k.op("vector", lambda v, x_=xcur, o=lcur: v.tensor_scalar(
                o[:], x_[:], stat[:, 2:3], stat[:, 7:8], ALU.subtract, ALU.mult),
                reads=[bx, b_stat], writes=[bl])
            pi = t % 2
            for j in range(DT):
                k.op("tensor", lambda tt_, j=j, pi=pi, src=lcur: tt_.transpose(
                    ps_tp[pi][:, 128 * j:128 * (j + 1)],
                    src[:, 128 * j:128 * (j + 1)], ident[:]),
                    reads=[bl, b_ident], writes=[b_pstp[pi]], tag=f"tpln{t}_{j}")
            dstv = ln0T[:].rearrange("p (d c) -> p d c", d=DT)[:, :, 128 * t:128 * (t + 1)]
            srcv = ps_tp[pi][:].rearrange("p (d c) -> p d c", d=DT)
            k.op("vector", lambda v, dstv=dstv, srcv=srcv: v.tensor_copy(dstv, srcv),
                 reads=[b_pstp[pi]], writes=[b_ln0T[t]], tag=f"evln0T{t}")

            k.op("tensor", lambda tt_: tt_.matmul(
                ps_lr[:, 0:20], ones_row[:], bmbs[:], start=True, stop=False,
                skip_group_check=True),
                reads=[b_ones, b_bmbs], writes=[b_pslr], tag=f"lrbias{t}")
            for d in range(DT):
                k.op("tensor", lambda tt_, d=d, t=t: tt_.matmul(
                    ps_lr[:, 0:20], ln0T_tile(d, t),
                    lrms_w[:, d * 20:(d + 1) * 20],
                    start=False, stop=(d == DT - 1), skip_group_check=True),
                    reads=[b_ln0T[t], b_lrmsw], writes=[b_pslr], tag=f"lrmm{t}")
            c0 = t * 20
            k.op("vector", lambda v: v.tensor_scalar_add(
                lr_scr[:, 0:12], ps_lr[:, 0:12], BASE_LR_INV),
                reads=[b_pslr], writes=[b_lrscr])
            k.op("scalar", lambda s: s.activation(
                lr_scr[:, 12:24], lr_scr[:, 0:12], AF.Abs),
                reads=[b_lrscr, k.const_buf], writes=[b_lrscr])
            k.op("scalar", lambda s: s.activation(
                lr_scr[:, 12:24], lr_scr[:, 12:24], AF.Exp, scale=k.constv(-1.0)),
                reads=[b_lrscr, k.const_buf], writes=[b_lrscr])
            k.op("scalar", lambda s: s.activation(
                lr_scr[:, 12:24], lr_scr[:, 12:24], AF.Ln, bias=k.constv(1.0)),
                reads=[b_lrscr, k.const_buf], writes=[b_lrscr])
            k.op("vector", lambda v: v.tensor_scalar_max(
                lr_scr[:, 0:12], lr_scr[:, 0:12], 0.0),
                reads=[b_lrscr], writes=[b_lrscr])
            k.op("vector", lambda v, c0=c0: v.tensor_add(
                lrv[:, c0:c0 + 12], lr_scr[:, 0:12], lr_scr[:, 12:24]),
                reads=[b_lrscr], writes=[b_lrv[t]])
            k.op("scalar", lambda s: s.activation(
                lr_scr[:, 0:8], ps_lr[:, 12:20], AF.Exp, scale=k.constv(-1.0)),
                reads=[b_pslr, k.const_buf], writes=[b_lrscr])
            k.op("vector", lambda v: v.tensor_scalar_add(
                lr_scr[:, 0:8], lr_scr[:, 0:8], 1.0),
                reads=[b_lrscr], writes=[b_lrscr])
            k.op("vector", lambda v: v.reciprocal(
                lr_scr[:, 8:16], lr_scr[:, 0:8]),
                reads=[b_lrscr], writes=[b_lrscr])
            k.op("vector", lambda v, c0=c0: v.tensor_copy(
                lrv[:, c0 + 12:c0 + 16], lr_scr[:, 8:12]),
                reads=[b_lrscr], writes=[b_lrv[t]])
            k.op("vector", lambda v, c0=c0: v.tensor_tensor(
                lrv[:, c0 + 16:c0 + 20], lr_scr[:, 12:16], ps_lr[:, 16:20],
                ALU.mult),
                reads=[b_lrscr, b_pslr], writes=[b_lrv[t]])
    k.barrier()

    if debug:
        k.dma("scalar", dbg["lrv"], lrv[:], reads=[*b_lrv], tag="dbg_lrv")
        k.dma("scalar", dbg["ln0T"], ln0T[:], reads=[*b_ln0T], tag="dbg_ln0T")

    if 'qkv' in stages:
     with ExitStack() as esq:
        wqb = [k.sb(f"wqb{i}", [128, DT * 512], BF16, es=esq) for i in range(2)]
        b_wqb = [Buf(f"wqb{i}") for i in range(2)]
        sigA = [k.sb(f"sig{i}", [128, 512], F32, es=esq) for i in range(2)]
        b_sigA = [Buf(f"sig{i}") for i in range(2)]
        fvalA = [k.sb(f"fval{i}", [128, 512], F32, es=esq) for i in range(2)]
        b_fvalA = [Buf(f"fval{i}") for i in range(2)]
        st8 = [k.sb(f"st8_{i}", [128, 512], BF16, es=esq) for i in range(8)]
        b_st8 = [Buf(f"st8_{i}") for i in range(8)]
        qsil = k.sb("qsil", [128, 16 * 512], BF16, es=esq)
        b_qsil = Buf("qsil")
        l2st = k.sb("l2st", [128, 32], F32, es=esq)
        b_l2st = Buf("l2st")
        sqj = [k.sb(f"sqj{i}", [128, 512], BF16, es=esq) for i in range(2)]
        b_sqj = [Buf(f"sqj{i}") for i in range(2)]
        bc_sA = k.sb("bc_sA", [128, 2048], F32, es=esq)
        bc_oA = k.sb("bc_oA", [128, 2048], F32, es=esq)
        bc_sB = k.sb("bc_sB", [128, 2048], F32, es=esq)
        bc_oB = k.sb("bc_oB", [128, 2048], F32, es=esq)
        b_bcqk = Buf("bcqk")
        rowtmp = k.sb("rowtmp", [16, 128], F32, es=esq)
        b_rowtmp = Buf("rowtmp")
        rowbuf = k.sb("rowbuf", [1, 2048], F32, es=esq)
        b_rowbuf = Buf("rowbuf")
        rowdram = nc.dram_tensor("rowdram", [16, 128], F32, kind="Internal").ap()

        def build_qk_bcast(src_ap, b_src, dst, tag):
            k.op("tensor", lambda tt_, s=src_ap: tt_.transpose(
                ps_f[0][0:16, 0:128], s, ident_f[:]),
                reads=[b_src, b_identf], writes=[b_psf[0]], tag=f"tp_{tag}")
            k.op("vector", lambda v: v.tensor_copy(rowtmp[:], ps_f[0][0:16, 0:128]),
                 reads=[b_psf[0]], writes=[b_rowtmp])
            k.dma("sync", rowdram, rowtmp[:], reads=[b_rowtmp], writes=[b_rowtmp],
                  tag=f"r1_{tag}")
            k.dma("sync", rowbuf[:], rowdram.rearrange("p n -> (p n)")[None, :],
                  reads=[b_rowtmp], writes=[b_rowbuf], tag=f"r2_{tag}")
            for i in range(4):
                k.op("tensor", lambda tt_, i=i: tt_.matmul(
                    ps_f[i][:], ones_row_f[:],
                    rowbuf[:, 512 * i:512 * (i + 1)], start=True, stop=True),
                    reads=[b_ones, b_rowbuf], writes=[b_psf[i]], tag=f"bc_{tag}")
            for i in range(4):
                k.op("vector", lambda v, i=i, dst=dst: v.tensor_copy(
                    dst[:, 512 * i:512 * (i + 1)], ps_f[i][:]),
                    reads=[b_psf[i]], writes=[b_bcqk], tag=f"bcc_{tag}")

        import os
        if os.environ.get("NO_BCAST") != "1":
            build_qk_bcast(sAs[:], b_qks, bc_sA, "sA")
            build_qk_bcast(oAs[:], b_qks, bc_oA, "oA")
            build_qk_bcast(qksc_r[:, 1], b_qk, bc_sB, "sB")
            build_qk_bcast(qkof_r[:, 1], b_qk, bc_oB, "oB")

        stg_i = [0]
        def next_stage():
            i = stg_i[0] % 8
            stg_i[0] += 1
            return st8[i], b_st8[i]

        def silu_sq(bp, ps, t, tag, ri=0):
            sig, b_sig = sigA[ri], b_sigA[ri]
            k.op("scalar", lambda s, ps=ps: s.activation(sig[:], ps[:], AF.Sigmoid),
                 reads=[bp, k.const_buf], writes=[b_sig])
            k.op("vector", lambda v, ps=ps, t=t: v.tensor_tensor(
                qsil[:, 512 * t:512 * (t + 1)], sig[:], ps[:], ALU.mult),
                reads=[b_sig, bp], writes=[b_qsil])
            k.op("scalar", lambda s, t=t, ri=ri: s.activation(
                sqj[ri][:], qsil[:, 512 * t:512 * (t + 1)], AF.Square,
                accum_out=l2st[:, t:t + 1]),
                reads=[b_qsil, k.const_buf], writes=[b_sqj[ri], b_l2st], tag=tag)

        def l2_flush(dst_name, cs, tag):
            k.op("scalar", lambda s: s.activation(
                l2st[:, 16:32], l2st[:, 0:16], AF.Ln),
                reads=[b_l2st, k.const_buf], writes=[b_l2st])
            k.op("scalar", lambda s: s.activation(
                l2st[:, 16:32], l2st[:, 16:32], AF.Exp, scale=k.constv(0.5)),
                reads=[b_l2st, k.const_buf], writes=[b_l2st])
            k.op("vector", lambda v: v.tensor_scalar_add(
                l2st[:, 16:32], l2st[:, 16:32], 1e-5),
                reads=[b_l2st], writes=[b_l2st])
            k.op("vector", lambda v: v.reciprocal(l2st[:, 16:32], l2st[:, 16:32]),
                 reads=[b_l2st], writes=[b_l2st])
            for t in range(TT):
                stg2, bstg2 = next_stage()
                k.op("vector", lambda v, t=t, stg=stg2: v.tensor_scalar_mul(
                    stg[:], qsil[:, 512 * t:512 * (t + 1)], l2st[:, 16 + t:17 + t]),
                    reads=[b_qsil, b_l2st], writes=[bstg2])
                k.dma("sync", sc[dst_name][128 * t:128 * (t + 1), cs], stg2[:],
                      reads=[bstg2], writes=[scb[dst_name]], tag=f"{tag}{t}")

        def affine_store(bp, ps, bcs, bco, dst_ap, dst_buf, cs, tag, ri=0):
            fval, b_fval = fvalA[ri], b_fvalA[ri]
            stg, bstg = next_stage()
            k.op("vector", lambda v, ps=ps, c=cs: v.tensor_tensor(
                fval[:], ps[:], bcs[:, c], ALU.mult),
                reads=[bp, b_bcqk], writes=[b_fval])
            k.op("vector", lambda v, stg=stg, c=cs: v.tensor_add(
                stg[:], fval[:], bco[:, c]),
                reads=[b_fval, b_bcqk], writes=[bstg])
            k.dma("sync", dst_ap, stg[:], reads=[bstg],
                  writes=[dst_buf], tag=tag)

        qkv_blocks = [int(v) for v in os.environ.get(
            "QKV_BLOCKS", "0,1,2,3,4,5,6,7,8,9,10,11").split(",") if v]
        for b in qkv_blocks:
            wi = b % 2
            bw = b_wqb[wi]
            wcur = wqb[wi]
            for d in range(DT):
                k.dma("gpsimd", wcur[:, 512 * d:512 * (d + 1)],
                      I["Wqkv"][128 * d:128 * (d + 1), 512 * b:512 * (b + 1)],
                      writes=[bw], tag=f"wqb{b}_{d}")
            for t in range(TT):
                pi = (b * TT + t) % 4
                bp = b_psf[pi]
                ps = ps_f[pi]
                for d in range(DT):
                    k.op("tensor", lambda tt_, d=d, t=t, w=wcur, ps=ps: tt_.matmul(
                        ps[:], ln0T_tile(d, t), w[:, 512 * d:512 * (d + 1)],
                        start=(d == 0), stop=(d == DT - 1)),
                        reads=[b_ln0T[t], bw], writes=[bp], tag=f"qkv{b}_{t}")
                tr = slice(128 * t, 128 * (t + 1))
                cs = slice(512 * (b % 4), 512 * (b % 4 + 1))
                if os.environ.get("SKIP_EVAC") == "1":
                    continue
                ri = t % 2
                if b < 4:
                    affine_store(bp, ps, bc_sA, bc_oA, sc["aq_tm"][tr, cs], scb["aq_tm"], cs,
                                 f"sp_aq{b}_{t}", ri=ri)
                    silu_sq(bp, ps, t, f"sq_Q{b}_{t}", ri=ri)
                    if t == TT - 1:
                        l2_flush("Q_tm", cs, f"sp_Q{b}_")
                elif b < 8:
                    affine_store(bp, ps, bc_sB, bc_oB, sc["ak_tm"][tr, cs], scb["ak_tm"], cs,
                                 f"sp_ak{b}_{t}", ri=ri)
                    silu_sq(bp, ps, t, f"sq_K{b}_{t}", ri=ri)
                    if t == TT - 1:
                        l2_flush("K", cs, f"sp_K{b}_")
                else:
                    sub = os.environ.get("VSUB", "all")
                    sig, b_sig = sigA[t % 2], b_sigA[t % 2]
                    stg, bstg = next_stage()
                    if sub in ("all", "copy"):
                        k.op("vector", lambda v, ps=ps, stg=stg: v.tensor_copy(
                            stg[:], ps[:]), reads=[bp], writes=[bstg])
                    if sub in ("all", "copy-spill"):
                        k.op("vector", lambda v, ps=ps, stg=stg: v.tensor_copy(
                            stg[:], ps[:]), reads=[bp], writes=[bstg])
                        k.dma("sync", sc["av"][tr, cs], stg[:], reads=[bstg], writes=[scb["av"]],
                              tag=f"sp_av{b}_{t}")
                    if sub == "sig":
                        stg2, bstg2 = next_stage()
                        k.op("scalar", lambda s, ps=ps: s.activation(
                            sig[:], ps[:], AF.Sigmoid),
                            reads=[bp, k.const_buf], writes=[b_sig])
                        k.op("vector", lambda v, ps=ps, stg=stg2: v.tensor_tensor(
                            stg[:], sig[:], ps[:], ALU.mult),
                            reads=[b_sig, bp], writes=[bstg2])
                        k.dma("sync", sc["V"][tr, cs], stg2[:], reads=[bstg2], writes=[scb["V"]],
                              tag=f"sp_V{b}_{t}")
                    if sub == "all":
                        k.dma("sync", sc["av"][tr, cs], stg[:], reads=[bstg], writes=[scb["av"]],
                              tag=f"sp_av{b}_{t}")
                        stg2, bstg2 = next_stage()
                        k.op("scalar", lambda s, ps=ps: s.activation(
                            sig[:], ps[:], AF.Sigmoid),
                            reads=[bp, k.const_buf], writes=[b_sig])
                        k.op("vector", lambda v, ps=ps, stg=stg2: v.tensor_tensor(
                            stg[:], sig[:], ps[:], ALU.mult),
                            reads=[b_sig, bp], writes=[bstg2])
                        k.dma("sync", sc["V"][tr, cs], stg2[:], reads=[bstg2], writes=[scb["V"]],
                              tag=f"sp_V{b}_{t}")
    ln0T_es.close()
    k.barrier()

    if 'p1c' in stages:
     with ExitStack() as esc:
        fmres = k.sb("fmres", [128, DT * 2048], BF16, es=esc)
        b_fm = Buf("fmres")
        tmb = [k.sb(f"tmb{i}", [128, 2048], BF16, es=esc) for i in range(2)]
        b_tmb = [Buf(f"tmb{i}") for i in range(2)]
        for src_name, dst_name in (("aq_tm", "aq"), ("ak_tm", "ak"),
                                   ("Q_tm", "QT"), ("K", "KT"), ("V", "VT")):
            for t in range(TT):
                ti = t % 2
                tcur = tmb[ti]
                k.dma("sync", tcur[:], sc[src_name][128 * t:128 * (t + 1), :],
                      reads=[scb[src_name]], writes=[b_tmb[ti]],
                      tag=f"ld_{src_name}{t}")
                pi = t % 2
                for j in range(DT):
                    k.op("tensor", lambda tt_, j=j, pi=pi, s=tcur: tt_.transpose(
                        ps_tp[pi][:, 128 * j:128 * (j + 1)],
                        s[:, 128 * j:128 * (j + 1)], ident[:]),
                        reads=[b_tmb[ti], b_ident], writes=[b_pstp[pi]],
                        tag=f"tp_{src_name}{t}_{j}")
                dstv = fmres[:].rearrange("p (d c) -> p d c", d=DT)[:, :, 128 * t:128 * (t + 1)]
                srcv = ps_tp[pi][:].rearrange("p (d c) -> p d c", d=DT)
                k.op("vector", lambda v, dstv=dstv, srcv=srcv: v.tensor_copy(dstv, srcv),
                     reads=[b_pstp[pi]], writes=[b_fm], tag=f"ev_{src_name}{t}")
            for j in range(DT):
                k.dma("sync", sc[dst_name][128 * j:128 * (j + 1), :],
                      fmres[:, 2048 * j:2048 * (j + 1)],
                      reads=[b_fm], writes=[scb[dst_name]],
                      tag=f"sp_{dst_name}{j}")
            k.barrier()
    p1ps.close()
    k.barrier()

    if 'p2' in stages:
     with ExitStack() as es2:
        pb = [k.ps(f"pb{i}", [128, 512], F32, es=es2) for i in range(6)]
        bpb = [Buf(f"pb{i}", psum=True) for i in range(6)]
        ps2tp = k.ps("ps2tp", [128, 2048], BF16, es=es2); b_ps2tp = Buf("ps2tp", psum=True)
        _bi = [0]

        def nb():
            i = _bi[0] % 6
            _bi[0] += 1
            return pb[i], bpb[i]

        W = [k.sb(f"Wst{s}", [128, 2048], BF16, es=es2) for s in range(3)]
        b_W = [Buf(f"W{s}") for s in range(3)]
        WT = [k.sb(f"WTst{s}", [128, 2048], BF16, es=es2) for s in range(3)]
        b_WT = [Buf(f"WT{s}") for s in range(3)]
        Mst = [k.sb(f"Mst{s}", [128, 2048], BF16, es=es2) for s in range(3)]
        b_M = [Buf(f"M{s}") for s in range(3)]
        nbuf = k.sb("nbuf", [128, 12], F32, es=es2); b_nbuf = Buf("nbuf")
        qt_h = k.sb("qt_h", [128, 4096], BF16, es=es2); b_qt = Buf("qt")
        kt_h = k.sb("kt_h", [128, 4096], BF16, es=es2); b_kt = Buf("kt")
        vt_h = k.sb("vt_h", [128, 4096], BF16, es=es2); b_vt = Buf("vt")
        k_h = k.sb("k_h", [128, 4096], BF16, es=es2); b_kh = Buf("kh")
        v_h = k.sb("v_h", [128, 4096], BF16, es=es2); b_vh = Buf("vh")
        GHT = k.sb("GHT", [128, 4096], BF16, es=es2); b_GHT = Buf("GHT")
        sgg2 = [k.sb(f"sgg2_{i}", [128, 512], F32, es=es2) for i in range(2)]
        b_sgg2 = [Buf(f"sgg2_{i}") for i in range(2)]
        t1g2 = [k.sb(f"t1g2_{i}", [128, 512], F32, es=es2) for i in range(2)]
        b_t1g2 = [Buf(f"t1g2_{i}") for i in range(2)]
        outT = k.sb("outT", [128, 4096], F32, es=es2); b_outT = Buf("outT")
        wf32v = outT
        b_wf32 = b_outT
        sq2 = outT[:, 2048:2560]
        b_sq2 = b_outT
        wnscr = outT[:, 2560:3072]
        b_wnscr = b_outT
        t_scr = outT[:, 0:2048]
        b_tscr = b_outT
        sqb1_0 = k.sb("sqb1", [128, 1024], BF16, es=es2)
        sqb1 = [sqb1_0, sqb1_0]
        bsq0 = Buf("sqb1", )
        b_sqb1 = [bsq0, bsq0]
        gbaF = [k.sb(f"gbaF{i}", [128, 512], F32, es=es2) for i in range(2)]
        b_gbaF = [Buf(f"gbaF{i}") for i in range(2)]
        hbmF = [k.sb(f"hbmF{i}", [128, 512], F32, es=es2) for i in range(2)]
        b_hbmF = [Buf(f"hbmF{i}") for i in range(2)]
        sFA = [k.sb(f"sF{i}", [128, 512], F32, es=es2) for i in range(2)]
        b_sFA = [Buf(f"sF{i}") for i in range(2)]
        dhF_0 = k.sb("dhF", [128, 512], F32, es=es2)
        dhFA = [dhF_0, dhF_0]
        bdh0 = Buf("dhF")
        b_dhFA = [bdh0, bdh0]
        sgF = k.sb("sgF", [128, 512], F32, es=es2); b_sgF = Buf("sgF")
        t1F = k.sb("t1F", [128, 512], F32, es=es2); b_t1F = Buf("t1F")
        v1F = k.sb("v1F", [128, 512], F32, es=es2); b_v1F = Buf("v1F")
        dgba = k.sb("dgba", [128, 4096], BF16, es=es2); b_dgba = Buf("dgba")
        dhbm = k.sb("dhbm", [128, 4096], BF16, es=es2); b_dhbm = Buf("dhbm")
        H1b = k.sb("H1b", [128, 4096], BF16, es=es2); b_H1 = Buf("H1")
        KL0 = k.sb("KL0", [128, 4096], BF16, es=es2); b_KL0 = Buf("KL0")
        KL2 = k.sb("KL2", [128, 4096], BF16, es=es2); b_KL2 = Buf("KL2")
        Xb = [k.sb(f"Xb{i}", [128, 2048], BF16, es=es2) for i in range(2)]
        b_Xb = [Buf(f"X{i}") for i in range(2)]
        Ab = k.sb("Ab", [128, 2048], BF16, es=es2); b_Ab = Buf("Ab")
        XTb = k.sb("XTb", [128, 2048], BF16, es=es2); b_XTb = Buf("XTb")
        Yb = k.sb("Yb", [128, 2048], BF16, es=es2); b_Yb = Buf("Yb")
        nst2 = k.sb("nst2", [128, 8], F32, es=es2); b_nst2 = Buf("nst2")
        mmb = k.sb("mmb", [128, 1], F32, es=es2); b_mmb = Buf("mmb")
        momb = k.sb("momb", [128, 8], BF16, es=es2); b_momb = Buf("momb")
        rnst = k.sb("rnst", [128, 8], F32, es=es2); b_rnst = Buf("rnst")
        srow_c = k.sb("srow_c", [128, 8], F32, es=es2); b_srowc = Buf("srow_c")
        rowt2 = k.sb("rowt2", [8, 128], F32, es=es2); b_rowt2 = Buf("rowt2")
        scrow = k.sb("scrow", [1, 1024], F32, es=es2); b_scrow = Buf("scrow")
        Btl = k.sb("Btl", [128, 1024], F32, es=es2); b_Btl = Buf("Btl")
        msrow = Btl[0:1, 0:1024]; b_msrow = b_Btl
        frow = msrow; b_frow = b_Btl
        stg2 = [k.sb(f"stg2_{i}", [128, 1024], BF16, es=es2) for i in range(2)]
        b_stg2 = [Buf(f"stg2_{i}") for i in range(2)]
        rowd2 = nc.dram_tensor("rowd2", [8, 128], F32, kind="Internal").ap()
        lrview = lrv[:].rearrange("p (t j) -> p j t", j=20)

        for a_, b_, c_ in NS_COEFFS:
            k.constv(b_)
            k.constv(c_ / b_)

        def tp16_pack(src_sb, b_src, dst_sb, b_dst, tag):
            for r in range(4):
                for j in range(4):
                    k.op("tensor", lambda tt_, r=r, j=j: tt_.transpose(
                        ps2tp[:, (r * 4 + j) * 128:(r * 4 + j) * 128 + 128],
                        src_sb[:, 512 * r + 128 * j: 512 * r + 128 * j + 128],
                        ident[:]),
                        reads=[b_src, b_ident], writes=[b_ps2tp], tag=f"tp_{tag}")
            dstv = dst_sb[:].rearrange("p (j r c) -> p j r c", j=4, r=4)
            dstv = dstv.transpose([0, 2, 1, 3])
            srcv = ps2tp[:].rearrange("p (r j c) -> p r j c", r=4, j=4)
            k.op("vector", lambda v: v.tensor_copy(dstv, srcv),
                 reads=[b_ps2tp], writes=[b_dst], tag=f"ev_{tag}")

        def wtile(sb_, r, j):
            return sb_[:, 512 * r + 128 * j: 512 * r + 128 * j + 128]

        def wblk(sb_, r):
            return sb_[:, 512 * r:512 * (r + 1)]

        for h in range(NH):
            for s, wname in enumerate(("w0", "w1", "w2")):
                for r in range(4):
                    k.dma("gpsimd", wblk(W[s], r),
                          I[wname][h, 128 * r:128 * (r + 1), :],
                          writes=[b_W[s]], tag=f"p2W{s}")
                    k.dma("sync", wblk(wf32v, r),
                          I[wname][h, 128 * r:128 * (r + 1), :],
                          writes=[b_wf32], tag=f"p2Wf{s}")
                for r in range(4):
                    k.op("vector", lambda v, s=s, r=r: v.scalar_tensor_tensor(
                        sq2, wblk(wf32v, r), 1.0, wblk(wf32v, r),
                        ALU.mult, ALU.mult,
                        accum_out=nbuf[:, 4 * s + r:4 * s + r + 1]),
                        reads=[b_wf32], writes=[b_sq2, b_nbuf], tag="p2norm")
            k.op("scalar", lambda s_: s_.activation(nbuf[:], nbuf[:], AF.Ln),
                 reads=[b_nbuf, k.const_buf], writes=[b_nbuf])
            k.op("scalar", lambda s_: s_.activation(
                nbuf[:], nbuf[:], AF.Exp, scale=k.constv(0.5)),
                reads=[b_nbuf, k.const_buf], writes=[b_nbuf])

            for c in range(2):
                cs_ = slice(1024 * c, 1024 * (c + 1))
                for d in range(4):
                    rr = slice(512 * h + 128 * d, 512 * h + 128 * (d + 1))
                    k.dma("sync", qt_h[:, 1024 * d:1024 * (d + 1)],
                          sc["QT"][rr, cs_], reads=[scb["QT"]], writes=[b_qt],
                          tag=f"p2qt{h}{c}")
                    k.dma("sync", kt_h[:, 1024 * d:1024 * (d + 1)],
                          sc["KT"][rr, cs_], reads=[scb["KT"]], writes=[b_kt],
                          tag=f"p2kt{h}{c}")
                    k.dma("sync", vt_h[:, 1024 * d:1024 * (d + 1)],
                          sc["VT"][rr, cs_], reads=[scb["VT"]], writes=[b_vt],
                          tag=f"p2vt{h}{c}")
                for ct in range(8):
                    tr = slice(128 * (8 * c + ct), 128 * (8 * c + ct + 1))
                    hc = slice(512 * h, 512 * (h + 1))
                    k.dma("sync", k_h[:, 512 * ct:512 * (ct + 1)],
                          sc["K"][tr, hc], reads=[scb["K"]], writes=[b_kh],
                          tag=f"p2k{h}{c}")
                    k.dma("sync", v_h[:, 512 * ct:512 * (ct + 1)],
                          sc["V"][tr, hc], reads=[scb["V"]], writes=[b_vh],
                          tag=f"p2v{h}{c}")

                for s in range(3):
                    tp16_pack(W[s], b_W[s], WT[s], b_WT[s], f"WT{h}{c}{s}")

                for ht in range(4):
                    for cc in range(2):
                        pA, bA = nb()
                        pB, bB = nb()
                        for d in range(4):
                            k.op("tensor", lambda tt_, d=d, cc=cc, ht=ht, pA=pA:
                                 tt_.matmul(
                                     pA[:], wtile(WT[0], d, ht),
                                     qt_h[:, 1024 * d + 512 * cc:
                                          1024 * d + 512 * cc + 512],
                                     start=(d == 0), stop=(d == 3)),
                                 reads=[b_WT[0], b_qt], writes=[bA], tag="gp_mm")
                        for d in range(4):
                            k.op("tensor", lambda tt_, d=d, cc=cc, ht=ht, pB=pB:
                                 tt_.matmul(
                                     pB[:], wtile(WT[2], d, ht),
                                     qt_h[:, 1024 * d + 512 * cc:
                                          1024 * d + 512 * cc + 512],
                                     start=(d == 0), stop=(d == 3)),
                                 reads=[b_WT[2], b_qt], writes=[bB], tag="hd_mm")
                        ri = (ht * 2 + cc) % 2
                        k.op("scalar", lambda s_, pA=pA, ri=ri: s_.activation(
                            sgg2[ri][:], pA[:], AF.Sigmoid),
                            reads=[bA, k.const_buf], writes=[b_sgg2[ri]])
                        k.op("vector", lambda v, pA=pA, ri=ri: v.tensor_tensor(
                            t1g2[ri][:], sgg2[ri][:], pA[:], ALU.mult),
                            reads=[b_sgg2[ri], bA], writes=[b_t1g2[ri]])
                        k.op("vector", lambda v, ht=ht, cc=cc, pB=pB, ri=ri:
                             v.tensor_tensor(
                                 GHT[:, 1024 * ht + 512 * cc:
                                     1024 * ht + 512 * cc + 512],
                                 t1g2[ri][:], pB[:], ALU.mult),
                             reads=[b_t1g2[ri], bB], writes=[b_GHT], tag="GHT")

                for ot in range(4):
                    for cc in range(2):
                        pC, bC = nb()
                        for ht in range(4):
                            k.op("tensor", lambda tt_, ht=ht, cc=cc, ot=ot, pC=pC:
                                 tt_.matmul(
                                     pC[:], wtile(WT[1], ht, ot),
                                     GHT[:, 1024 * ht + 512 * cc:
                                         1024 * ht + 512 * cc + 512],
                                     start=(ht == 0), stop=(ht == 3)),
                                 reads=[b_WT[1], b_GHT], writes=[bC], tag="out_mm")
                        k.op("vector", lambda v, ot=ot, cc=cc, pC=pC: v.tensor_copy(
                            outT[:, 1024 * ot + 512 * cc:1024 * ot + 512 * cc + 512],
                            pC[:]),
                            reads=[bC], writes=[b_outT], tag="ev_out")

                for ct in range(8):
                    gt = c * 8 + ct
                    ri = ct % 2
                    p1_, b1_ = nb()
                    p2_, b2_ = nb()
                    p3_, b3_ = nb()
                    for d in range(4):
                        k.op("tensor", lambda tt_, d=d, ct=ct, p1_=p1_: tt_.matmul(
                            p1_[:], kt_h[:, 1024 * d + 128 * ct:
                                         1024 * d + 128 * ct + 128],
                            wblk(WT[0], d), start=(d == 0), stop=(d == 3)),
                            reads=[b_kt, b_WT[0]], writes=[b1_], tag="gba_mm")
                    for d in range(4):
                        k.op("tensor", lambda tt_, d=d, ct=ct, p2_=p2_: tt_.matmul(
                            p2_[:], kt_h[:, 1024 * d + 128 * ct:
                                         1024 * d + 128 * ct + 128],
                            wblk(WT[2], d), start=(d == 0), stop=(d == 3)),
                            reads=[b_kt, b_WT[2]], writes=[b2_], tag="hbm_mm")
                    for o in range(4):
                        k.op("tensor", lambda tt_, o=o, ct=ct, p3_=p3_: tt_.matmul(
                            p3_[:], vt_h[:, 1024 * o + 128 * ct:
                                         1024 * o + 128 * ct + 128],
                            wblk(W[1], o), start=(o == 0), stop=(o == 3)),
                            reads=[b_vt, b_W[1]], writes=[b3_], tag="dh_mm")
                    k.op("scalar", lambda s_, p1_=p1_, ri=ri: s_.activation(
                        sFA[ri][:], p1_[:], AF.Sigmoid),
                        reads=[b1_, k.const_buf], writes=[b_sFA[ri]])
                    k.op("vector", lambda v, p1_=p1_, ri=ri: v.tensor_copy(
                        gbaF[ri][:], p1_[:]), reads=[b1_], writes=[b_gbaF[ri]])
                    k.op("vector", lambda v, p2_=p2_, ri=ri: v.tensor_copy(
                        hbmF[ri][:], p2_[:]), reads=[b2_], writes=[b_hbmF[ri]])
                    k.op("vector", lambda v, p3_=p3_, ri=ri: v.tensor_copy(
                        dhFA[ri][:], p3_[:]), reads=[b3_], writes=[b_dhFA[ri]])
                    k.op("vector", lambda v, ri=ri: v.tensor_tensor(
                        sgF[:], sFA[ri][:], gbaF[ri][:], ALU.mult),
                        reads=[b_sFA[ri], b_gbaF[ri]], writes=[b_sgF])
                    k.op("gpsimd", lambda g, ct=ct, ri=ri: g.tensor_tensor(
                        dhbm[:, 512 * ct:512 * (ct + 1)], dhFA[ri][:], sgF[:],
                        ALU.mult),
                        reads=[b_dhFA[ri], b_sgF], writes=[b_dhbm], tag="dhbm")
                    k.op("vector", lambda v, ri=ri: v.tensor_tensor(
                        t1F[:], dhFA[ri][:], hbmF[ri][:], ALU.mult),
                        reads=[b_dhFA[ri], b_hbmF[ri]], writes=[b_t1F])
                    k.op("vector", lambda v, ri=ri: v.scalar_tensor_tensor(
                        v1F[:], gbaF[ri][:], 1.0, sgF[:], ALU.add, ALU.subtract),
                        reads=[b_gbaF[ri], b_sgF], writes=[b_v1F])
                    k.op("vector", lambda v, ri=ri: v.tensor_tensor(
                        t1F[:], t1F[:], sFA[ri][:], ALU.mult),
                        reads=[b_t1F, b_sFA[ri]], writes=[b_t1F])
                    k.op("vector", lambda v, ct=ct: v.tensor_tensor(
                        dgba[:, 512 * ct:512 * (ct + 1)], t1F[:], v1F[:], ALU.mult),
                        reads=[b_t1F, b_v1F], writes=[b_dgba], tag="dgba")
                    l0ap = lrview[:, 3 * h + 0, gt:gt + 1]
                    l1ap = lrview[:, 3 * h + 1, gt:gt + 1]
                    l2ap = lrview[:, 3 * h + 2, gt:gt + 1]
                    k.op("vector", lambda v, ct=ct, l1ap=l1ap, ri=ri:
                         v.scalar_tensor_tensor(
                             H1b[:, 512 * ct:512 * (ct + 1)], sgF[:], l1ap,
                             hbmF[ri][:], ALU.mult, ALU.mult),
                         reads=[b_sgF, b_hbmF[ri], *b_lrv], writes=[b_H1], tag="H1")
                    k.op("vector", lambda v, ct=ct, l0ap=l0ap: v.tensor_scalar_mul(
                        KL0[:, 512 * ct:512 * (ct + 1)],
                        k_h[:, 512 * ct:512 * (ct + 1)], l0ap),
                        reads=[b_kh, *b_lrv], writes=[b_KL0], tag="KL0")
                    k.op("vector", lambda v, ct=ct, l2ap=l2ap: v.tensor_scalar_mul(
                        KL2[:, 512 * ct:512 * (ct + 1)],
                        k_h[:, 512 * ct:512 * (ct + 1)], l2ap),
                        reads=[b_kh, *b_lrv], writes=[b_KL2], tag="KL2")

                pM0, bM0 = nb()
                pM1, bM1 = nb()
                for ot in range(4):
                    sqi = ot % 2
                    k.op("scalar", lambda s_, ot=ot, sqi=sqi: s_.activation(
                        sqb1[sqi][:], outT[:, 1024 * ot:1024 * (ot + 1)], AF.Square),
                        reads=[b_outT, k.const_buf], writes=[b_sqb1[sqi]],
                        tag="ep_sq")
                    for cc in range(2):
                        pM, bM = (pM0, bM0) if cc == 0 else (pM1, bM1)
                        k.op("tensor", lambda tt_, ot=ot, cc=cc, pM=pM, sqi=sqi:
                             tt_.matmul(
                                 pM[0:1, :], ones_col[:],
                                 sqb1[sqi][:, 512 * cc:512 * cc + 512],
                                 start=(ot == 0), stop=(ot == 3)),
                             reads=[b_ones, b_sqb1[sqi]], writes=[bM], tag="ep_ms")
                k.op("vector", lambda v, pM0=pM0: v.tensor_copy(
                    msrow[:, 0:512], pM0[0:1, :]), reads=[bM0], writes=[b_msrow])
                k.op("vector", lambda v, pM1=pM1: v.tensor_copy(
                    msrow[:, 512:1024], pM1[0:1, :]), reads=[bM1], writes=[b_msrow])
                k.op("scalar", lambda s_: s_.activation(
                    msrow[:], msrow[:], AF.Ln, bias=k.constv(1e-5),
                    scale=k.constv(1.0 / 512.0)),
                    reads=[b_msrow, k.const_buf], writes=[b_msrow])
                k.op("scalar", lambda s_: s_.activation(
                    msrow[:], msrow[:], AF.Exp, scale=k.constv(-0.5)),
                    reads=[b_msrow, k.const_buf], writes=[b_msrow])
                k.op("vector", lambda v, h=h, c=c: v.tensor_copy(
                    srow_c[:], lrview[:, 16 + h, 8 * c:8 * c + 8]),
                    reads=[*b_lrv], writes=[b_srowc], tag="ep_sc")
                pT, bT = nb()
                k.op("tensor", lambda tt_, pT=pT: tt_.transpose(
                    pT[0:8, 0:128], srow_c[:], ident_f[:]),
                    reads=[b_srowc, b_identf], writes=[bT], tag="ep_tp")
                k.op("vector", lambda v, pT=pT: v.tensor_copy(
                    rowt2[:], pT[0:8, 0:128]), reads=[bT], writes=[b_rowt2])
                k.dma("sync", rowd2, rowt2[:], reads=[b_rowt2], writes=[b_rowt2],
                      tag="ep_r1")
                k.dma("sync", scrow[:], rowd2.rearrange("p n -> (p n)")[None, :],
                      reads=[b_rowt2], writes=[b_scrow], tag="ep_r2")
                k.op("vector", lambda v: v.tensor_tensor(
                    frow[:], msrow[:], scrow[:], ALU.mult),
                    reads=[b_msrow, b_scrow], writes=[b_frow])
                pB0, bB0 = nb()
                pB1, bB1 = nb()
                for cc in range(2):
                    pX, bX = (pB0, bB0) if cc == 0 else (pB1, bB1)
                    k.op("tensor", lambda tt_, cc=cc, pX=pX: tt_.matmul(
                        pX[:], ones_row_f[:],
                        frow[:, 512 * cc:512 * cc + 512], start=True, stop=True),
                        reads=[b_ones, b_frow], writes=[bX], tag="ep_bc")
                k.op("vector", lambda v, pB0=pB0: v.tensor_copy(
                    Btl[:, 0:512], pB0[:]), reads=[bB0], writes=[b_Btl])
                k.op("vector", lambda v, pB1=pB1: v.tensor_copy(
                    Btl[:, 512:1024], pB1[:]), reads=[bB1], writes=[b_Btl])
                for ot in range(4):
                    sg_i = ot % 2
                    k.op("vector", lambda v, ot=ot, sg_i=sg_i: v.scalar_tensor_tensor(
                        stg2[sg_i][:], outT[:, 1024 * ot:1024 * (ot + 1)],
                        onorm[:, ot:ot + 1], Btl[:], ALU.mult, ALU.mult),
                        reads=[b_outT, b_onorm, b_Btl], writes=[b_stg2[sg_i]],
                        tag="ep_apply")
                    k.dma("sync",
                          sc["ttt"][512 * h + 128 * ot:512 * h + 128 * (ot + 1), cs_],
                          stg2[sg_i][:], reads=[b_stg2[sg_i]],
                          writes=[scb["ttt"]], tag="ep_spill")

                if c == 1:
                    k.op("vector", lambda v, h=h, c=c: v.tensor_copy(
                        momb[:], lrview[:, 12 + h, 8 * c:8 * c + 8]),
                        reads=[*b_lrv], writes=[b_momb])
                    pm, bm_ = nb()
                    k.op("tensor", lambda tt_, pm=pm: tt_.matmul(
                        pm[0:1, 0:8], ones_col[:], momb[:], start=True, stop=True),
                        reads=[b_ones, b_momb], writes=[bm_], tag="mom_mm")
                    k.op("vector", lambda v, pm=pm: v.reduce_sum(
                        nst2[0:1, 6:7], pm[0:1, 0:8], axis=AX.X),
                        reads=[bm_], writes=[b_nst2])
                    k.op("vector", lambda v: v.tensor_scalar_mul(
                        nst2[0:1, 7:8], nst2[0:1, 6:7], 1.0 / CH),
                        reads=[b_nst2], writes=[b_nst2])
                    pm2, bm2 = nb()
                    k.op("tensor", lambda tt_, pm2=pm2: tt_.matmul(
                        pm2[:, 0:1], ones_row_f[:], nst2[0:1, 7:8],
                        start=True, stop=True),
                        reads=[b_ones, b_nst2], writes=[bm2], tag="mom_bc")
                    k.op("vector", lambda v, pm2=pm2: v.tensor_copy(
                        mmb[:], pm2[:, 0:1]), reads=[bm2], writes=[b_mmb])

                for s, (lh, bl, rh, br) in enumerate((
                        (dgba, b_dgba, KL0, b_KL0),
                        (v_h, b_vh, H1b, b_H1),
                        (dhbm, b_dhbm, KL2, b_KL2))):
                    for ht in range(4):
                        pD, bD = nb()
                        for ct in range(8):
                            k.op("tensor", lambda tt_, lh=lh, rh=rh, ht=ht, ct=ct,
                                 pD=pD: tt_.matmul(
                                     pD[:], lh[:, 512 * ct + 128 * ht:
                                               512 * ct + 128 * ht + 128],
                                     rh[:, 512 * ct:512 * (ct + 1)],
                                     start=(ct == 0), stop=(ct == 7)),
                                 reads=[bl, br], writes=[bD], tag=f"dw{s}_mm")
                        if c == 0:
                            k.op("vector", lambda v, s=s, ht=ht, pD=pD: v.tensor_copy(
                                wblk(Mst[s], ht), pD[:]),
                                reads=[bD], writes=[b_M[s]], tag=f"dw{s}_ev")
                        else:
                            k.op("vector", lambda v, s=s, ht=ht, pD=pD:
                                 v.scalar_tensor_tensor(
                                     wblk(Mst[s], ht), wblk(Mst[s], ht), mmb[:],
                                     pD[:], ALU.mult, ALU.add),
                                 reads=[bD, b_mmb, b_M[s]], writes=[b_M[s]],
                                 tag=f"dw{s}_ev")

                for s in range(3):
                    G, bG = Mst[s], b_M[s]
                    k.op("vector", lambda v, G=G: v.scalar_tensor_tensor(
                        GHT[:, 0:2048], G[:], 1.0, G[:], ALU.mult, ALU.mult,
                        accum_out=nst2[:, 0:1]),
                        reads=[bG], writes=[b_GHT, b_nst2], tag="ns_frob")
                    pF, bF = nb()
                    k.op("tensor", lambda tt_, pF=pF: tt_.matmul(
                        pF[0:1, 0:1], ones_col_f[:], nst2[:, 0:1],
                        start=True, stop=True),
                        reads=[b_ones, b_nst2], writes=[bF], tag="ns_fsum")
                    k.op("vector", lambda v, pF=pF: v.tensor_copy(
                        nst2[0:1, 1:2], pF[0:1, 0:1]), reads=[bF], writes=[b_nst2])
                    k.op("scalar", lambda s_: s_.activation(
                        nst2[0:1, 2:3], nst2[0:1, 1:2], AF.Ln),
                        reads=[b_nst2, k.const_buf], writes=[b_nst2])
                    k.op("scalar", lambda s_: s_.activation(
                        nst2[0:1, 3:4], nst2[0:1, 2:3], AF.Exp, scale=k.constv(0.5)),
                        reads=[b_nst2, k.const_buf], writes=[b_nst2])
                    k.op("vector", lambda v: v.tensor_scalar_add(
                        nst2[0:1, 4:5], nst2[0:1, 3:4], 1e-7),
                        reads=[b_nst2], writes=[b_nst2])
                    k.op("vector", lambda v: v.reciprocal(
                        nst2[0:1, 5:6], nst2[0:1, 4:5]),
                        reads=[b_nst2], writes=[b_nst2])
                    pF2, bF2 = nb()
                    k.op("tensor", lambda tt_, pF2=pF2: tt_.matmul(
                        pF2[:, 0:1], ones_row_f[:], nst2[0:1, 5:6],
                        start=True, stop=True),
                        reads=[b_ones, b_nst2], writes=[bF2], tag="ns_fbc")
                    k.op("vector", lambda v, pF2=pF2: v.tensor_copy(
                        nst2[:, 6:7], pF2[:, 0:1]), reads=[bF2], writes=[b_nst2])
                    xcur, bx = Xb[0], b_Xb[0]
                    k.op("vector", lambda v, G=G, xcur=xcur: v.tensor_scalar_mul(
                        xcur[:], G[:], nst2[:, 6:7]),
                        reads=[bG, b_nst2], writes=[bx], tag="ns_x0")
                    for it, (a_, b_, c_) in enumerate(NS_COEFFS):
                        xn, bxn = Xb[(it + 1) % 2], b_Xb[(it + 1) % 2]
                        tp16_pack(xcur, bx, XTb, b_XTb, f"XT{s}{it}")
                        abanks = [nb(), nb()]
                        ybanks = [nb() for _ in range(4)]

                        def emit_A(j):
                            pA2, bA2 = abanks[j % 2]
                            for kk in range(4):
                                k.op("tensor", lambda tt_, j=j, kk=kk, pA2=pA2:
                                     tt_.matmul(
                                         pA2[:], wtile(XTb, kk, j), wblk(XTb, kk),
                                         start=(kk == 0), stop=(kk == 3)),
                                     reads=[b_XTb], writes=[bA2], tag="ns_A")

                        def evac_A(j):
                            pA2, bA2 = abanks[j % 2]
                            k.op("scalar", lambda s_, j=j, pA2=pA2: s_.copy(
                                wblk(Ab, j), pA2[:]),
                                reads=[bA2, k.const_buf], writes=[b_Ab],
                                tag="ns_evA")

                        def emit_Y(j):
                            for i in range(4):
                                pY, bY = ybanks[i]
                                k.op("tensor", lambda tt_, i=i, j=j, pY=pY,
                                     xcur=xcur: tt_.matmul(
                                         pY[:], wtile(Ab, j, i), wblk(xcur, j),
                                         start=(j == 0), stop=(j == 3)),
                                     reads=[b_Ab, bx], writes=[bY], tag="ns_Y")

                        emit_A(0)
                        emit_A(1)
                        evac_A(0)
                        emit_A(2)
                        emit_Y(0)
                        evac_A(1)
                        emit_A(3)
                        emit_Y(1)
                        evac_A(2)
                        emit_Y(2)
                        evac_A(3)
                        emit_Y(3)
                        for i in range(4):
                            pY, bY = ybanks[i]
                            k.op("scalar", lambda s_, i=i, pY=pY, b_=b_: s_.mul(
                                wblk(Yb, i), pY[:], k.constv(b_)),
                                reads=[bY, k.const_buf], writes=[b_Yb],
                                tag="ns_yh")
                        for i in range(4):
                            k.op("vector", lambda v, i=i, a_=a_, xcur=xcur:
                                 v.scalar_tensor_tensor(
                                     t_scr[:, 512 * i:512 * (i + 1)],
                                     wblk(xcur, i), a_, wblk(Yb, i),
                                     ALU.mult, ALU.add),
                                 reads=[bx, b_Yb], writes=[b_tscr], tag="ns_t")
                        for j in range(4):
                            for i in range(4):
                                pY, bY = ybanks[i]
                                k.op("tensor", lambda tt_, i=i, j=j, pY=pY:
                                     tt_.matmul(
                                         pY[:], wtile(Ab, j, i), wblk(Yb, j),
                                         start=(j == 0), stop=(j == 3)),
                                     reads=[b_Ab, b_Yb], writes=[bY], tag="ns_Z")
                        for i in range(4):
                            pY, bY = ybanks[i]
                            k.op("vector", lambda v, i=i, pY=pY, cb=c_ / b_, xn=xn:
                                 v.scalar_tensor_tensor(
                                     wblk(xn, i), t_scr[:, 512 * i:512 * (i + 1)],
                                     1.0, pY[:], ALU.mult, ALU.add,
                                     ) if False else v.scalar_tensor_tensor(
                                     wblk(xn, i), pY[:], cb,
                                     t_scr[:, 512 * i:512 * (i + 1)],
                                     ALU.mult, ALU.add),
                                 reads=[bY, b_tscr], writes=[bxn], tag="ns_xn")
                        xcur, bx = xn, bxn
                    for r in range(4):
                        k.op("vector", lambda v, s=s, r=r, xcur=xcur:
                             v.scalar_tensor_tensor(
                                 wnscr, wblk(W[s], r), 1.0, wblk(xcur, r),
                                 ALU.mult, ALU.add),
                             reads=[b_W[s], bx], writes=[b_wnscr], tag="rn_add")
                        k.op("vector", lambda v, r=r: v.scalar_tensor_tensor(
                            sq2, wnscr, 1.0, wnscr, ALU.mult, ALU.mult,
                            accum_out=rnst[:, r:r + 1]),
                            reads=[b_wnscr], writes=[b_sq2, b_rnst], tag="rn_sq")
                        k.op("scalar", lambda s_, r=r: s_.activation(
                            rnst[:, 4 + r:5 + r], rnst[:, r:r + 1], AF.Ln),
                            reads=[b_rnst, k.const_buf], writes=[b_rnst])
                        k.op("scalar", lambda s_, r=r: s_.activation(
                            rnst[:, 4 + r:5 + r], rnst[:, 4 + r:5 + r], AF.Exp,
                            scale=k.constv(0.5)),
                            reads=[b_rnst, k.const_buf], writes=[b_rnst])
                        k.op("vector", lambda v, r=r: v.tensor_scalar_add(
                            rnst[:, 4 + r:5 + r], rnst[:, 4 + r:5 + r], 1e-5),
                            reads=[b_rnst], writes=[b_rnst])
                        k.op("vector", lambda v, r=r: v.reciprocal(
                            rnst[:, 4 + r:5 + r], rnst[:, 4 + r:5 + r]),
                            reads=[b_rnst], writes=[b_rnst])
                        k.op("vector", lambda v, s=s, r=r: v.tensor_tensor(
                            rnst[:, 4 + r:5 + r], rnst[:, 4 + r:5 + r],
                            nbuf[:, 4 * s + r:4 * s + r + 1], ALU.mult),
                            reads=[b_rnst, b_nbuf], writes=[b_rnst])
                        k.op("vector", lambda v, s=s, r=r: v.tensor_scalar_mul(
                            wblk(W[s], r), wnscr, rnst[:, 4 + r:5 + r]),
                            reads=[b_wnscr, b_rnst], writes=[b_W[s]], tag="rn_app")

    k.barrier()

    if 'p3' in stages:
     with ExitStack() as es34:
        sum_fm = k.sb("sum_fm", [128, 16 * 2048], BF16, es=es34)
        b_sum = Buf("sum_fm")
        with ExitStack() as es3:
            ps_sA = [k.ps(f"ps_s{i}", [128, 1536], F32, es=es3) for i in range(2)]
            b_pssA = [Buf(f"pss{i}", psum=True) for i in range(2)]
            ps_et = k.ps("ps_et", [128, 2048], BF16, es=es3); b_pset = Buf("pset", psum=True)
            ps_pv = ps_sA[0][:, 0:1024]; b_pspv = b_pssA[0]
            ps_x = ps_sA[1][:, 0:512]; b_psx = b_pssA[1]

            aq_hA = [k.sb(f"aq_h{i}", [128, 1024], BF16, es=es3) for i in range(2)]
            b_aqhA = [Buf(f"aqh{i}") for i in range(2)]
            ak_hA = [k.sb(f"ak_h{i}", [128, 2048], BF16, es=es3) for i in range(2)]
            b_akhA = [Buf(f"akh{i}") for i in range(2)]
            av_hA = [k.sb(f"av_h{i}", [128, 2048], BF16, es=es3) for i in range(2)]
            b_avhA = [Buf(f"avh{i}") for i in range(2)]
            ttt_hA = [k.sb(f"ttt_h{i}", [128, 1024], BF16, es=es3) for i in range(2)]
            b_tthA = [Buf(f"ttth{i}") for i in range(2)]
            E2 = [k.sb(f"E2_{i}", [128, 1152], BF16, es=es3) for i in range(2)]
            b_E2 = [Buf(f"E2_{i}") for i in range(2)]
            ETb = k.sb("ETb", [128, 16384], BF16, es=es3); b_ET = Buf("ET")
            linv8 = k.sb("linv8", [128, 8], F32, es=es3); b_linv = Buf("linv")
            negm = k.sb("negm", [128, 2], F32, es=es3); b_negm = Buf("negm")
            rowt3 = k.sb("rowt3", [8, 128], F32, es=es3); b_rowt3 = Buf("rowt3")
            lrow = k.sb("lrow", [1, 1024], F32, es=es3); b_lrow = Buf("lrow")
            Bsb = k.sb("Bsb", [128, 1024], F32, es=es3); b_Bsb = Buf("Bsb")
            tF3 = k.sb("tF3", [128, 1024], F32, es=es3); b_tF3 = Buf("tF3")
            Mlo = k.sb("Mlo", [128, 128], F32, es=es3); b_Mlo = Buf("Mlo")
            Mhi = k.sb("Mhi", [128, 128], F32, es=es3); b_Mhi = Buf("Mhi")
            rowd3 = nc.dram_tensor("rowd3", [8, 128], F32, kind="Internal").ap()

            k.op("gpsimd", lambda g: g.memset(Mlo[:], 0.0), writes=[b_Mlo])
            k.op("gpsimd", lambda g: g.affine_select(
                Mlo[:], Mlo[:], [[1, 128]], ALU.is_ge, -30000.0,
                base=0, channel_multiplier=-1), reads=[b_Mlo], writes=[b_Mlo])
            k.op("gpsimd", lambda g: g.memset(Mhi[:], 0.0), writes=[b_Mhi])
            k.op("gpsimd", lambda g: g.affine_select(
                Mhi[:], Mhi[:], [[-1, 128]], ALU.is_ge, -30000.0,
                base=0, channel_multiplier=1), reads=[b_Mhi], writes=[b_Mhi])
            k.op("vector", lambda v: v.memset(ETb[:], 0.0), writes=[b_ET])

            ETv = ETb[:].rearrange("p (kt q) -> p kt q", kt=16)

            for blk in range(2):
                if blk == 1:
                    for kt in range(8):
                        for qt in range(kt + 1, 8):
                            k.op("vector", lambda v, kt=kt, qt=qt: v.memset(
                                ETv[:, kt, 128 * qt:128 * (qt + 1)], 0.0),
                                writes=[b_ET], tag="et_clr")
                nk_blk = 8 if blk == 0 else 16
                for h in range(NAH):
                    hi_ = (blk * NAH + h) % 2
                    aq_h, b_aqh = aq_hA[hi_], b_aqhA[hi_]
                    ak_h, b_akh = ak_hA[hi_], b_akhA[hi_]
                    av_h, b_avh = av_hA[hi_], b_avhA[hi_]
                    ttt_h, b_ttth = ttt_hA[hi_], b_tthA[hi_]
                    hr = slice(128 * h, 128 * (h + 1))
                    k.dma("sync", aq_h[:], sc["aq"][hr, 1024 * blk:1024 * (blk + 1)],
                          reads=[scb["aq"]], writes=[b_aqh], tag=f"a_q{blk}{h}")
                    k.dma("sync", ak_h[:, 0:1024 * (blk + 1)],
                          sc["ak"][hr, 0:1024 * (blk + 1)],
                          reads=[scb["ak"]], writes=[b_akh], tag=f"a_k{blk}{h}")
                    for kt in range(nk_blk):
                        k.dma("sync", av_h[:, 128 * kt:128 * (kt + 1)],
                              sc["av"][128 * kt:128 * (kt + 1), 128 * h:128 * (h + 1)],
                              reads=[scb["av"]], writes=[b_avh], tag=f"a_v{blk}{h}")
                    k.dma("scalar", ttt_h[:], sc["ttt"][hr, 1024 * blk:1024 * (blk + 1)],
                          reads=[scb["ttt"]], writes=[b_ttth], tag=f"a_t{blk}{h}")

                    for qt in range(8):
                        ps_s = ps_sA[qt % 2]
                        b_pss = b_pssA[qt % 2]
                        w0 = 128 * qt if blk == 1 else 0
                        w = 1152 if blk == 1 else 128 * (qt + 1)
                        nk = w // 128
                        ei = qt % 2
                        off = 0
                        first = True
                        while off < w:
                            n = min(512, w - off)
                            k.op("tensor", lambda tt_, off=off, n=n, qt=qt, w0=w0,
                                 ps_s=ps_s, aq_h=aq_h, ak_h=ak_h,
                                 first=first: tt_.matmul(
                                     ps_s[:, off:off + n],
                                     aq_h[:, 128 * qt:128 * (qt + 1)],
                                     ak_h[:, w0 + off:w0 + off + n],
                                     start=True, stop=True),
                                 reads=[b_aqh, b_akh], writes=[b_pss], tag="s_mm")
                            off += n
                            first = False
                        if blk == 1:
                            k.op("vector", lambda v, ps_s=ps_s: v.tensor_tensor(
                                ps_s[:, 0:128], ps_s[:, 0:128], Mlo[:], ALU.add),
                                reads=[b_Mlo, b_pss], writes=[b_pss], tag="m_lo")
                            k.op("vector", lambda v, ps_s=ps_s: v.tensor_tensor(
                                ps_s[:, 1024:1152], ps_s[:, 1024:1152], Mhi[:],
                                ALU.add),
                                reads=[b_Mhi, b_pss], writes=[b_pss], tag="m_hi")
                        else:
                            k.op("vector", lambda v, w=w, ps_s=ps_s: v.tensor_tensor(
                                ps_s[:, w - 128:w], ps_s[:, w - 128:w], Mhi[:],
                                ALU.add),
                                reads=[b_Mhi, b_pss], writes=[b_pss], tag="m_hi0")
                        k.op("vector", lambda v, w=w, ps_s=ps_s, ei=ei: v.reduce_max(
                            negm[:, ei:ei + 1], ps_s[:, 0:w], axis=AX.X, negate=True),
                            reads=[b_pss], writes=[b_negm])
                        k.op("scalar", lambda s_, w=w, ei=ei, qt=qt, ps_s=ps_s:
                             s_.activation(
                            E2[ei][:, 0:w], ps_s[:, 0:w], AF.Exp, bias=negm[:, ei:ei + 1],
                            accum_out=linv8[:, qt:qt + 1]),
                            reads=[b_pss, b_negm, k.const_buf],
                            writes=[b_E2[ei], b_linv], tag="exp")
                        k.op("vector", lambda v, qt=qt: v.reciprocal(
                            linv8[:, qt:qt + 1], linv8[:, qt:qt + 1]),
                            reads=[b_linv], writes=[b_linv])
                        for j in range(nk):
                            k.op("tensor", lambda tt_, j=j, ei=ei: tt_.transpose(
                                ps_et[:, 128 * j:128 * (j + 1)],
                                E2[ei][:, 128 * j:128 * (j + 1)], ident[:]),
                                reads=[b_E2[ei], b_ident], writes=[b_pset],
                                tag="e_tp")
                        ka0 = w0 // 128
                        dstv = ETv[:, ka0:ka0 + nk, 128 * qt:128 * (qt + 1)]
                        srcv = ps_et[:, 0:128 * nk].rearrange(
                            "p (n c) -> p n c", n=nk)
                        k.op("vector", lambda v, dstv=dstv, srcv=srcv: v.tensor_copy(
                            dstv, srcv),
                            reads=[b_pset], writes=[b_ET], tag="e_ev")

                    for kt in range(nk_blk):
                        for qg in range(2):
                            k.op("tensor", lambda tt_, kt=kt, qg=qg, nk_blk=nk_blk,
                                 av_h=av_h: tt_.matmul(
                                ps_pv[:, 512 * qg:512 * (qg + 1)],
                                av_h[:, 128 * kt:128 * (kt + 1)],
                                ETv[:, kt, 512 * qg:512 * (qg + 1)],
                                start=(kt == 0), stop=(kt == nk_blk - 1)),
                                reads=[b_avh, b_ET], writes=[b_pspv], tag="pv_mm")
                    k.op("tensor", lambda tt_: tt_.transpose(
                        ps_x[0:8, 0:128], linv8[:], ident_f[:]),
                        reads=[b_linv, b_identf], writes=[b_psx], tag="l_tp")
                    k.op("vector", lambda v: v.tensor_copy(rowt3[:], ps_x[0:8, 0:128]),
                         reads=[b_psx], writes=[b_rowt3])
                    k.dma("sync", rowd3, rowt3[:], reads=[b_rowt3], writes=[b_rowt3],
                          tag="l_r1")
                    k.dma("sync", lrow[:], rowd3.rearrange("p n -> (p n)")[None, :],
                          reads=[b_rowt3], writes=[b_lrow], tag="l_r2")
                    for qg in range(2):
                        k.op("tensor", lambda tt_, qg=qg: tt_.matmul(
                            ps_sA[1][:, 512 * qg:512 * (qg + 1)], ones_row_f[:],
                            lrow[:, 512 * qg:512 * (qg + 1)], start=True, stop=True),
                            reads=[b_ones, b_lrow], writes=[b_pssA[1]], tag="l_bc")
                    k.op("vector", lambda v: v.tensor_copy(Bsb[:], ps_sA[1][:, 0:1024]),
                         reads=[b_pssA[1]], writes=[b_Bsb])
                    for qg in range(2):
                        k.op("vector", lambda v, qg=qg: v.tensor_tensor(
                            tF3[:, 512 * qg:512 * (qg + 1)],
                            ps_pv[:, 512 * qg:512 * (qg + 1)],
                            Bsb[:, 512 * qg:512 * (qg + 1)], ALU.mult),
                            reads=[b_pspv, b_Bsb], writes=[b_tF3], tag="pv_ev")
                    k.op("vector", lambda v, h=h, blk=blk, ttt_h=ttt_h: v.tensor_tensor(
                        sum_fm[:, 2048 * h + 1024 * blk:2048 * h + 1024 * (blk + 1)],
                        tF3[:], ttt_h[:], ALU.add),
                        reads=[b_tF3, b_ttth], writes=[b_sum], tag="sum_w")
        k.barrier()

        with ExitStack() as es4:
            ps4 = [k.ps(f"ps4_{i}", [128, 2048], F32, es=es4) for i in range(2)]
            b_ps4 = [Buf(f"ps4_{i}", psum=True) for i in range(2)]
            Wo_sb = k.sb("Wo_sb", [128, 16 * 2048], BF16, es=es4)
            b_Wo = Buf("Wo_sb")
            xb4 = [k.sb(f"xb4_{i}", [128, 2048], F32, es=es4) for i in range(2)]
            b_xb4 = [Buf(f"xb4_{i}") for i in range(2)]
            aos = k.sb("aos", [128, 2048], F32, es=es4); b_aos = Buf("aos")
            x2s = k.sb("x2s", [128, 2048], F32, es=es4); b_x2s = Buf("x2s")
            xfs = k.sb("xfs", [128, 2048], F32, es=es4); b_xfs = Buf("xfs")
            xfb = k.sb("xfb", [128, 2048], BF16, es=es4); b_xfb = Buf("xfb")
            sqv4 = k.sb("sqv4", [128, 2048], F32, es=es4); b_sqv4 = Buf("sqv4")
            st4 = k.sb("st4", [128, 8], F32, es=es4); b_st4 = Buf("st4")

            for d in range(DT):
                k.dma("gpsimd", Wo_sb[:, 2048 * d:2048 * (d + 1)],
                      I["Wo"][128 * d:128 * (d + 1), :], writes=[b_Wo],
                      tag=f"wold{d}")

            for t in [15] + list(range(15)):
                pi = t % 2
                bp4 = b_ps4[pi]
                for d in range(DT):
                    for oc in range(4):
                        k.op("tensor", lambda tt_, d=d, oc=oc, t=t, pi=pi: tt_.matmul(
                            ps4[pi][:, 512 * oc:512 * (oc + 1)],
                            sum_fm[:, 2048 * d + 128 * t:2048 * d + 128 * (t + 1)],
                            Wo_sb[:, 2048 * d + 512 * oc:2048 * d + 512 * (oc + 1)],
                            start=(d == 0), stop=(d == DT - 1)),
                            reads=[b_sum, b_Wo], writes=[bp4], tag="wo_mm")
                xi = t % 2
                k.dma("sync", xb4[xi][:], I["x"][128 * t:128 * (t + 1), :],
                      writes=[b_xb4[xi]], tag=f"x4_{t}")
                k.op("vector", lambda v, pi=pi: v.reduce_sum(
                    st4[:, 0:1], ps4[pi][:], axis=AX.X),
                    reads=[bp4], writes=[b_st4])
                k.op("scalar", lambda s_, pi=pi: s_.activation(
                    sqv4[:], ps4[pi][:], AF.Square, accum_out=st4[:, 1:2]),
                    reads=[bp4, k.const_buf], writes=[b_sqv4, b_st4])
                k.op("vector", lambda v: v.tensor_scalar_mul(
                    st4[:, 2:4], st4[:, 0:2], 1.0 / D),
                    reads=[b_st4], writes=[b_st4])
                k.op("vector", lambda v: v.tensor_tensor(
                    st4[:, 4:5], st4[:, 2:3], st4[:, 2:3], ALU.mult),
                    reads=[b_st4], writes=[b_st4])
                k.op("vector", lambda v: v.tensor_sub(
                    st4[:, 5:6], st4[:, 3:4], st4[:, 4:5]),
                    reads=[b_st4], writes=[b_st4])
                k.op("scalar", lambda s_: s_.activation(
                    st4[:, 6:7], st4[:, 5:6], AF.Ln, bias=k.constv(1e-5)),
                    reads=[b_st4, k.const_buf], writes=[b_st4])
                k.op("scalar", lambda s_: s_.activation(
                    st4[:, 7:8], st4[:, 6:7], AF.Exp, scale=k.constv(-0.5)),
                    reads=[b_st4, k.const_buf], writes=[b_st4])
                k.op("vector", lambda v, pi=pi: v.tensor_scalar(
                    aos[:], ps4[pi][:], st4[:, 2:3], st4[:, 7:8],
                    ALU.subtract, ALU.mult),
                    reads=[bp4, b_st4], writes=[b_aos], tag="wo_ln")
                k.op("vector", lambda v, xi=xi: v.tensor_add(
                    x2s[:], aos[:], xb4[xi][:]),
                    reads=[b_aos, b_xb4[xi]], writes=[b_x2s], tag="x2")
                k.dma("scalar", sc["x2"][128 * t:128 * (t + 1), :], x2s[:],
                      reads=[b_x2s], writes=[scb["x2"]], tag=f"sp_x2_{t}")
                k.op("vector", lambda v: v.reduce_sum(
                    st4[:, 0:1], x2s[:], axis=AX.X),
                    reads=[b_x2s], writes=[b_st4])
                k.op("scalar", lambda s_: s_.activation(
                    sqv4[:], x2s[:], AF.Square, accum_out=st4[:, 1:2]),
                    reads=[b_x2s, k.const_buf], writes=[b_sqv4, b_st4])
                k.op("vector", lambda v: v.tensor_scalar_mul(
                    st4[:, 2:4], st4[:, 0:2], 1.0 / D),
                    reads=[b_st4], writes=[b_st4])
                k.op("vector", lambda v: v.tensor_tensor(
                    st4[:, 4:5], st4[:, 2:3], st4[:, 2:3], ALU.mult),
                    reads=[b_st4], writes=[b_st4])
                k.op("vector", lambda v: v.tensor_sub(
                    st4[:, 5:6], st4[:, 3:4], st4[:, 4:5]),
                    reads=[b_st4], writes=[b_st4])
                k.op("scalar", lambda s_: s_.activation(
                    st4[:, 6:7], st4[:, 5:6], AF.Ln, bias=k.constv(1e-5)),
                    reads=[b_st4, k.const_buf], writes=[b_st4])
                k.op("scalar", lambda s_: s_.activation(
                    st4[:, 7:8], st4[:, 6:7], AF.Exp, scale=k.constv(-0.5)),
                    reads=[b_st4, k.const_buf], writes=[b_st4])
                k.op("vector", lambda v: v.tensor_scalar(
                    xfs[:], x2s[:], st4[:, 2:3], st4[:, 7:8],
                    ALU.subtract, ALU.mult),
                    reads=[b_x2s, b_st4], writes=[b_xfs], tag="xf")
                k.dma("scalar", sc["xf"][128 * t:128 * (t + 1), :], xfs[:],
                      reads=[b_xfs], writes=[scb["xf"]], tag=f"sp_xf_{t}")
                k.op("gpsimd", lambda g: g.tensor_copy(xfb[:], xfs[:]),
                     reads=[b_xfs], writes=[b_xfb], tag="xfb")
                k.dma("gpsimd", sc["xfT"][128 * t:128 * (t + 1), :], xfb[:],
                      reads=[b_xfb], writes=[scb["xfT"]], tag=f"sp_xfb_{t}")
        k.barrier()

    if 'p5' in stages:
     with ExitStack() as es5o:
        Bt5 = k.sb("Bt5", [128, 2048], F32, es=es5o); b_Bt5 = Buf("Bt5")
        Gam6 = k.sb("Gam6", [128, 2048], F32, es=es5o); b_Gam6 = Buf("Gam6")
        Bet6 = k.sb("Bet6", [128, 2048], F32, es=es5o); b_Bet6 = Buf("Bet6")

        with ExitStack() as es5a:
            ps5 = k.ps("ps5", [8, 2048], F32, es=es5a); b_ps5 = Buf("ps5", psum=True)
            ps5t = k.ps("ps5t", [128, 2048], BF16, es=es5a)
            b_ps5t = Buf("ps5t", psum=True)
            psg = k.ps("psg", [128, 1024], F32, es=es5a)
            b_psg = Buf("psg", psum=True)
            gatew = k.sb("gatew", [128, 16 * 2048], BF16, es=es5a)
            b_gw = Buf("gatew")
            d1b2 = [k.sb(f"d1b2_{i}", [128, 1024], BF16, es=es5a) for i in range(2)]
            b_d1b2 = [Buf(f"d1b2_{i}") for i in range(2)]
            d2b2 = [k.sb(f"d2b2_{i}", [128, 2048], BF16, es=es5a) for i in range(2)]
            b_d2b2 = [Buf(f"d2b2_{i}") for i in range(2)]
            aggb_blk = [k.sb(f"aggblk{i}", [128, 512], BF16, es=es5a)
                        for i in range(2)]
            b_aggblk = [Buf(f"aggblk{i}") for i in range(2)]
            rows5 = k.sb("rows5", [1, 2048], F32, es=es5a); b_rows5 = Buf("rows5")
            rows5b = k.sb("rows5b", [1, 2048], F32, es=es5a); b_rows5b = Buf("rows5b")
            st5 = k.sb("st5", [8, 8], F32, es=es5a); b_st5 = Buf("st5")
            throw = k.sb("throw", [8, 2048], F32, es=es5a); b_throw = Buf("throw")
            trow = k.sb("trow", [8, 2048], F32, es=es5a); b_trow = Buf("trow")
            trowb = k.sb("trowb", [8, 2048], BF16, es=es5a); b_trowb = Buf("trowb")
            tT5 = k.sb("tT5", [128, 128], BF16, es=es5a); b_tT5 = Buf("tT5")
            hrow = k.sb("hrow", [8, 1024], F32, es=es5a); b_hrow = Buf("hrow")
            hrowb = k.sb("hrowb", [8, 1024], BF16, es=es5a); b_hrowb = Buf("hrowb")
            hT5 = k.sb("hT5", [128, 64], BF16, es=es5a); b_hT5 = Buf("hT5")
            rsum = k.sb("rsum", [8, 2048], F32, es=es5a); b_rsum = Buf("rsum")
            Gpn = k.sb("Gpn", [8, 2048], BF16, es=es5a); b_Gpn = Buf("Gpn")
            Bpn = k.sb("Bpn", [8, 2048], BF16, es=es5a); b_Bpn = Buf("Bpn")
            D2b = k.sb("D2b", [8, 2048], BF16, es=es5a); b_D2b = Buf("D2b")
            oh5 = k.sb("oh5", [8, 1], F32, es=es5a); b_oh5 = Buf("oh5")
            sel8 = k.sb("sel8", [8, 128], F32, es=es5a); b_sel8 = Buf("sel8")
            brow = k.sb("brow", [1, 5120], BF16, es=es5a); b_brow = Buf("brow")
            grow = k.sb("grow", [1, 2048], BF16, es=es5a); b_grow = Buf("grow")
            TB8 = rsum; b_TB8 = b_rsum
            xl5 = trow[0:1, :]; b_xl5 = b_trow
            xn5 = rsum[0:1, :]; b_xn5 = b_rsum
            xg8 = throw; b_xg8 = b_throw
            xg8b = trowb; b_xg8b = b_trowb
            xftm = [k.sb(f"xftm{i}", [128, 2048], BF16, es=es5a) for i in range(2)]
            b_xftm = [Buf(f"xftm{i}") for i in range(2)]
            stg5A = [k.sb(f"stg5{i}", [128, 2048], BF16, es=es5a) for i in range(2)]
            b_stg5A = [Buf(f"stg5{i}") for i in range(2)]
            gst = [k.sb(f"gst{i}", [128, 1024], BF16, es=es5a) for i in range(2)]
            b_gst = [Buf(f"gst{i}") for i in range(2)]
            gsx0 = k.sb("gsx", [128, 1024], F32, es=es5a)
            gsx = [gsx0, gsx0]
            bgsx0 = Buf("gsx")
            b_gsx = [bgsx0, bgsx0]
            cg_in = nc.dram_tensor("cg_in", [1, 2048], F32, kind="Internal").ap()
            cg_out = nc.dram_tensor("cg_out", [8, 2048], F32, kind="Internal").ap()
            cc_in = nc.dram_tensor("cc_in", [8, 2048], F32, kind="Internal").ap()
            cc_out = nc.dram_tensor("cc_out", [8, 2048], F32, kind="Internal").ap()
            b_cg = Buf("cg"); b_cc = Buf("cc")

            segs = []

            def seg_loads():
                k.dma("gpsimd", brow[:, 0:2048], I["agg_b"][None, :], writes=[b_brow])
                k.dma("gpsimd", brow[:, 2048:3072], I["d1b_sl"][None, :],
                      writes=[b_brow])
                k.dma("gpsimd", brow[:, 3072:5120], I["bc_b"][None, :],
                      writes=[b_brow])
                k.dma("sync", oh5[:], I["onehot"], writes=[b_oh5])
                k.dma("sync", rows5[:], I["pn_gamma"][None, :], writes=[b_rows5])
                k.dma("sync", rows5b[:], I["pn_beta"][None, :], writes=[b_rows5b])

            def bcast8(src_row, b_src, dst, b_dst, tag):
                for i4 in range(4):
                    k.op("tensor", lambda tt_, i4=i4: tt_.matmul(
                        ps5[0:8, 512 * i4:512 * (i4 + 1)], ones_row_f[0:1, 0:8],
                        src_row[:, 512 * i4:512 * (i4 + 1)], start=True, stop=True),
                        reads=[b_ones, b_src], writes=[b_ps5], tag=f"bc8_{tag}")
                k.op("vector", lambda v: v.tensor_copy(dst[:], ps5[0:8, :]),
                     reads=[b_ps5], writes=[b_dst], tag=f"bc8e_{tag}")

            def seg_pre():
                bcast8(rows5, b_rows5, Gpn, b_Gpn, "gpn")
                bcast8(rows5b, b_rows5b, Bpn, b_Bpn, "bpn")
                k.dma("sync", rows5[:], I["d2_b"][None, :], reads=[b_Gpn],
                      writes=[b_rows5], tag="d2brow")
                bcast8(rows5, b_rows5, D2b, b_D2b, "d2b")
                k.op("vector", lambda v: v.memset(sel8[:], 1.0), writes=[b_sel8])
                k.op("vector", lambda v: v.tensor_scalar_mul(
                    sel8[:], sel8[:], oh5[:]), reads=[b_oh5, b_sel8],
                    writes=[b_sel8])
                for nm, dst, b_dst in (("out_gamma", Gam6, b_Gam6),
                                       ("out_beta", Bet6, b_Bet6)):
                    k.dma("sync", rows5b[:], I[nm][None, :],
                          reads=[b_Bpn, b_Bet6], writes=[b_rows5b], tag=nm)
                    for i4 in range(4):
                        pp_, bp_ = (psg[:, 0:512], b_psg) if i4 % 2 == 0 else (
                            psg[:, 512:1024], b_psg)
                        k.op("tensor", lambda tt_, i4=i4, pp_=pp_: tt_.matmul(
                            pp_, ones_row_f[:],
                            rows5b[:, 512 * i4:512 * (i4 + 1)],
                            start=True, stop=True),
                            reads=[b_ones, b_rows5b], writes=[bp_],
                            tag=f"bcg_{nm}")
                        k.op("vector", lambda v, i4=i4, dst=dst, pp_=pp_:
                             v.tensor_copy(dst[:, 512 * i4:512 * (i4 + 1)], pp_),
                             reads=[bp_], writes=[b_dst], tag=f"bcge_{nm}")

            def seg_xn():
                k.dma("sync", xl5[:], sc["xf"][L - 1:L, :], reads=[scb["xf"]],
                      writes=[b_xl5], tag="xl5")
                k.op("vector", lambda v: v.reduce_sum(
                    st5[0:1, 0:1], xl5[:], axis=AX.X), reads=[b_xl5], writes=[b_st5])
                k.op("scalar", lambda s_: s_.activation(
                    xn5[:], xl5[:], AF.Square, accum_out=st5[0:1, 1:2]),
                    reads=[b_xl5, k.const_buf], writes=[b_xn5, b_st5])
                k.op("vector", lambda v: v.tensor_scalar_mul(
                    st5[0:1, 2:4], st5[0:1, 0:2], 1.0 / D),
                    reads=[b_st5], writes=[b_st5])
                k.op("vector", lambda v: v.tensor_tensor(
                    st5[0:1, 4:5], st5[0:1, 2:3], st5[0:1, 2:3], ALU.mult),
                    reads=[b_st5], writes=[b_st5])
                k.op("vector", lambda v: v.tensor_sub(
                    st5[0:1, 5:6], st5[0:1, 3:4], st5[0:1, 4:5]),
                    reads=[b_st5], writes=[b_st5])
                k.op("scalar", lambda s_: s_.activation(
                    st5[0:1, 6:7], st5[0:1, 5:6], AF.Ln, bias=k.constv(1e-6)),
                    reads=[b_st5, k.const_buf], writes=[b_st5])
                k.op("scalar", lambda s_: s_.activation(
                    st5[0:1, 7:8], st5[0:1, 6:7], AF.Exp, scale=k.constv(-0.5)),
                    reads=[b_st5, k.const_buf], writes=[b_st5])
                k.op("vector", lambda v: v.tensor_scalar(
                    xn5[:], xl5[:], st5[0:1, 2:3], st5[0:1, 7:8],
                    ALU.subtract, ALU.mult),
                    reads=[b_xl5, b_st5], writes=[b_xn5])
                k.dma("sync", rows5[:], I["in_gamma"][None, :], reads=[b_D2b],
                      writes=[b_rows5], tag="ing")
                k.dma("sync", rows5b[:], I["in_beta"][None, :],
                      reads=[b_Bet6, b_Gam6], writes=[b_rows5b], tag="inb")
                k.op("vector", lambda v: v.tensor_tensor(
                    xn5[:], xn5[:], rows5[:], ALU.mult),
                    reads=[b_xn5, b_rows5], writes=[b_xn5])
                k.op("vector", lambda v: v.tensor_tensor(
                    xn5[:], xn5[:], rows5b[:], ALU.add),
                    reads=[b_xn5, b_rows5b], writes=[b_xn5])
                k.dma("sync", cg_in, xn5[:], reads=[b_xn5], writes=[b_cg], tag="cgw")
                k.op("gpsimd", lambda g: g.collective_compute(
                    "AllGather", ALU.bypass, replica_groups=[list(range(8))],
                    ins=[cg_in], outs=[cg_out]),
                    reads=[b_cg], writes=[b_cg], tag="allgather")
                k.dma("sync", xg8[:], cg_out, reads=[b_cg], writes=[b_xg8],
                      tag="cgr")
                k.op("vector", lambda v: v.tensor_copy(xg8b[:], xg8[:]),
                     reads=[b_xg8], writes=[b_xg8b])

            def row_to_cols(src_b, b_src, dst, b_dst, nt, tag):
                for j in range(nt):
                    k.op("tensor", lambda tt_, j=j: tt_.transpose(
                        ps5t[:, 8 * j:8 * (j + 1)],
                        src_b[:, 128 * j:128 * (j + 1)], ident[0:8, 0:8]),
                        reads=[b_src, b_ident], writes=[b_ps5t], tag=f"tp5_{tag}")
                k.op("vector", lambda v, nt=nt: v.tensor_copy(
                    dst[:, 0:8 * nt], ps5t[:, 0:8 * nt]),
                    reads=[b_ps5t], writes=[b_dst], tag=f"tp5e_{tag}")

            def mm_throw(src_cols, b_src, wname, bias_off, tag):
                for cb in range(4):
                    k.op("tensor", lambda tt_, cb=cb, bias_off=bias_off: tt_.matmul(
                        ps5[0:8, 512 * cb:512 * (cb + 1)], ones_row[0:1, 0:8],
                        brow[:, bias_off + 512 * cb:bias_off + 512 * (cb + 1)],
                        start=True, stop=False, skip_group_check=True),
                        reads=[b_ones, b_brow], writes=[b_ps5], tag=f"{tag}b{cb}")
                    for d in range(DT):
                        wi = (cb * DT + d) % 2
                        k.dma("gpsimd", aggb_blk[wi][:],
                              I[wname][128 * d:128 * (d + 1),
                                       512 * cb:512 * (cb + 1)],
                              writes=[b_aggblk[wi]], tag=f"{tag}w{cb}_{d}")
                        k.op("tensor", lambda tt_, cb=cb, d=d, wi=wi: tt_.matmul(
                            ps5[0:8, 512 * cb:512 * (cb + 1)],
                            src_cols[:, 8 * d:8 * (d + 1)],
                            aggb_blk[wi][:],
                            start=False, stop=(d == DT - 1),
                            skip_group_check=True),
                            reads=[b_src, b_aggblk[wi]], writes=[b_ps5],
                            tag=f"{tag}m{cb}")

            def seg_agg():
                row_to_cols(xg8b, b_xg8b, tT5, b_tT5, 16, "xn")
                mm_throw(tT5, b_tT5, "agg_w", 0, "agg")
                k.op("vector", lambda v: v.tensor_copy(throw[:], ps5[0:8, :]),
                     reads=[b_ps5], writes=[b_throw], tag="th0")
                if debug:
                    k.dma("sync", dbg["throw"], throw[:], reads=[b_throw],
                          tag="dbg_throw")

            def seg_iter(it5):
                k.op("vector", lambda v: v.reduce_sum(
                    st5[:, 0:1], throw[:], axis=AX.X),
                    reads=[b_throw], writes=[b_st5])
                k.op("scalar", lambda s_: s_.activation(
                    trow[:], throw[:], AF.Square, accum_out=st5[:, 1:2]),
                    reads=[b_throw, k.const_buf], writes=[b_trow, b_st5])
                k.op("vector", lambda v: v.tensor_scalar_mul(
                    st5[:, 2:4], st5[:, 0:2], 1.0 / D),
                    reads=[b_st5], writes=[b_st5])
                k.op("vector", lambda v: v.tensor_tensor(
                    st5[:, 4:5], st5[:, 2:3], st5[:, 2:3], ALU.mult),
                    reads=[b_st5], writes=[b_st5])
                k.op("vector", lambda v: v.tensor_sub(
                    st5[:, 5:6], st5[:, 3:4], st5[:, 4:5]),
                    reads=[b_st5], writes=[b_st5])
                k.op("scalar", lambda s_: s_.activation(
                    st5[:, 6:7], st5[:, 5:6], AF.Ln, bias=k.constv(1e-6)),
                    reads=[b_st5, k.const_buf], writes=[b_st5])
                k.op("scalar", lambda s_: s_.activation(
                    st5[:, 7:8], st5[:, 6:7], AF.Exp, scale=k.constv(-0.5)),
                    reads=[b_st5, k.const_buf], writes=[b_st5])
                k.op("vector", lambda v: v.tensor_scalar(
                    trow[:], throw[:], st5[:, 2:3], st5[:, 7:8],
                    ALU.subtract, ALU.mult),
                    reads=[b_throw, b_st5], writes=[b_trow])
                k.op("vector", lambda v: v.tensor_tensor(
                    trow[:], trow[:], Gpn[:], ALU.mult),
                    reads=[b_trow, b_Gpn], writes=[b_trow])
                k.op("vector", lambda v: v.tensor_tensor(
                    trow[:], trow[:], Bpn[:], ALU.add),
                    reads=[b_trow, b_Bpn], writes=[b_trow])
                k.op("vector", lambda v: v.tensor_copy(trowb[:], trow[:]),
                     reads=[b_trow], writes=[b_trowb])
                row_to_cols(trowb, b_trowb, tT5, b_tT5, 16, f"t{it5}")
                for nb_ in range(2):
                    k.op("tensor", lambda tt_, nb_=nb_: tt_.matmul(
                        ps5[0:8, 512 * nb_:512 * (nb_ + 1)], ones_row[0:1, 0:8],
                        brow[:, 2048 + 512 * nb_:2048 + 512 * (nb_ + 1)],
                        start=True, stop=False, skip_group_check=True),
                        reads=[b_ones, b_brow], writes=[b_ps5], tag="d1b")
                for d in range(DT):
                    di = d % 2
                    k.dma("gpsimd", d1b2[di][:],
                          I["d1_sl"][128 * d:128 * (d + 1), :],
                          writes=[b_d1b2[di]], tag=f"d1s{d}")
                    for nb_ in range(2):
                        k.op("tensor", lambda tt_, nb_=nb_, d=d, di=di: tt_.matmul(
                            ps5[0:8, 512 * nb_:512 * (nb_ + 1)],
                            tT5[:, 8 * d:8 * (d + 1)],
                            d1b2[di][:, 512 * nb_:512 * nb_ + 512],
                            start=False, stop=(d == DT - 1),
                            skip_group_check=True),
                            reads=[b_tT5, b_d1b2[di]], writes=[b_ps5], tag="d1m")
                k.op("scalar", lambda s_: s_.activation(
                    hrow[:], ps5[0:8, 0:1024], AF.Gelu),
                    reads=[b_ps5, k.const_buf], writes=[b_hrow], tag="gelu")
                k.op("vector", lambda v: v.tensor_copy(hrowb[:], hrow[:]),
                     reads=[b_hrow], writes=[b_hrowb])
                row_to_cols(hrowb, b_hrowb, hT5, b_hT5, 8, f"h{it5}")
                for kk2 in range(8):
                    ki = kk2 % 2
                    k.dma("gpsimd", d2b2[ki][:],
                          I["d2_sl"][128 * kk2:128 * (kk2 + 1), :],
                          writes=[b_d2b2[ki]], tag=f"d2s{kk2}")
                    for oc in range(4):
                        k.op("tensor", lambda tt_, oc=oc, kk2=kk2, ki=ki: tt_.matmul(
                            ps5[0:8, 512 * oc:512 * (oc + 1)],
                            hT5[:, 8 * kk2:8 * (kk2 + 1)],
                            d2b2[ki][:, 512 * oc:512 * oc + 512],
                            start=(kk2 == 0), stop=(kk2 == 7)),
                            reads=[b_hT5, b_d2b2[ki]], writes=[b_ps5], tag="d2m")
                k.op("vector", lambda v: v.tensor_copy(rsum[:], ps5[0:8, :]),
                     reads=[b_ps5], writes=[b_rsum], tag="d2ev")
                k.dma("sync", cc_in, rsum[:], reads=[b_rsum], writes=[b_cc],
                      tag="ccw")
                k.op("gpsimd", lambda g: g.collective_compute(
                    "AllReduce", ALU.add, replica_groups=[list(range(8))],
                    ins=[cc_in], outs=[cc_out]),
                    reads=[b_cc], writes=[b_cc], tag="allreduce")
                k.dma("sync", rsum[:], cc_out, reads=[b_cc], writes=[b_rsum],
                      tag="ccr")
                k.op("vector", lambda v: v.tensor_tensor(
                    rsum[:], rsum[:], D2b[:], ALU.add),
                    reads=[b_rsum, b_D2b], writes=[b_rsum])
                k.op("vector", lambda v: v.tensor_tensor(
                    throw[:], throw[:], rsum[:], ALU.add),
                    reads=[b_throw, b_rsum], writes=[b_throw], tag="thadd")

            def seg_tb():
                k.op("vector", lambda v: v.tensor_copy(trowb[:], throw[:]),
                     reads=[b_throw], writes=[b_trowb])
                row_to_cols(trowb, b_trowb, tT5, b_tT5, 16, "tb")
                mm_throw(tT5, b_tT5, "bc_w", 3072, "bc")
                k.op("vector", lambda v: v.tensor_copy(TB8[:], ps5[0:8, :]),
                     reads=[b_ps5], writes=[b_TB8], tag="tb8")
                if debug:
                    k.dma("sync", dbg["TB8"], TB8[:], reads=[b_TB8], tag="dbg_tb8")
                for i4 in range(4):
                    pp_ = psg[:, 0:512] if i4 % 2 == 0 else psg[:, 512:1024]
                    k.op("tensor", lambda tt_, i4=i4, pp_=pp_: tt_.matmul(
                        pp_, sel8[:], TB8[:, 512 * i4:512 * (i4 + 1)],
                        start=True, stop=True),
                        reads=[b_sel8, b_TB8], writes=[b_psg], tag="sel_mm")
                    k.op("vector", lambda v, i4=i4, pp_=pp_: v.tensor_copy(
                        Bt5[:, 512 * i4:512 * (i4 + 1)], pp_),
                        reads=[b_psg], writes=[b_Bt5], tag="sel_ev")
                if debug:
                    k.dma("sync", dbg["Bt5"], Bt5[:], reads=[b_Bt5], tag="dbg_bt5")

            segs = [seg_loads, seg_pre, seg_xn, seg_agg] + \
                   [lambda it5=i: seg_iter(it5) for i in range(4)] + [seg_tb]

            for d in range(DT):
                k.dma("gpsimd", gatew[:, 2048 * d:2048 * (d + 1)],
                      I["gate_w"][128 * d:128 * (d + 1), :], writes=[b_gw],
                      tag=f"gw{d}")
            k.dma("gpsimd", grow[:], I["gate_b"][None, :], writes=[b_grow])

            si = 0
            for t in range(TT):
                while si < len(segs) and si * 2 <= t:
                    segs[si]()
                    si += 1
                xi = t % 2
                k.dma("sync", xftm[xi][:], sc["xfT"][128 * t:128 * (t + 1), :],
                      reads=[scb["xfT"]], writes=[b_xftm[xi]], tag=f"g_xf{t}")
                for j in range(DT):
                    k.op("tensor", lambda tt_, j=j, xi=xi: tt_.transpose(
                        ps5t[:, 128 * j:128 * (j + 1)],
                        xftm[xi][:, 128 * j:128 * (j + 1)], ident[:]),
                        reads=[b_xftm[xi], b_ident], writes=[b_ps5t],
                        tag=f"g_tp{t}")
                stg5, b_stg5 = stg5A[t % 2], b_stg5A[t % 2]
                k.op("vector", lambda v, stg5=stg5: v.tensor_copy(stg5[:], ps5t[:]),
                     reads=[b_ps5t], writes=[b_stg5], tag="g_ev")
                for half in range(2):
                    gi = (2 * t + half) % 2
                    for gc in range(2):
                        k.op("tensor", lambda tt_, gc=gc, half=half: tt_.matmul(
                            psg[:, 512 * gc:512 * (gc + 1)], ones_row[:],
                            grow[:, 1024 * half + 512 * gc:
                                 1024 * half + 512 * gc + 512],
                            start=True, stop=False, skip_group_check=True),
                            reads=[b_ones, b_grow], writes=[b_psg], tag="g_b")
                    for d in range(DT):
                        for gc in range(2):
                            k.op("tensor", lambda tt_, d=d, gc=gc, half=half,
                                 stg5=stg5: tt_.matmul(
                                     psg[:, 512 * gc:512 * (gc + 1)],
                                     stg5[:, 128 * d:128 * (d + 1)],
                                     gatew[:, 2048 * d + 1024 * half + 512 * gc:
                                           2048 * d + 1024 * half + 512 * gc + 512],
                                     start=False, stop=(d == DT - 1),
                                     skip_group_check=True),
                                 reads=[b_stg5, b_gw], writes=[b_psg], tag="g_mm")
                    k.op("scalar", lambda s_, gi=gi: s_.activation(
                        gsx[gi][:], psg[:], AF.Exp, scale=k.constv(-1.0)),
                        reads=[b_psg, k.const_buf], writes=[b_gsx[gi]],
                        tag="g_sige")
                    k.op("vector", lambda v, gi=gi: v.tensor_scalar_add(
                        gsx[gi][:], gsx[gi][:], 1.0),
                        reads=[b_gsx[gi]], writes=[b_gsx[gi]])
                    k.op("vector", lambda v, gi=gi: v.reciprocal(
                        gsx[gi][:], gsx[gi][:]),
                        reads=[b_gsx[gi]], writes=[b_gsx[gi]])
                    k.op("vector", lambda v, gi=gi: v.tensor_copy(
                        gst[gi][:], gsx[gi][:]),
                        reads=[b_gsx[gi]], writes=[b_gst[gi]], tag="g_sig")
                    k.dma("sync",
                          sc["gate"][128 * t:128 * (t + 1),
                                     1024 * half:1024 * (half + 1)],
                          gst[gi][:], reads=[b_gst[gi]], writes=[scb["gate"]],
                          tag=f"g_sp{t}_{half}")
            while si < len(segs):
                segs[si]()
                si += 1
        k.barrier()

        with ExitStack() as es5b:
            ggb = [k.sb(f"ggb{i}", [128, 2048], BF16, es=es5b) for i in range(2)]
            b_ggb = [Buf(f"ggb{i}") for i in range(2)]
            xfb5 = [k.sb(f"xfb5{i}", [128, 2048], F32, es=es5b) for i in range(2)]
            b_xfb5 = [Buf(f"xfb5{i}") for i in range(2)]
            x2b5 = [k.sb(f"x2b5{i}", [128, 2048], F32, es=es5b) for i in range(2)]
            b_x2b5 = [Buf(f"x2b5{i}") for i in range(2)]
            sc1b = k.sb("sc1b", [128, 2048], F32, es=es5b); b_sc1b = Buf("sc1b")
            sc2b = k.sb("sc2b", [128, 2048], F32, es=es5b); b_sc2b = Buf("sc2b")
            outb = [k.sb(f"outb{i}", [128, 2048], F32, es=es5b) for i in range(2)]
            b_outb = [Buf(f"outb{i}") for i in range(2)]
            st6 = k.sb("st6", [128, 8], F32, es=es5b); b_st6 = Buf("st6")
            sqv6 = k.sb("sqv6", [128, 2048], F32, es=es5b); b_sqv6 = Buf("sqv6")

            for t in range(TT):
                xi = 0
                k.dma("sync", ggb[xi][:], sc["gate"][128 * t:128 * (t + 1), :],
                      reads=[scb["gate"]], writes=[b_ggb[xi]], tag=f"c_g{t}")
                k.dma("sync", xfb5[xi][:], sc["xf"][128 * t:128 * (t + 1), :],
                      reads=[scb["xf"]], writes=[b_xfb5[xi]], tag=f"c_xf{t}")
                k.dma("scalar", x2b5[xi][:], sc["x2"][128 * t:128 * (t + 1), :],
                      reads=[scb["x2"]], writes=[b_x2b5[xi]], tag=f"c_x2{t}")
                k.op("vector", lambda v, xi=xi: v.tensor_tensor(
                    sc1b[:], ggb[xi][:], Bt5[:], ALU.mult),
                    reads=[b_ggb[xi], b_Bt5], writes=[b_sc1b], tag="c_gTB")
                k.op("gpsimd", lambda g, xi=xi: g.tensor_tensor(
                    sc2b[:], sc1b[:], xfb5[xi][:], ALU.add),
                    reads=[b_sc1b, b_xfb5[xi]], writes=[b_sc2b], tag="c_open")
                if debug and t == 0:
                    k.dma("sync", dbg["open0"], sc2b[:], reads=[b_sc2b],
                          tag="dbg_open0")
                k.op("vector", lambda v: v.reduce_sum(
                    st6[:, 0:1], sc2b[:], axis=AX.X),
                    reads=[b_sc2b], writes=[b_st6])
                k.op("scalar", lambda s_: s_.activation(
                    sqv6[:], sc2b[:], AF.Square, accum_out=st6[:, 1:2]),
                    reads=[b_sc2b, k.const_buf], writes=[b_sqv6, b_st6])
                k.op("vector", lambda v: v.tensor_scalar_mul(
                    st6[:, 2:4], st6[:, 0:2], 1.0 / D),
                    reads=[b_st6], writes=[b_st6])
                k.op("vector", lambda v: v.tensor_tensor(
                    st6[:, 4:5], st6[:, 2:3], st6[:, 2:3], ALU.mult),
                    reads=[b_st6], writes=[b_st6])
                k.op("vector", lambda v: v.tensor_sub(
                    st6[:, 5:6], st6[:, 3:4], st6[:, 4:5]),
                    reads=[b_st6], writes=[b_st6])
                k.op("scalar", lambda s_: s_.activation(
                    st6[:, 6:7], st6[:, 5:6], AF.Ln, bias=k.constv(1e-6)),
                    reads=[b_st6, k.const_buf], writes=[b_st6])
                k.op("scalar", lambda s_: s_.activation(
                    st6[:, 7:8], st6[:, 6:7], AF.Exp, scale=k.constv(-0.5)),
                    reads=[b_st6, k.const_buf], writes=[b_st6])
                k.op("vector", lambda v: v.tensor_scalar(
                    sc1b[:], sc2b[:], st6[:, 2:3], st6[:, 7:8],
                    ALU.subtract, ALU.mult),
                    reads=[b_sc2b, b_st6], writes=[b_sc1b], tag="c_ln6")
                k.op("gpsimd", lambda g: g.tensor_tensor(
                    sc2b[:], sc1b[:], Gam6[:], ALU.mult),
                    reads=[b_sc1b, b_Gam6], writes=[b_sc2b])
                k.op("vector", lambda v: v.tensor_tensor(
                    sc1b[:], sc2b[:], Bet6[:], ALU.add),
                    reads=[b_sc2b, b_Bet6], writes=[b_sc1b], tag="c_gb")
                if debug and t == 0:
                    k.dma("sync", dbg["gb0"], sc1b[:], reads=[b_sc1b],
                          tag="dbg_gb0")
                    k.dma("sync", dbg["gam6"], Gam6[:], reads=[b_Gam6],
                          tag="dbg_gam6")
                k.op("vector", lambda v: v.reduce_sum(
                    st6[:, 0:1], sc1b[:], axis=AX.X),
                    reads=[b_sc1b], writes=[b_st6])
                k.op("scalar", lambda s_: s_.activation(
                    sqv6[:], sc1b[:], AF.Square, accum_out=st6[:, 1:2]),
                    reads=[b_sc1b, k.const_buf], writes=[b_sqv6, b_st6])
                k.op("vector", lambda v: v.tensor_scalar_mul(
                    st6[:, 2:4], st6[:, 0:2], 1.0 / D),
                    reads=[b_st6], writes=[b_st6])
                k.op("vector", lambda v: v.tensor_tensor(
                    st6[:, 4:5], st6[:, 2:3], st6[:, 2:3], ALU.mult),
                    reads=[b_st6], writes=[b_st6])
                k.op("vector", lambda v: v.tensor_sub(
                    st6[:, 5:6], st6[:, 3:4], st6[:, 4:5]),
                    reads=[b_st6], writes=[b_st6])
                k.op("scalar", lambda s_: s_.activation(
                    st6[:, 6:7], st6[:, 5:6], AF.Ln, bias=k.constv(1e-5)),
                    reads=[b_st6, k.const_buf], writes=[b_st6])
                k.op("scalar", lambda s_: s_.activation(
                    st6[:, 7:8], st6[:, 6:7], AF.Exp, scale=k.constv(-0.5)),
                    reads=[b_st6, k.const_buf], writes=[b_st6])
                k.op("vector", lambda v: v.tensor_scalar(
                    sc2b[:], sc1b[:], st6[:, 2:3], st6[:, 7:8],
                    ALU.subtract, ALU.mult),
                    reads=[b_sc1b, b_st6], writes=[b_sc2b], tag="c_ln7")
                pi = xi
                k.op("gpsimd", lambda g, xi=xi, pi=pi: g.tensor_tensor(
                    outb[pi][:], sc2b[:], x2b5[xi][:], ALU.add),
                    reads=[b_sc2b, b_x2b5[xi]], writes=[b_outb[pi]], tag="c_out")
                k.dma("sync", out_t[128 * t:128 * (t + 1), :], outb[pi][:],
                      reads=[b_outb[pi]], tag=f"st_out{t}")

    k.build_program()
    return nc, k, sc, dbg


NAME_MAP = {
    "Wqkv": "L0_attn_Wqkv", "Wlr": "L0_attn_Wlr", "Wm": "L0_attn_Wm",
    "bm": "L0_attn_bm", "Ws": "L0_attn_Ws", "bs": "L0_attn_bs",
    "Wo": "L0_attn_Wo", "w0": "L0_attn_w0", "w1": "L0_attn_w1",
    "w2": "L0_attn_w2", "o_norm_g": "L0_attn_o_norm_g",
    "qk_scale": "L0_attn_qk_scale", "qk_offset": "L0_attn_qk_offset",
    "in_gamma": "L0_ffn_in_gamma", "in_beta": "L0_ffn_in_beta",
    "agg_w": "L0_ffn_agg_w", "agg_b": "L0_ffn_agg_b",
    "pn_gamma": "L0_ffn_pn_gamma", "pn_beta": "L0_ffn_pn_beta",
    "d2_b": "L0_ffn_d2_b",
    "bc_w": "L0_ffn_bc_w", "bc_b": "L0_ffn_bc_b",
    "gate_w": "L0_ffn_gate_w", "gate_b": "L0_ffn_gate_b",
    "out_gamma": "L0_ffn_out_gamma", "out_beta": "L0_ffn_out_beta",
}


def kernel(**inputs):
    x = np.ascontiguousarray(inputs["L0_attn_x"])
    nc, k, sc, dbg = build(debug=False)
    base = {k_: np.ascontiguousarray(inputs[v]) for k_, v in NAME_MAP.items()}
    d1w = inputs["L0_ffn_d1_w"]
    d1b = inputs["L0_ffn_d1_b"]
    d2w = inputs["L0_ffn_d2_w"]
    in_maps = []
    for c in range(8):
        m = dict(base)
        m["x"] = np.ascontiguousarray(x[c])
        m["d1_sl"] = np.ascontiguousarray(d1w[:, 1024 * c:1024 * (c + 1)])
        m["d1b_sl"] = np.ascontiguousarray(d1b[1024 * c:1024 * (c + 1)])
        m["d2_sl"] = np.ascontiguousarray(d2w[1024 * c:1024 * (c + 1), :])
        oh = np.zeros((8, 1), dtype=np.float32)
        oh[c, 0] = 1.0
        m["onehot"] = oh
        in_maps.append(m)
    res = run_bass_kernel_spmd(nc, in_maps, core_ids=list(range(8)))
    return np.stack([np.asarray(r["out"]) for r in res.results])
